# version 22
# baseline (speedup 1.0000x reference)
import math
from contextlib import ExitStack

import numpy as np
import ml_dtypes

import concourse.bass as bass
import concourse.mybir as mybir
from concourse.bass_utils import run_bass_kernel_spmd

F32 = mybir.dt.float32
BF16 = mybir.dt.bfloat16
AF = mybir.ActivationFunctionType
ALU = mybir.AluOpType
AX = mybir.AxisListType
bf = ml_dtypes.bfloat16

B, T, D = 2, 16384, 1024
SEG = 4096
NCH = 32
NTT = 8
H, DK, M = 16, 64, 128
CP = 2.8284271247461903
LSM = 0.5 * math.log(128.0)
EPS = 1e-6
LNEPS = 1e-5
R1024 = 1.0 / 1024.0

ENGS = ("sync", "gpsimd", "vector", "scalar", "tensor")

_STAGE = 3
_P1SUB = 9
_NTTRUN = NTT


class Sem:
    __slots__ = ("h", "n")

    def __init__(self, h):
        self.h = h
        self.n = 0

    def inc(self, k=1):
        self.n += k
        return self.n


def build_nc():
    nc = bass.Bass("TRN2", target_bir_lowering=False, debug=False)

    xseg = nc.dram_tensor("xseg", [SEG, D], F32, kind="ExternalInput").ap()
    wqT = nc.dram_tensor("wqT", [D, D], F32, kind="ExternalInput").ap()
    wkT = nc.dram_tensor("wkT", [D, D], F32, kind="ExternalInput").ap()
    wvT = nc.dram_tensor("wvT", [D, D], F32, kind="ExternalInput").ap()
    woT = nc.dram_tensor("woT", [D, D], F32, kind="ExternalInput").ap()
    wfT = nc.dram_tensor("wfT", [D, D], F32, kind="ExternalInput").ap()
    bqi = nc.dram_tensor("bqi", [128, 8], F32, kind="ExternalInput").ap()
    bki = nc.dram_tensor("bki", [128, 8], F32, kind="ExternalInput").ap()
    bvi = nc.dram_tensor("bvi", [1, D], F32, kind="ExternalInput").ap()
    boi = nc.dram_tensor("boi", [1, D], F32, kind="ExternalInput").ap()
    projT2 = nc.dram_tensor("projT2", [128, 256], F32, kind="ExternalInput").ap()
    id128i = nc.dram_tensor("id128i", [128, 128], BF16, kind="ExternalInput").ap()
    id2i = nc.dram_tensor("id2i", [2, 2], F32, kind="ExternalInput").ap()
    triuTi = nc.dram_tensor("triuTi", [128, 128], F32, kind="ExternalInput").ap()
    oblki = nc.dram_tensor("oblki", [128, 2], BF16, kind="ExternalInput").ap()
    orowi = nc.dram_tensor("orowi", [1, 128], BF16, kind="ExternalInput").ap()
    segwi = nc.dram_tensor("segwi", [128, 4], F32, kind="ExternalInput").ap()
    idSi = nc.dram_tensor("idSi", [128, 128], F32, kind="ExternalInput").ap()

    yseg = nc.dram_tensor("yseg", [SEG, D], F32, kind="ExternalOutput").ap()

    xnk_sp = nc.dram_tensor("xnk_sp", [128, NTT * 4096], BF16, kind="Internal").ap()
    vau_sp = nc.dram_tensor("vau_sp", [128, NCH * 1040], BF16, kind="Internal").ap()
    cc_in = nc.dram_tensor("cc_in", [128, 1040], F32, kind="Internal").ap()
    cc_out = nc.dram_tensor("cc_out", [512, 1040], F32, kind="Internal").ap()

    Q = {k: [] for k in ENGS}

    def add(eng, fn):
        Q[eng].append(fn)

    ctx = ExitStack()
    with ctx:
        sbt = lambda name, shape, dt: ctx.enter_context(nc.sbuf_tensor(name, shape, dt))
        pst = lambda name, shape, dt: ctx.enter_context(nc.psum_tensor(name, shape, dt))
        mksem = lambda name: Sem(ctx.enter_context(nc.semaphore(name)))

        wsA = sbt("wsA", [128, 8 * D], BF16)
        wsB = sbt("wsB", [128, 8 * D], BF16)
        wsC = sbt("wsC", [128, 8 * D], BF16)
        projb = sbt("projb", [128, 256], BF16)
        idb = sbt("idb", [128, 128], BF16)
        idS = sbt("idS", [128, 128], F32)
        id2 = sbt("id2", [2, 2], F32)
        triuT = sbt("triuT", [128, 128], F32)
        oblk = sbt("oblk", [128, 2], BF16)
        orow = sbt("orow", [1, 128], BF16)
        segw = sbt("segw", [128, 4], F32)
        bqs = sbt("bqs", [128, 8], F32)
        bks = sbt("bks", [128, 8], F32)
        bqraw = sbt("bqraw", [128, 8], F32)
        bkraw = sbt("bkraw", [128, 8], F32)
        bvb = sbt("bvb", [1, D], BF16)
        bob = sbt("bob", [1, D], BF16)
        lsmn = sbt("lsmn", [128, 1], F32)
        epsc = sbt("epsc", [128, 1], F32)

        xb = [sbt(f"xb{i}", [128, D], F32) for i in range(5)]
        htok = [sbt(f"htok{i}", [128, D], BF16) for i in range(2)]
        hfm = sbt("hfm", [128, 4096], BF16)
        xnq_t = sbt("xnq_t", [128, 4096], BF16)
        xnk_t = [sbt(f"xnk_t{i}", [128, 4096], BF16) for i in range(2)]
        vau_t = [sbt(f"vau_t{i}", [128, 4160], BF16) for i in range(2)]
        xnsqw = sbt("xnsqw", [128, 4096], BF16)
        sqrow = sbt("sqrow", [2, 512], F32)
        sqk_c = sbt("sqk_c", [128, NCH * 16], F32)
        sqq_c = sbt("sqq_c", [128, NCH * 16], F32)
        ln1_s = sbt("ln1_s", [128, NCH], F32)
        ln1_b = sbt("ln1_b", [128, NCH], F32)
        alf = sbt("alf", [128, 32], F32)
        alfb = sbt("alfb", [128, 32], BF16)
        etok = sbt("etok", [128, 2048], BF16)
        eqtok = sbt("eqtok", [128, 2048], BF16)
        kfm = sbt("kfm", [128, 2048], BF16)
        qfm = sbt("qfm", [128, 2048], BF16)
        amask = sbt("amask", [128, 1024], BF16)
        sprev = [sbt(f"sprev{i}", [128, 1040], BF16) for i in range(2)]
        sfin = sbt("sfin", [128, 1040], F32)
        sinit = sfin
        gbuf = sbt("gbuf", [128, 1040], F32)
        sqdump = gbuf
        dgath = sbt("dgath", [128, 16], F32)
        rcol = sbt("rcol", [128, 16], F32)
        attok = [sbt(f"attok{i}", [128, D], BF16) for i in range(2)]
        atfm = sbt("atfm", [128, D], BF16)
        yf = [sbt(f"yf{i}", [128, D], F32) for i in range(2)]
        lnsc = [sbt(f"lnsc{i}", [128, D], F32) for i in range(2)]
        x1b = [sbt(f"x1b{i}", [128, D], F32) for i in range(2)]
        h3tok = [sbt(f"h3tok{i}", [128, D], BF16) for i in range(2)]
        h3fm = sbt("h3fm", [128, D], BF16)
        stat = sbt("stat", [128, 64], F32)
        bstat = sbt("bstat", [128, 56], F32)

        S0 = pst("S0", [128, 455], F32)
        S1 = pst("S1", [128, 455], F32)
        S2 = pst("S2", [128, 130], F32)
        pAB = pst("pAB", [128, 1024], F32)
        pEF = pst("pEF", [128, 1024], F32)
        pBf = pst("pBf", [128, 1024], BF16)

        def s_slice(h):
            if h < 7:
                return S0[:, h * 65:(h + 1) * 65]
            if h < 14:
                return S1[:, (h - 7) * 65:(h - 6) * 65]
            return S2[:, (h - 14) * 65:(h - 13) * 65]

        s_w = mksem("s_w")
        s_wb = mksem("s_wb")
        s_wp = mksem("s_wp")
        s_x = [mksem(f"s_x{i}") for i in range(5)]
        s_ln = mksem("s_ln")
        s_tp = mksem("s_tp")
        f_Bf = mksem("f_Bf")
        s_pj = mksem("s_pj")
        s_um = mksem("s_um")
        s_sq = mksem("s_sq")
        s_sqe = mksem("s_sqe")
        s_xk = mksem("s_xk")
        s_va = mksem("s_va")
        s_sp = [mksem(f"s_sp{i}") for i in range(2)]
        s_rb = [mksem(f"s_rb{i}") for i in range(2)]
        s_ue = mksem("s_ue")
        s_ph = mksem("s_ph")
        s_kv = mksem("s_kv")
        s_sn = mksem("s_sn")
        s_cc = mksem("s_cc")
        s_ccd = mksem("s_ccd")
        s_pe = mksem("s_pe")
        s_am = mksem("s_am")
        s_au = mksem("s_au")
        s_ep = mksem("s_ep")
        s_ve = mksem("s_ve")
        s_ac = mksem("s_ac")
        s_st = mksem("s_st")
        s_ou = [mksem(f"s_ou{i}") for i in range(2)]

        abreads = [dict(), dict()]
        prev_sq_base = [-1]
        efreads = [dict(), dict()]
        nBf = [0]
        bfhalf = [0, 0]
        htrans_vals = []
        attn_tp_vals = []
        x1vals = []
        prev_alb = {"q": 0, "k": 0}
        p1_ln_box = [0]

        def rec_read(dics, b, sem):
            dics[b][sem] = sem.n

        def snap_reads(dics, *banks):
            out = {}
            for b in banks:
                for sem, v in dics[b].items():
                    out[sem] = max(out.get(sem, 0), v)
            return tuple(out.items())

        def wait_snap(e, snapshot):
            for sem, v in snapshot:
                if v > 0:
                    e.wait_ge(sem.h, v)

        def g_loads(e):
            for dst, src in ((idb, id128i), (id2, id2i), (triuT, triuTi),
                             (oblk, oblki), (orow, orowi), (segw, segwi),
                             (bqraw, bqi), (bkraw, bki), (idS, idSi)):
                e.dma_start(out=dst[:], in_=src).then_inc(s_w.h, 16)
            e.dma_start(out=bvb[:], in_=bvi).then_inc(s_w.h, 16)
            e.dma_start(out=bob[:], in_=boi).then_inc(s_w.h, 16)
            e.dma_start(out=projb[:], in_=projT2).then_inc(s_w.h, 16)
            for dst, src in ((wsA, wkT), (wsB, wvT), (wsC, wfT)):
                e.dma_start(out=dst[:], in_=src.rearrange("(j p) o -> p j o", p=128)
                            ).then_inc(s_w.h, 16)
        add("gpsimd", g_loads)
        s_w.inc(16 * 15)
        wload = s_w.n

        def v_prep(e):
            e.wait_ge(s_w.h, wload)
            e.tensor_scalar(out=bqs[:], in0=bqraw[:], scalar1=1.0 / CP,
                            scalar2=None, op0=ALU.mult)
            e.tensor_scalar(out=bks[:], in0=bkraw[:], scalar1=1.0 / CP,
                            scalar2=None, op0=ALU.mult)
            e.memset(lsmn[:], -LSM)
            e.memset(epsc[:], LNEPS)
            e.memset(vau_t[0][:, :], 1.0)
            e.memset(vau_t[1][:, :], 1.0).then_inc(s_wp.h, 1)
        add("vector", v_prep)
        s_wp.inc(1)

        def a_prep(e):
            e.wait_ge(s_w.h, wload)
            e.activation(out=wsC[:], in_=wsC[:], func=AF.Sign).then_inc(s_wp.h, 1)
        add("scalar", a_prep)
        s_wp.inc(1)
        prep_done = s_wp.n

        xload_n = [0, 0]

        def emit_xload(c, phase):
            nb = 4 if phase == 1 else 5
            xbuf = xb[c % nb]
            sx = s_x[c % nb]
            idx = xload_n[phase - 1]
            xload_n[phase - 1] += 1
            val = sx.inc(16)
            waits = []
            if phase == 1:
                if idx >= nb:
                    waits.append((s_ln, idx - nb + 1))
            else:
                if idx < nb:
                    waits.append((s_ln, p1_ln_box[0]))
                else:
                    waits.append((s_ve, x1vals[idx - nb]))

            def f(e, c=c, xbuf=xbuf, waits=tuple(waits), sx=sx):
                for smh, vv in waits:
                    if vv > 0:
                        e.wait_ge(smh.h, vv)
                e.dma_start(out=xbuf[:], in_=xseg[c * 128:(c + 1) * 128, :]
                            ).then_inc(sx.h, 16)
            add("sync", f)
            return xbuf, (sx, val)

        def stat_cols(lnid, c):
            base = lnid * 16 + (c % 2) * 8
            return [stat[:, base + i:base + i + 1] for i in range(7)]

        def emit_stats_math(lnid, c, sum_ap, sumsq_ap, deps, guard,
                            cols=None):
            csum, csumsq, cm, cbarg, cstd, cs_, cneg = (
                cols if cols is not None else stat_cols(lnid, c))
            w0 = s_st.inc(1)
            w1 = s_st.inc(1)
            w2 = s_st.inc(1)
            s_st.inc(1)
            adone = s_st.n

            def a(e, deps=deps, guard=guard, w0=w0, w1=w1, w2=w2):
                for smh, vv in deps:
                    if vv > 0:
                        e.wait_ge(smh.h, vv)
                if guard[1] > 0:
                    e.wait_ge(guard[0].h, guard[1])
                e.activation(out=cm, in_=sum_ap, func=AF.Identity,
                             scale=-R1024).then_inc(s_st.h, 1)
                e.wait_ge(s_st.h, w0)
                e.activation(out=cbarg, in_=cm, func=AF.Square
                             ).then_inc(s_st.h, 1)
                e.wait_ge(s_st.h, w1)
                e.activation(out=cbarg, in_=cbarg, func=AF.Identity,
                             scale=-1.0, bias=epsc[:, 0:1]).then_inc(s_st.h, 1)
                e.wait_ge(s_st.h, w2)
                e.activation(out=cstd, in_=sumsq_ap, func=AF.Sqrt,
                             bias=cbarg, scale=R1024).then_inc(s_st.h, 1)
            add("scalar", a)

            wv = s_ve.inc(1)
            s_ve.inc(1)
            vdone = s_ve.n

            def v2(e, adone=adone, wv=wv):
                e.wait_ge(s_st.h, adone)
                e.reciprocal(out=cs_, in_=cstd).then_inc(s_ve.h, 1)
                e.wait_ge(s_ve.h, wv)
                e.scalar_tensor_tensor(out=cneg, in0=cm, scalar=1.0, in1=cs_,
                                       op0=ALU.mult, op1=ALU.mult
                                       ).then_inc(s_ve.h, 1)
            add("vector", v2)
            return cs_, cneg, (s_ve, vdone)

        def emit_front1_batch(tt):
            par = tt % 2
            base = par * 28
            bc = [bstat[:, base + g * 4: base + (g + 1) * 4] for g in range(7)]
            war = 4 * (tt - 1)
            xs = []
            for cs in range(4):
                c = tt * 4 + cs
                xbuf, xval = emit_xload(c, 1)
                xs.append((xbuf, xval))
                scol = bstat[:, base + cs: base + cs + 1]
                qcol = bstat[:, base + 4 + cs: base + 4 + cs + 1]

                def v(e, xbuf=xbuf, xval=xval, scol=scol, war=war):
                    e.wait_ge(xval[0].h, xval[1])
                    if war > 0:
                        e.wait_ge(s_ln.h, war)
                    e.reduce_sum(out=scol, in_=xbuf[:], axis=AX.X
                                 ).then_inc(s_ve.h, 1)
                add("vector", v)
                s_ve.inc(1)

                aprev = s_ac.n

                def a(e, xbuf=xbuf, xval=xval, qcol=qcol, war=war,
                      aprev=aprev):
                    e.wait_ge(xval[0].h, xval[1])
                    if war > 0:
                        e.wait_ge(s_ln.h, war)
                    if aprev > 0:
                        e.wait_ge(s_ac.h, aprev)
                    e.activation(out=sqdump[:, 0:1024], in_=xbuf[:],
                                 func=AF.Square, accum_out=qcol
                                 ).then_inc(s_ac.h, 1)
                add("scalar", a)
                s_ac.inc(1)
            rval, aval = s_ve.n, s_ac.n
            csum4, csumsq4, cm4, cbarg4, cstd4, cs4, cneg4 = bc
            w0 = s_ve.inc(1)
            w1 = s_ve.inc(1)
            s_ve.inc(1)
            vdone = s_ve.n

            def vS(e, rval=rval, aval=aval, war=war, w0=w0, w1=w1):
                e.wait_ge(s_ve.h, rval)
                e.wait_ge(s_ac.h, aval)
                e.tensor_scalar(out=cm4, in0=csum4, scalar1=-R1024,
                                scalar2=None, op0=ALU.mult).then_inc(s_ve.h, 1)
                e.wait_ge(s_ve.h, w0)
                e.tensor_tensor(out=cbarg4, in0=cm4, in1=cm4, op=ALU.mult
                                ).then_inc(s_ve.h, 1)
                e.wait_ge(s_ve.h, w1)
                e.scalar_tensor_tensor(out=cbarg4, in0=csumsq4, scalar=R1024,
                                       in1=cbarg4, op0=ALU.mult,
                                       op1=ALU.subtract).then_inc(s_ve.h, 1)
            add("vector", vS)

            def aS(e, vdone=vdone):
                e.wait_ge(s_ve.h, vdone)
                e.activation(out=cstd4, in_=cbarg4, func=AF.Sqrt,
                             bias=epsc[:, 0:1]).then_inc(s_st.h, 1)
            add("scalar", aS)
            s_st.inc(1)
            adone = s_st.n

            w2 = s_ve.inc(1)
            s_ve.inc(1)
            svv = s_ve.n

            def v2S(e, adone=adone, w2=w2):
                e.wait_ge(s_st.h, adone)
                e.reciprocal(out=cs4, in_=cstd4).then_inc(s_ve.h, 1)
                e.wait_ge(s_ve.h, w2)
                e.scalar_tensor_tensor(out=cneg4, in0=cm4, scalar=1.0,
                                       in1=cs4, op0=ALU.mult, op1=ALU.mult
                                       ).then_inc(s_ve.h, 1)
            add("vector", v2S)
            scol4, negcol4, sv = cs4, cneg4, (s_ve, svv)

            def v3(e, tt=tt, sv=sv, scol4=scol4, negcol4=negcol4):
                e.wait_ge(sv[0].h, sv[1])
                e.tensor_copy(out=ln1_s[:, tt * 4:(tt + 1) * 4], in_=scol4)
                e.tensor_copy(out=ln1_b[:, tt * 4:(tt + 1) * 4], in_=negcol4
                              ).then_inc(s_ve.h, 1)
            add("vector", v3)
            s_ve.inc(1)
            dep = s_ve.n

            out = []
            tp_base = s_tp.n
            for cs in range(4):
                c = tt * 4 + cs
                xbuf, xval = xs[cs]
                hb = htok[c % 2]
                ht_need = tp_base + cs - 1 if cs >= 2 else (
                    htrans_vals[-1] if htrans_vals else 0)

                def a2(e, c=c, xbuf=xbuf, xval=xval, dep=dep, hb=hb,
                       ht_need=ht_need):
                    e.wait_ge(xval[0].h, xval[1])
                    e.wait_ge(s_ve.h, dep)
                    if ht_need > 0:
                        e.wait_ge(s_tp.h, ht_need)
                    e.activation(out=hb[:], in_=xbuf[:], func=AF.Identity,
                                 bias=ln1_b[:, c:c + 1], scale=ln1_s[:, c:c + 1]
                                 ).then_inc(s_ln.h, 1)
                add("scalar", a2)
                s_ln.inc(1)
                out.append((hb, s_ln.n, xbuf))
            return out

        def emit_front(c, phase):
            xbuf, xval = emit_xload(c, phase)
            if phase == 1:
                cols = stat_cols(0, c)
                csum, csumsq = cols[0], cols[1]

                def v(e, xbuf=xbuf, xval=xval, csum=csum):
                    e.wait_ge(xval[0].h, xval[1])
                    e.reduce_sum(out=csum, in_=xbuf[:], axis=AX.X).then_inc(s_ve.h, 1)
                s_ve.inc(1)
                rval = s_ve.n
                add("vector", v)

                def a(e, xbuf=xbuf, xval=xval, csumsq=csumsq):
                    e.wait_ge(xval[0].h, xval[1])
                    e.activation(out=sqdump[:, 0:1024], in_=xbuf[:], func=AF.Square,
                                 accum_out=csumsq).then_inc(s_ac.h, 1)
                s_ac.inc(1)
                aval = s_ac.n
                add("scalar", a)
                guard = (s_ln, s_ln.n - 1)
                scol, negcol, sv = emit_stats_math(
                    0, c, csum, csumsq, [(s_ve, rval), (s_ac, aval)], guard)

                def v3(e, c=c, sv=sv, scol=scol, negcol=negcol):
                    e.wait_ge(sv[0].h, sv[1])
                    e.tensor_copy(out=ln1_s[:, c:c + 1], in_=scol)
                    e.tensor_copy(out=ln1_b[:, c:c + 1], in_=negcol).then_inc(s_ve.h, 1)
                s_ve.inc(1)
                add("vector", v3)
                dep_sem, dep = s_ve, s_ve.n
            else:
                dep_sem, dep = s_wp, 0

            hb = htok[c % 2]
            nfront = len(htrans_vals)
            ht_need = htrans_vals[nfront - 2] if nfront >= 2 else 0

            def a2(e, c=c, xbuf=xbuf, xval=xval, dep=dep, hb=hb,
                   dep_sem=dep_sem, ht_need=ht_need):
                e.wait_ge(xval[0].h, xval[1])
                if dep > 0:
                    e.wait_ge(dep_sem.h, dep)
                if ht_need > 0:
                    e.wait_ge(s_tp.h, ht_need)
                e.activation(out=hb[:], in_=xbuf[:], func=AF.Identity,
                             bias=ln1_b[:, c:c + 1], scale=ln1_s[:, c:c + 1]
                             ).then_inc(s_ln.h, 1)
            add("scalar", a2)
            s_ln.inc(1)
            return hb, s_ln.n, xbuf

        def emit_htrans(c, hb, lnval, extra_pe_wait):
            cs = c % 4
            bf_need = nBf[0]

            def t(e, hb=hb, lnval=lnval, bf_need=bf_need):
                e.wait_ge(s_ln.h, lnval)
                if bf_need > 0:
                    e.wait_ge(f_Bf.h, bf_need)
                for j in range(8):
                    ins = e.transpose(out=pBf[:, j * 128:(j + 1) * 128],
                                      in_=hb[:, j * 128:(j + 1) * 128],
                                      identity=idb[:])
                ins.then_inc(s_tp.h, 1)
            add("tensor", t)
            s_tp.inc(1)
            htrans_vals.append(s_tp.n)
            tval = s_tp.n

            def v(e, cs=cs, tval=tval, extra=extra_pe_wait):
                e.wait_ge(s_tp.h, tval)
                if extra[1] > 0:
                    e.wait_ge(extra[0].h, extra[1])
                src = pBf[:, 0:1024].rearrange("p (j t) -> p j t", j=8)
                dst = hfm[:, 0:4096].rearrange("p (j t) -> p j t", j=8)[
                    :, :, cs * 128:(cs + 1) * 128]
                e.tensor_copy(out=dst, in_=src).then_inc(f_Bf.h, 1)
            add("vector", v)
            nBf[0] += 1
            f_Bf.inc(1)
            bfhalf[0] = bfhalf[1] = nBf[0]
            return f_Bf.n

        def emit_tile_proj(tt, which, hf_val, phase, sp_guard=None, w_wait=0,
                           fillers=None):
            if sp_guard is None:
                sp_guard = (s_sp[0], 0)
            wbuf = wsA
            bcol = bks if which == "k" else bqs
            xnt = xnk_t[tt % 2] if which == "k" else xnq_t
            sqc = sqk_c if which == "k" else sqq_c

            sq_jobs = []
            sq_base = s_sq.n
            prev_base = prev_sq_base[0]
            prev_sq_base[0] = sq_base

            for j in range(8):
                bank = j % 2
                po = pAB[:, bank * 512:(bank + 1) * 512]
                absnap = snap_reads(abreads, bank)

                def t(e, j=j, po=po, hf_val=hf_val, wbuf=wbuf, absnap=absnap,
                      w_wait=w_wait):
                    e.wait_ge(f_Bf.h, hf_val)
                    e.wait_ge(s_wp.h, prep_done)
                    if w_wait > 0:
                        e.wait_ge(s_w.h, w_wait)
                    wait_snap(e, absnap)
                    for kk in range(8):
                        ins = e.matmul(
                            out=po,
                            lhsT=wbuf[:, kk * D + j * 128: kk * D + (j + 1) * 128],
                            rhs=hfm[:, kk * 512:(kk + 1) * 512],
                            start=(kk == 0), stop=(kk == 7),
                            skip_group_check=True)
                    ins.then_inc(s_pj.h, 1)
                add("tensor", t)
                s_pj.inc(1)
                mmval = s_pj.n

                sq_guard = prev_base + 2 * j + 1 if prev_base >= 0 else 0

                def a(e, j=j, po=po, mmval=mmval, xnt=xnt, bcol=bcol,
                      sp_guard=sp_guard, sq_guard=sq_guard):
                    e.wait_ge(s_pj.h, mmval)
                    if sp_guard[1] > 0:
                        e.wait_ge(sp_guard[0].h, sp_guard[1])
                    if sq_guard > 0:
                        e.wait_ge(s_sq.h, sq_guard)
                    e.activation(out=xnt[:, j * 512:(j + 1) * 512], in_=po,
                                 func=AF.Identity, bias=bcol[:, j:j + 1],
                                 scale=1.0 / CP).then_inc(s_xk.h, 1)
                    e.activation(out=xnsqw[:, j * 512:(j + 1) * 512], in_=po,
                                 func=AF.Square, bias=bcol[:, j:j + 1],
                                 scale=1.0 / CP).then_inc(s_ac.h, 1)
                add("scalar", a)
                s_xk.inc(1)
                s_ac.inc(1)
                sqin = s_ac.n
                rec_read(abreads, bank, s_ac)
                rows_val = sq_base + 2 * j + 1
                sq_jobs.append((j, sqin, rows_val))

            for (j, sqin, rows_val) in sq_jobs:
                efsnap0 = snap_reads(efreads, 0)

                def t2(e, j=j, sqin=sqin, efsnap0=efsnap0):
                    e.wait_ge(s_ac.h, sqin)
                    wait_snap(e, efsnap0)
                    e.matmul(out=pEF[0:2, 0:512], lhsT=oblk[:],
                             rhs=xnsqw[:, j * 512:(j + 1) * 512],
                             start=True, stop=True, skip_group_check=True
                             ).then_inc(s_sq.h, 1)
                add("tensor", t2)
                s_sq.inc(1)

                def v(e, rows_val=rows_val):
                    e.wait_ge(s_sq.h, rows_val)
                    e.tensor_copy(out=sqrow[:], in_=pEF[0:2, 0:512]
                                  ).then_inc(s_sqe.h, 1)
                add("vector", v)
                s_sqe.inc(1)
                rev = s_sqe.n
                rec_read(efreads, 0, s_sqe)
                if fillers:
                    fillers.pop(0)()

                efsnap1 = snap_reads(efreads, 1)

                def t3(e, rev=rev, efsnap1=efsnap1):
                    e.wait_ge(s_sqe.h, rev)
                    wait_snap(e, efsnap1)
                    for cs_ in range(4):
                        ins = e.transpose(
                            out=pEF[:, 512 + cs_ * 2:512 + cs_ * 2 + 2],
                            in_=sqrow[:, cs_ * 128:(cs_ + 1) * 128],
                            identity=id2[:])
                    ins.then_inc(s_sq.h, 1)
                add("tensor", t3)
                s_sq.inc(1)
                trval = s_sq.n

                def v2(e, tt=tt, j=j, trval=trval, sqc=sqc):
                    e.wait_ge(s_sq.h, trval)
                    src = pEF[:, 512:520].rearrange("p (c two) -> p c two", two=2)
                    dst = sqc[:, 0:NCH * 16].rearrange("p (cc hh) -> p cc hh", hh=16)[
                        :, tt * 4:(tt + 1) * 4, 2 * j:2 * j + 2]
                    e.tensor_copy(out=dst, in_=src).then_inc(s_sqe.h, 1)
                add("vector", v2)
                s_sqe.inc(1)
                rec_read(efreads, 1, s_sqe)
            return s_xk.n, s_sqe.n

        def emit_vproj_half(c, half, hf_val, sp_guard):
            cs = c % 4
            vt = vau_t[(c // 4) % 2]
            po = pAB[:, half * 512:(half + 1) * 512]
            absnap = snap_reads(abreads, half)

            def t(e, cs=cs, half=half, po=po, hf_val=hf_val, absnap=absnap):
                e.wait_ge(f_Bf.h, hf_val)
                wait_snap(e, absnap)
                for kk in range(8):
                    e.matmul(
                        out=po,
                        lhsT=hfm[:, kk * 512 + cs * 128: kk * 512 + (cs + 1) * 128],
                        rhs=wsB[:, kk * D + half * 512: kk * D + (half + 1) * 512],
                        start=(kk == 0), stop=False, skip_group_check=True)
                e.matmul(out=po, lhsT=orow[:, 0:128],
                         rhs=bvb[:, half * 512:(half + 1) * 512],
                         start=False, stop=True, skip_group_check=True
                         ).then_inc(s_pj.h, 1)
            add("tensor", t)
            s_pj.inc(1)
            mmval = s_pj.n

            def v(e, c=c, cs=cs, half=half, po=po, mmval=mmval, vt=vt,
                  sp_guard=sp_guard):
                e.wait_ge(s_pj.h, mmval)
                if sp_guard[1] > 0:
                    e.wait_ge(sp_guard[0].h, sp_guard[1])
                src = po.rearrange("p (hh d) -> p hh d", hh=8)
                dst = vt[:, cs * 1040:(cs + 1) * 1040].rearrange(
                    "p (hh dd) -> p hh dd", hh=16)[:, half * 8:(half + 1) * 8, 0:64]
                e.tensor_copy(out=dst, in_=src).then_inc(s_va.h, 1)
            add("vector", v)
            s_va.inc(1)
            rec_read(abreads, half, s_va)
            return s_va.n

        def emit_phi(c, which, deps, etbuf=None, fills=None, end=None):
            xnt = xnk_t[(c // 4) % 2] if which == "k" else xnq_t
            sqc = sqk_c if which == "k" else sqq_c
            et = etbuf if etbuf is not None else (etok if which == "k" else eqtok)
            acol = 16 if which == "k" else 0
            cs = c % 4
            bk = c % 2
            po = pAB[:, bk * 512:(bk + 1) * 512]

            alb_guard = prev_alb[which]

            def a0(e, c=c, sqc=sqc, deps=deps, acol=acol, alb_guard=alb_guard):
                for smh, vv in deps:
                    if vv > 0:
                        e.wait_ge(smh.h, vv)
                if alb_guard > 0:
                    e.wait_ge(s_ve.h, alb_guard)
                src = sqc[:, 0:NCH * 16].rearrange("p (cc hh) -> p cc hh", hh=16)[:, c, :]
                e.wait_ge(s_wp.h, prep_done)
                e.activation(out=alf[:, acol:acol + 16], in_=src,
                             func=AF.Exp, bias=lsmn[:, 0:1], scale=-1.0
                             ).then_inc(s_ac.h, 1)
            add("scalar", a0)
            s_ac.inc(1)
            alval = s_ac.n

            def v0(e, alval=alval, acol=acol):
                e.wait_ge(s_ac.h, alval)
                e.tensor_copy(out=alfb[:, acol:acol + 16],
                              in_=alf[:, acol:acol + 16]).then_inc(s_ve.h, 1)
            add("vector", v0)
            s_ve.inc(1)
            albval = s_ve.n
            prev_alb[which] = albval

            ue_hist = []
            for rnd in range(4):
                X = bk if rnd % 2 == 0 else 1 - bk
                poX = pAB[:, X * 512:(X + 1) * 512]
                if rnd < 2:
                    waitset = snap_reads(abreads, X)
                    rdeps = tuple(deps)
                else:
                    waitset = ((s_ue, ue_hist[rnd - 2]),)
                    rdeps = ()

                def t(e, cs=cs, rnd=rnd, xnt=xnt, rdeps=rdeps,
                      waitset=waitset, poX=poX):
                    for smh, vv in rdeps:
                        if vv > 0:
                            e.wait_ge(smh.h, vv)
                    wait_snap(e, waitset)
                    for pp in range(2):
                        pair = rnd * 2 + pp
                        lhs = xnt[:, pair * 512 + cs * 128: pair * 512 + (cs + 1) * 128]
                        ins = e.matmul(out=poX[:, pp * 256:(pp + 1) * 256],
                                       lhsT=lhs, rhs=projb[:],
                                       start=True, stop=True,
                                       skip_group_check=True)
                    ins.then_inc(s_um.h, 1)
                add("tensor", t)
                s_um.inc(1)
                umval = s_um.n

                def a(e, rnd=rnd, umval=umval, et=et, poX=poX):
                    e.wait_ge(s_um.h, umval)
                    e.activation(out=et[:, rnd * 512:(rnd + 1) * 512],
                                 in_=poX, func=AF.Exp).then_inc(s_ue.h, 1)
                add("scalar", a)
                s_ue.inc(1)
                rec_read(abreads, X, s_ue)
                ue_hist.append(s_ue.n)
                if fills is not None and rnd in fills:
                    fills[rnd]()
            ue_last = ue_hist[-1]
            if end is not None:
                end()

            wv0 = s_ve.inc(1)

            def v(e, ueval=ue_last, albval=albval, et=et, acol=acol, wv0=wv0):
                e.wait_ge(s_ue.h, ueval)
                e.wait_ge(s_ve.h, albval)
                a3 = alfb[:, acol:acol + 16].unsqueeze(2).broadcast_to([128, 16, 128])
                e3 = et[:, 0:2048].rearrange("p (hh m) -> p hh m", hh=16)
                e.tensor_tensor(out=e3, in0=e3, in1=a3, op=ALU.mult
                                ).then_inc(s_ve.h, 1)
                e.wait_ge(s_ve.h, wv0)
                e.tensor_scalar(out=et[:, 0:2048], in0=et[:, 0:2048],
                                scalar1=EPS, scalar2=None, op0=ALU.add
                                ).then_inc(s_ph.h, 1)
            add("vector", v)
            s_ph.inc(1)
            return s_ph.n

        def emit_kv(c, phval, vwaits, phase, snval=None, first=False, etbuf=None):
            vt = vau_t[(c // 4) % 2]
            cs = c % 4
            et = etbuf if etbuf is not None else etok
            kvorder = [0, 7, 14, 1, 8, 15, 2, 9, 3, 10, 4, 11, 5, 12, 6, 13]

            def t(e, cs=cs, phval=phval, vwaits=tuple(vwaits), vt=vt, first=first,
                  snval=snval, et=et):
                e.wait_ge(s_ph.h, phval)
                for smh, vv in vwaits:
                    if vv > 0:
                        e.wait_ge(smh.h, vv)
                if snval is not None:
                    e.wait_ge(s_sn.h, snval)
                for h in kvorder:
                    st = first and (h in (0, 7, 14))
                    ins = e.matmul(out=s_slice(h),
                                   lhsT=et[:, h * 128:(h + 1) * 128],
                                   rhs=vt[:, cs * 1040 + h * 65: cs * 1040 + (h + 1) * 65],
                                   start=st, stop=False, skip_group_check=True)
                ins.then_inc(s_kv.h, 1)
            add("tensor", t)
            s_kv.inc(1)
            return s_kv.n

        def finish():
            block = ctx.enter_context(nc.Block())

            @block.sync
            def _(e):
                for f in Q["sync"]:
                    f(e)

            @block.gpsimd
            def _(e):
                for f in Q["gpsimd"]:
                    f(e)

            @block.vector
            def _(e):
                for f in Q["vector"]:
                    f(e)

            @block.scalar
            def _(e):
                for f in Q["scalar"]:
                    f(e)

            @block.tensor
            def _(e):
                for f in Q["tensor"]:
                    f(e)

        tile_uk_done = {}
        tile_m6_done = {}
        last_proj_pe = [(s_pj, 0)]
        dbg_last = [(s_ln, 0)]

        for tt in range(_NTTRUN):
            hf_tile = None
            fr = emit_front1_batch(tt)
            for cs in range(4):
                c = tt * 4 + cs
                hb, lnval, _ = fr[cs]
                if _P1SUB >= 2:
                    hf_tile = emit_htrans(c, hb, lnval,
                                          last_proj_pe[0] if cs == 0 else (s_pj, 0))
            dbg_last[0] = (s_ln, s_ln.n) if _P1SUB < 2 else (f_Bf, f_Bf.n)
            if _P1SUB < 3:
                continue
            prev_sp = ((s_sp[tt % 2], (tt // 2) * 32)
                       if (tt >= 2 and _P1SUB >= 7) else (s_sp[0], 0))
            vav = {}

            def mk_fill(c, half, hf_tile=hf_tile, prev_sp=prev_sp, vav=vav):
                def f():
                    vav[(c % 4, half)] = emit_vproj_half(c, half, hf_tile,
                                                         prev_sp)
                return f

            fillers = ([mk_fill(tt * 4 + cs, half)
                        for cs in range(4) for half in range(2)]
                       if _P1SUB >= 4 else None)
            xkval, sqev = emit_tile_proj(tt, "k", hf_tile, 1, sp_guard=prev_sp,
                                         fillers=fillers)
            dbg_last[0] = (s_sqe, s_sqe.n)
            if _P1SUB < 4:
                continue
            vavals = [vav[(cs, 1)] for cs in range(4)]
            last_proj_pe[0] = (s_pj, s_pj.n)
            dbg_last[0] = (s_va, s_va.n)
            if _P1SUB < 5:
                continue

            kv_pend = None
            for cs in range(4):
                c = tt * 4 + cs
                etb = etok if c % 2 == 0 else eqtok
                phv = emit_phi(c, "k", [(s_xk, xkval), (s_sqe, sqev)], etbuf=etb)
                if _P1SUB >= 6:
                    if kv_pend is not None:
                        emit_kv(*kv_pend)
                    kv_pend = (c, phv, [(s_va, vavals[cs])], 1, None, c == 0, etb)
            if kv_pend is not None:
                emit_kv(*kv_pend)
            tile_uk_done[tt] = s_um.n
            dbg_last[0] = (s_ph, s_ph.n) if _P1SUB < 6 else (s_kv, s_kv.n)
            if _P1SUB < 7:
                continue

            kvv = s_kv.n
            xnt = xnk_t[tt % 2]
            vt = vau_t[tt % 2]
            ssp = s_sp[tt % 2]

            def sp(e, tt=tt, kvv=kvv, xnt=xnt, vt=vt, ssp=ssp):
                e.wait_ge(s_kv.h, kvv)
                e.dma_start(out=xnk_sp[:, tt * 4096:(tt + 1) * 4096],
                            in_=xnt[:]).then_inc(ssp.h, 16)
                e.dma_start(out=vau_sp[:, tt * 4160:(tt + 1) * 4160],
                            in_=vt[:]).then_inc(ssp.h, 16)
            add("gpsimd", sp)
            ssp.inc(32)

        p1_kv = s_kv.n
        p1_sp = (s_sp[0].n, s_sp[1].n)
        p1_ln_box[0] = s_ln.n

        if _P1SUB < 8:
            def stD(e, w=dbg_last[0][1], sh=dbg_last[0][0]):
                if w > 0:
                    e.wait_ge(sh.h, w)
                e.dma_start(out=yseg[0:128, 0:1024], in_=sfin[:, 0:1024]
                            ).then_inc(s_ou[0].h, 16)
                e.wait_ge(s_ou[0].h, 16)
            add("sync", stD)
            finish()
            return nc

        def v_exc(e, p1_kv=p1_kv):
            e.wait_ge(s_kv.h, p1_kv)
            e.tensor_copy(out=sfin[:, 0:455], in_=S0[:])
            e.tensor_copy(out=sfin[:, 455:910], in_=S1[:])
            e.tensor_copy(out=sfin[:, 910:1040], in_=S2[:]).then_inc(s_ve.h, 1)
        add("vector", v_exc)
        s_ve.inc(1)
        evacval = s_ve.n

        if _STAGE == 1:
            def st1(e, v=evacval):
                e.wait_ge(s_ve.h, v)
                e.dma_start(out=yseg[0:128, 0:1024], in_=sfin[:, 0:1024]
                            ).then_inc(s_ou[0].h, 16)
                e.dma_start(out=yseg[128:256, 0:16], in_=sfin[:, 1024:1040]
                            ).then_inc(s_ou[0].h, 16)
                e.wait_ge(s_ou[0].h, 32)
            add("sync", st1)
            finish()
            return nc

        p1_pj = s_pj.n

        def g_wreA(e, p1_pj=p1_pj):
            e.wait_ge(s_pj.h, p1_pj)
            e.dma_start(out=wsA[:], in_=wqT.rearrange("(j p) o -> p j o", p=128)
                        ).then_inc(s_w.h, 16)
        add("gpsimd", g_wreA)
        s_w.inc(16)
        wreA_val = s_w.n

        def g_wreB(e, p1_pj=p1_pj):
            e.dma_start(out=wsB[:], in_=woT.rearrange("(j p) o -> p j o", p=128)
                        ).then_inc(s_wb.h, 16)
        add("gpsimd", g_wreB)
        s_wb.inc(16)
        wreB_val = s_wb.n

        def emit_p2_head(tt):
            prev_uk = tile_uk_done.get(("p2", tt - 2), 0)
            prev_m6 = tile_m6_done.get(tt - 2, 0)
            srb = s_rb[tt % 2]
            rbval = (srb, srb.inc(32))

            def frb(e, tt=tt, prev_uk=prev_uk, prev_m6=prev_m6, srb=srb):
                e.wait_ge(s_sp[0].h, p1_sp[0])
                e.wait_ge(s_sp[1].h, p1_sp[1])
                if prev_uk > 0:
                    e.wait_ge(s_um.h, prev_uk)
                if prev_m6 > 0:
                    e.wait_ge(s_au.h, prev_m6)
                e.dma_start(out=xnk_t[tt % 2][:],
                            in_=xnk_sp[:, tt * 4096:(tt + 1) * 4096]
                            ).then_inc(srb.h, 16)
                e.dma_start(out=vau_t[tt % 2][:],
                            in_=vau_sp[:, tt * 4160:(tt + 1) * 4160]
                            ).then_inc(srb.h, 16)
            add("sync", frb)

            hf_tile = None
            xbufs_tile = []
            for cs in range(4):
                c = tt * 4 + cs
                hb, lnval, xbuf_c = emit_front(c, 2)
                xbufs_tile.append(xbuf_c)
                hf_tile = emit_htrans(c, hb, lnval,
                                      last_proj_pe[0] if cs == 0 else (s_pj, 0))
            xqval, sqqev = emit_tile_proj(tt, "q", hf_tile, 2, w_wait=wreA_val)
            last_proj_pe[0] = (s_pj, s_pj.n)
            return rbval, xqval, sqqev, xbufs_tile

        p2_head0 = emit_p2_head(0)

        def g_exc(e, evacval=evacval, p1_sp=p1_sp):
            e.wait_ge(s_ve.h, evacval)
            e.wait_ge(s_sp[0].h, p1_sp[0])
            e.wait_ge(s_sp[1].h, p1_sp[1])
            e.dma_start(out=cc_in[:, :], in_=sfin[:]).then_inc(s_cc.h, 16)
            e.wait_ge(s_cc.h, 16)
            e.collective_compute(
                "AllGather", ALU.bypass,
                replica_groups=[[0, 1, 2, 3], [4, 5, 6, 7]],
                ins=[cc_in[:, :].opt()], outs=[cc_out[:, :].opt()],
            ).then_inc(s_ccd.h, 1)
        add("gpsimd", g_exc)
        s_cc.inc(16)
        s_ccd.inc(1)
        ccval = s_ccd.n

        def v_ms(e, ccval=ccval):
            e.wait_ge(s_ccd.h, ccval)
            e.memset(sinit[:], 0.0).then_inc(s_ve.h, 1)
        add("vector", v_ms)
        s_ve.inc(1)
        gw = s_ve.n

        for r in range(4):
            def g_g(e, r=r, gw=gw):
                e.wait_ge(s_ve.h, gw)
                e.dma_start(out=gbuf[:],
                            in_=cc_out[r * 128:(r + 1) * 128, :]).then_inc(s_cc.h, 16)
            add("gpsimd", g_g)
            s_cc.inc(16)
            gval = s_cc.n

            def v_g(e, r=r, gval=gval):
                e.wait_ge(s_cc.h, gval)
                e.scalar_tensor_tensor(out=sinit[:], in0=gbuf[:],
                                       scalar=segw[:, r:r + 1], in1=sinit[:],
                                       op0=ALU.mult, op1=ALU.add
                                       ).then_inc(s_ve.h, 1)
            add("vector", v_g)
            s_ve.inc(1)
            gw = s_ve.n
        initval = s_ve.n

        if _STAGE == 2:
            def st2(e, v=initval):
                e.wait_ge(s_ve.h, v)
                e.dma_start(out=yseg[0:128, 0:1024], in_=sinit[:, 0:1024]
                            ).then_inc(s_ou[0].h, 16)
                e.dma_start(out=yseg[128:256, 0:16], in_=sinit[:, 1024:1040]
                            ).then_inc(s_ou[0].h, 16)
                e.wait_ge(s_ou[0].h, 32)
            add("sync", st2)
            finish()
            return nc

        def t_init(e, initval=initval):
            e.wait_ge(s_ve.h, initval)
            e.matmul(out=S0[:], lhsT=idS[:], rhs=sinit[:, 0:455], start=True,
                     stop=False, skip_group_check=True)
            e.matmul(out=S1[:], lhsT=idS[:], rhs=sinit[:, 455:910], start=True,
                     stop=False, skip_group_check=True)
            e.matmul(out=S2[:], lhsT=idS[:], rhs=sinit[:, 910:1040], start=True,
                     stop=False, skip_group_check=True).then_inc(s_pe.h, 1)
        add("tensor", t_init)
        s_pe.inc(1)
        sinj = s_pe.n

        def emit_tail_tT(tc, half):
            atb = tc["atb"]
            epval = tc["ep"] - 1 + half
            bf_need = nBf[0]

            def tT(e, atb=atb, half=half, epval=epval, bf_need=bf_need):
                e.wait_ge(s_ep.h, epval)
                if bf_need > 0:
                    e.wait_ge(f_Bf.h, bf_need)
                for j in range(4):
                    ins = e.transpose(
                        out=pBf[:, j * 128:(j + 1) * 128],
                        in_=atb[:, (half * 4 + j) * 128:
                                (half * 4 + j + 1) * 128],
                        identity=idb[:])
                ins.then_inc(s_tp.h, 1)
            add("tensor", tT)
            s_tp.inc(1)
            if half == 1:
                attn_tp_vals.append(s_tp.n)
            tc["tT"] = s_tp.n

        def emit_tail_vT(tc, half):
            tval = tc["tT"]
            wo_pe_prev = s_pj.n

            def vT(e, tval=tval, half=half, wo_prev=wo_pe_prev):
                e.wait_ge(s_tp.h, tval)
                if wo_prev > 0:
                    e.wait_ge(s_pj.h, wo_prev)
                e.tensor_copy(out=atfm[:, half * 512:(half + 1) * 512],
                              in_=pBf[:, 0:512]).then_inc(f_Bf.h, 1)
            add("vector", vT)
            nBf[0] += 1
            f_Bf.inc(1)
            bfhalf[0] = bfhalf[1] = nBf[0]
            if half == 1:
                tc["atval"] = f_Bf.n
                tc["cols2"] = stat_cols(1, tc["c"])
                tc["y2"] = yf[tc["c"] % 2]
                tc["ln2_evac"] = []

        def emit_tail_wo(tc, half):
            bk, PO, atval = tc["bk"], tc["PO"], tc["atval"]
            cols2, y2 = tc["cols2"], tc["y2"]
            absnap = snap_reads(abreads, bk)

            def tW(e, half=half, atval=atval, absnap=absnap,
                   w_wait=wreB_val, PO=PO):
                e.wait_ge(s_wb.h, w_wait)
                e.wait_ge(f_Bf.h, atval)
                wait_snap(e, absnap)
                for kk in range(8):
                    e.matmul(
                        out=PO, lhsT=atfm[:, kk * 128:(kk + 1) * 128],
                        rhs=wsB[:, kk * D + half * 512: kk * D + (half + 1) * 512],
                        start=(kk == 0), stop=False,
                        skip_group_check=True)
                e.matmul(out=PO, lhsT=orow[:, 0:128],
                         rhs=bob[:, half * 512:(half + 1) * 512],
                         start=False, stop=True, skip_group_check=True
                         ).then_inc(s_pj.h, 1)
            add("tensor", tW)
            s_pj.inc(1)
            woval = s_pj.n

            def aW(e, half=half, woval=woval, y2=y2, cols2=cols2, PO=PO):
                e.wait_ge(s_pj.h, woval)
                e.activation(out=y2[:, half * 512:(half + 1) * 512],
                             in_=PO, func=AF.Identity,
                             accum_out=cols2[0] if half == 0 else cols2[3]
                             ).then_inc(s_ac.h, 1)
            add("scalar", aW)
            s_ac.inc(1)
            rec_read(abreads, bk, s_ac)
            tc["ln2_evac"].append(s_ac.n)

        def emit_tail_b(tc):
            c, xbuf = tc["c"], tc["xbuf"]
            cols2, y2, ln2_evac = tc["cols2"], tc["y2"], tc["ln2_evac"]
            def aWs(e, ev=ln2_evac[1], cols2=cols2):
                e.wait_ge(s_ac.h, ev)
                e.activation(out=cols2[0], in_=cols2[0], func=AF.Identity,
                             bias=cols2[3]).then_inc(s_ac.h, 1)
            add("scalar", aWs)
            s_ac.inc(1)

            def aW2(e, ev=ln2_evac[1], y2=y2, cols2=cols2):
                e.wait_ge(s_ac.h, ev)
                e.activation(out=sqdump[:, 0:1024], in_=y2[:], func=AF.Square,
                             accum_out=cols2[1]).then_inc(s_ac.h, 1)
            add("scalar", aW2)
            s_ac.inc(1)
            sq2val = s_ac.n

            s2, neg2, st2 = emit_stats_math(
                1, c, cols2[0], cols2[1], [(s_ac, sq2val)], (s_ve, 0))

            lsc = lnsc[c % 2]

            def aW3(e, y2=y2, st2=st2, s2=s2, neg2=neg2, lsc=lsc):
                e.wait_ge(st2[0].h, st2[1])
                e.activation(out=lsc[:], in_=y2[:], func=AF.Identity,
                             bias=neg2, scale=s2).then_inc(s_ac.h, 1)
            add("scalar", aW3)
            s_ac.inc(1)
            ln2val = s_ac.n

            x1 = x1b[c % 2]
            cols3 = stat_cols(2, c)
            x1_guard = s_ou[c % 2].n

            def vR(e, lsc=lsc, ln2val=ln2val, x1=x1, xbuf=xbuf,
                   cols3=cols3, x1_guard=x1_guard, sou_g=s_ou[c % 2]):
                e.wait_ge(s_ac.h, ln2val)
                if x1_guard > 0:
                    e.wait_ge(sou_g.h, x1_guard)
                e.scalar_tensor_tensor(out=x1[:], in0=lsc[:], scalar=1.0,
                                       in1=xbuf[:], op0=ALU.mult, op1=ALU.add,
                                       accum_out=cols3[0]).then_inc(s_ve.h, 1)
            add("vector", vR)
            s_ve.inc(1)
            x1vals.append(s_ve.n)
            x1val = s_ve.n

            def aL3(e, x1=x1, x1val=x1val, cols3=cols3):
                e.wait_ge(s_ve.h, x1val)
                e.activation(out=sqdump[:, 0:1024], in_=x1[:], func=AF.Square,
                             accum_out=cols3[1]).then_inc(s_ac.h, 1)
            add("scalar", aL3)
            s_ac.inc(1)
            sq3val = s_ac.n

            s3, neg3, st3 = emit_stats_math(
                2, c, cols3[0], cols3[1], [(s_ve, x1val), (s_ac, sq3val)],
                (s_ve, 0))

            h3 = h3tok[c % 2]

            def aL3b(e, x1=x1, st3=st3, s3=s3, neg3=neg3, h3=h3):
                e.wait_ge(st3[0].h, st3[1])
                e.activation(out=h3[:], in_=x1[:], func=AF.Identity,
                             bias=neg3, scale=s3).then_inc(s_ac.h, 1)
            add("scalar", aL3b)
            s_ac.inc(1)
            tc["h3"], tc["h3val"], tc["lsc"], tc["x1"] = h3, s_ac.n, lsc, x1

        def emit_tail_h3(tc):
            h3, h3val = tc["h3"], tc["h3val"]
            bf_need = nBf[0]

            def tH3(e, h3=h3, h3val=h3val, bf_need=bf_need):
                e.wait_ge(s_ac.h, h3val)
                if bf_need > 0:
                    e.wait_ge(f_Bf.h, bf_need)
                for j in range(8):
                    ins = e.transpose(out=pBf[:, j * 128:(j + 1) * 128],
                                      in_=h3[:, j * 128:(j + 1) * 128],
                                      identity=idb[:])
                ins.then_inc(s_tp.h, 1)
            add("tensor", tH3)
            s_tp.inc(1)
            tval = s_tp.n
            ffn_prev = s_pj.n
            tc["tH3"] = (tval, ffn_prev)

        def emit_tail_f(tc):
            c, bk, PO = tc["c"], tc["bk"], tc["PO"]
            lsc, x1 = tc["lsc"], tc["x1"]
            tval, ffn_prev = tc["tH3"]

            def vH3(e, tval=tval, ffn_prev=ffn_prev):
                e.wait_ge(s_tp.h, tval)
                if ffn_prev > 0:
                    e.wait_ge(s_pj.h, ffn_prev)
                e.tensor_copy(out=h3fm[:], in_=pBf[:, 0:1024]).then_inc(f_Bf.h, 1)
            add("vector", vH3)
            nBf[0] += 1
            f_Bf.inc(1)
            bfhalf[0] = bfhalf[1] = nBf[0]
            h3fval = f_Bf.n
            cols4 = stat_cols(3, c)
            fb = yf[c % 2]
            ffn_evac = []
            for half in range(2):
                absnap = snap_reads(abreads, bk)

                def tF(e, half=half, h3fval=h3fval, absnap=absnap, PO=PO):
                    e.wait_ge(f_Bf.h, h3fval)
                    wait_snap(e, absnap)
                    for kk in range(8):
                        ins = e.matmul(
                            out=PO, lhsT=h3fm[:, kk * 128:(kk + 1) * 128],
                            rhs=wsC[:, kk * D + half * 512: kk * D + (half + 1) * 512],
                            start=(kk == 0), stop=(kk == 7),
                            skip_group_check=True)
                    ins.then_inc(s_pj.h, 1)
                add("tensor", tF)
                s_pj.inc(1)
                ffval = s_pj.n

                def aF(e, half=half, ffval=ffval, fb=fb, cols4=cols4, PO=PO):
                    e.wait_ge(s_pj.h, ffval)
                    e.activation(out=fb[:, half * 512:(half + 1) * 512],
                                 in_=PO, func=AF.Identity,
                                 accum_out=cols4[0] if half == 0 else cols4[3]
                                 ).then_inc(s_ac.h, 1)
                add("scalar", aF)
                s_ac.inc(1)
                rec_read(abreads, bk, s_ac)
                ffn_evac.append(s_ac.n)

            def aFs(e, ev=ffn_evac[1], cols4=cols4):
                e.wait_ge(s_ac.h, ev)
                e.activation(out=cols4[0], in_=cols4[0], func=AF.Identity,
                             bias=cols4[3]).then_inc(s_ac.h, 1)
            add("scalar", aFs)
            s_ac.inc(1)

            def aF2(e, ev=ffn_evac[1], fb=fb, cols4=cols4):
                e.wait_ge(s_ac.h, ev)
                e.activation(out=sqdump[:, 0:1024], in_=fb[:], func=AF.Square,
                             accum_out=cols4[1]).then_inc(s_ac.h, 1)
            add("scalar", aF2)
            s_ac.inc(1)
            sq4val = s_ac.n

            s4, neg4, st4 = emit_stats_math(
                3, c, cols4[0], cols4[1], [(s_ac, sq4val)], (s_ve, 0))

            def aL4(e, fb=fb, st4=st4, s4=s4, neg4=neg4, lsc=lsc):
                e.wait_ge(st4[0].h, st4[1])
                e.activation(out=lsc[:], in_=fb[:], func=AF.Identity,
                             bias=neg4, scale=s4).then_inc(s_ac.h, 1)
            add("scalar", aL4)
            s_ac.inc(1)
            ln4val = s_ac.n

            sou = s_ou[c % 2]

            def vO(e, lsc=lsc, ln4val=ln4val, x1=x1):
                e.wait_ge(s_ac.h, ln4val)
                e.tensor_add(out=x1[:], in0=x1[:], in1=lsc[:]).then_inc(s_ve.h, 1)
            add("vector", vO)
            s_ve.inc(1)
            oval = s_ve.n

            def fO(e, c=c, x1=x1, oval=oval, sou=sou):
                e.wait_ge(s_ve.h, oval)
                e.dma_start(out=yseg[c * 128:(c + 1) * 128, :], in_=x1[:]
                            ).then_inc(sou.h, 16)
            add("gpsimd", fO)
            sou.inc(16)

        ptc = None
        for tt in range(_NTTRUN):
            if tt == 0:
                rbval, xqval, sqqev, xbufs_tile = p2_head0
            else:
                rbval, xqval, sqqev, xbufs_tile = emit_p2_head(tt)

            for cs in range(4):
                c = tt * 4 + cs
                xbuf = xbufs_tile[cs]
                bk = c % 2
                PO = pAB[:, bk * 512:(bk + 1) * 512]
                prev = ptc

                spb = sprev[c % 2]
                kv_prev = s_kv.n
                m5_prev = s_au.n - 16

                def vsn(e, spb=spb, kv_prev=kv_prev, m5_prev=m5_prev, sinj=sinj):
                    e.wait_ge(s_pe.h, sinj)
                    e.wait_ge(s_kv.h, kv_prev)
                    if m5_prev > 0:
                        e.wait_ge(s_au.h, m5_prev)
                    e.tensor_copy(out=spb[:, 0:455], in_=S0[:])
                    e.tensor_copy(out=spb[:, 455:910], in_=S1[:])
                    e.tensor_copy(out=spb[:, 910:1040], in_=S2[:]).then_inc(s_sn.h, 1)
                add("vector", vsn)
                s_sn.inc(1)
                snval = s_sn.n

                def emit_fm(et, fmdst, phv, rnd):
                    bf_need = nBf[0]

                    def t(e, et=et, rnd=rnd, phv=phv, bf_need=bf_need):
                        e.wait_ge(s_ph.h, phv)
                        if bf_need > 0:
                            e.wait_ge(f_Bf.h, bf_need)
                        for hh in range(8):
                            h = rnd * 8 + hh
                            ins = e.transpose(
                                out=pBf[:, hh * 128:(hh + 1) * 128],
                                in_=et[:, h * 128:(h + 1) * 128],
                                identity=idb[:])
                        ins.then_inc(s_tp.h, 1)
                    add("tensor", t)
                    s_tp.inc(1)
                    tval = s_tp.n

                    def v(e, fmdst=fmdst, rnd=rnd, tval=tval):
                        e.wait_ge(s_tp.h, tval)
                        e.tensor_copy(out=fmdst[:, rnd * 1024:(rnd + 1) * 1024],
                                      in_=pBf[:, 0:1024]).then_inc(f_Bf.h, 1)
                    add("vector", v)
                    nBf[0] += 1
                    f_Bf.inc(1)
                    bfhalf[0] = bfhalf[1] = nBf[0]

                phq = emit_phi(c, "q", [(s_xk, xqval), (s_sqe, sqqev)])

                def fill1(p=prev):
                    if p is not None:
                        emit_tail_tT(p, 0)

                def fill2(p=prev):
                    if p is not None:
                        emit_tail_vT(p, 0)
                    emit_fm(eqtok, qfm, phq, 0)

                def fill3(p=prev):
                    if p is not None:
                        emit_tail_tT(p, 1)

                phk = emit_phi(c, "k", [rbval],
                               fills={1: fill1, 2: fill2, 3: fill3})

                if prev is not None:
                    emit_tail_vT(prev, 1)
                emit_fm(eqtok, qfm, phq, 1)
                if prev is not None:
                    emit_tail_wo(prev, 0)
                emit_kv(c, phk, [rbval], 2, snval=snval)
                emit_fm(etok, kfm, phk, 0)
                if prev is not None:
                    emit_tail_wo(prev, 1)
                emit_fm(etok, kfm, phk, 1)
                fm_done = f_Bf.n

                vt = vau_t[tt % 2]
                for half in range(2):
                    if half == 1 and prev is not None:
                        emit_tail_b(prev)
                    vavals_r = []
                    for rr in range(2):
                        X = bk if (half * 2 + rr) % 2 == 0 else 1 - bk
                        POX = pAB[:, X * 512:(X + 1) * 512]
                        absnap = snap_reads(abreads, X)

                        def tA(e, half=half, rr=rr, fm_done=fm_done,
                               absnap=absnap, POX=POX):
                            e.wait_ge(f_Bf.h, fm_done)
                            wait_snap(e, absnap)
                            for q4 in range(4):
                                h = half * 8 + rr * 4 + q4
                                ins = e.matmul(out=POX[:, q4 * 128:(q4 + 1) * 128],
                                               lhsT=kfm[:, h * 128:(h + 1) * 128],
                                               rhs=qfm[:, h * 128:(h + 1) * 128],
                                               start=True, stop=True,
                                               skip_group_check=True)
                            ins.then_inc(s_pe.h, 1)
                        add("tensor", tA)
                        s_pe.inc(1)
                        aval = s_pe.n

                        am_guard = s_au.n - 4

                        def vA(e, rr=rr, aval=aval, am_guard=am_guard, POX=POX):
                            e.wait_ge(s_pe.h, aval)
                            if am_guard > 0:
                                e.wait_ge(s_au.h, am_guard)
                            src = POX.rearrange("p (q4 cc) -> p q4 cc", q4=4)
                            m3 = triuT[:].unsqueeze(1).broadcast_to([128, 4, 128])
                            dst = amask[:, rr * 512:(rr + 1) * 512].rearrange(
                                "p (q4 cc) -> p q4 cc", q4=4)
                            e.tensor_tensor(out=dst, in0=src, in1=m3,
                                            op=ALU.mult).then_inc(s_am.h, 1)
                        add("vector", vA)
                        s_am.inc(1)
                        rec_read(abreads, X, s_am)
                        vavals_r.append(s_am.n)

                    if half == 1 and prev is not None:
                        emit_tail_h3(prev)
                    for rr in range(2):
                        amval = vavals_r[rr]
                        for q4 in range(4):
                            hh = rr * 4 + q4
                            h = half * 8 + hh
                            ef_bank = hh // 4
                            base = ef_bank * 512 + (hh % 4) * 65
                            efsnap = snap_reads(efreads, ef_bank) if (hh % 4) == 0 else ()

                            def tM(e, h=h, hh=hh, base=base, amval=amval,
                                   snval=snval, vt=vt, cs=cs, spb=spb,
                                   efsnap=efsnap, rr=rr, q4=q4):
                                e.wait_ge(s_am.h, amval)
                                e.wait_ge(s_sn.h, snval)
                                wait_snap(e, efsnap)
                                out = pEF[:, base:base + 65]
                                e.matmul(out=out, lhsT=qfm[:, h * 128:(h + 1) * 128],
                                         rhs=spb[:, h * 65:(h + 1) * 65],
                                         start=(hh % 4 == 0), stop=False,
                                         skip_group_check=True)
                                e.matmul(out=out,
                                         lhsT=amask[:, (rr * 4 + q4) * 128:(rr * 4 + q4 + 1) * 128],
                                         rhs=vt[:, cs * 1040 + h * 65: cs * 1040 + (h + 1) * 65],
                                         start=False, stop=True,
                                         skip_group_check=True).then_inc(s_au.h, 1)
                            add("tensor", tM)
                            s_au.inc(1)

                    auval = s_au.n
                    atb = attok[c % 2]
                    natt = len(attn_tp_vals)
                    at_guard = attn_tp_vals[natt - 2] if natt >= 2 else 0
                    wv1 = s_ve.inc(1)
                    wv2 = s_ve.inc(1)

                    def vE(e, half=half, auval=auval, atb=atb, at_guard=at_guard,
                           wv1=wv1, wv2=wv2):
                        e.wait_ge(s_au.h, auval)
                        if half == 0 and at_guard > 0:
                            e.wait_ge(s_tp.h, at_guard)
                        for g in range(2):
                            src = pEF[:, g * 512:g * 512 + 260].rearrange(
                                "p (hh d) -> p hh d", hh=4)[:, :, 64:65].squeeze(2)
                            ins = e.tensor_scalar(
                                out=dgath[:, half * 8 + g * 4:half * 8 + g * 4 + 4],
                                in0=src, scalar1=EPS, scalar2=None, op0=ALU.add)
                        ins.then_inc(s_ve.h, 1)
                        e.wait_ge(s_ve.h, wv1)
                        e.reciprocal(out=rcol[:, half * 8:half * 8 + 8],
                                     in_=dgath[:, half * 8:half * 8 + 8]
                                     ).then_inc(s_ve.h, 1)
                        e.wait_ge(s_ve.h, wv2)
                        for g in range(2):
                            num = pEF[:, g * 512:g * 512 + 260].rearrange(
                                "p (hh d) -> p hh d", hh=4)[:, :, 0:64]
                            rc = rcol[:, half * 8 + g * 4:half * 8 + g * 4 + 4]
                            rc3 = rc.unsqueeze(2).broadcast_to([128, 4, 64])
                            dst = atb[:, (half * 8 + g * 4) * 64:(half * 8 + g * 4 + 4) * 64]
                            dst3 = dst.rearrange("p (hh d) -> p hh d", hh=4)
                            ins = e.tensor_tensor(out=dst3, in0=num, in1=rc3,
                                                  op=ALU.mult)
                        ins.then_inc(s_ep.h, 1)
                    add("vector", vE)
                    s_ep.inc(1)
                    rec_read(efreads, 0, s_ep)
                    rec_read(efreads, 1, s_ep)
                epval = s_ep.n

                if prev is not None:
                    emit_tail_f(prev)
                ptc = dict(c=c, bk=bk, PO=PO, atb=attok[c % 2], ep=epval,
                           xbuf=xbuf)

            tile_uk_done[("p2", tt)] = s_um.n
            tile_m6_done[tt] = s_au.n

        emit_tail_tT(ptc, 0)
        emit_tail_vT(ptc, 0)
        emit_tail_tT(ptc, 1)
        emit_tail_vT(ptc, 1)
        emit_tail_wo(ptc, 0)
        emit_tail_wo(ptc, 1)
        emit_tail_b(ptc)
        emit_tail_h3(ptc)
        emit_tail_f(ptc)

        totals = (s_ou[0].n, s_ou[1].n)

        def fin(e, totals=totals):
            e.wait_ge(s_ou[0].h, totals[0])
            e.wait_ge(s_ou[1].h, totals[1])
        add("sync", fin)

        finish()

    return nc


_NC_CACHE = []


def kernel(L0_attn_x, L0_attn_Wq, L0_attn_bq, L0_attn_Wk, L0_attn_bk,
           L0_attn_Wv, L0_attn_bv, L0_attn_Wo, L0_attn_bo, L0_attn_proj,
           L0_ffn_weight):
    x = np.asarray(L0_attn_x, dtype=np.float32)
    wq = np.asarray(L0_attn_Wq, dtype=np.float32)
    wk = np.asarray(L0_attn_Wk, dtype=np.float32)
    wv = np.asarray(L0_attn_Wv, dtype=np.float32)
    wo = np.asarray(L0_attn_Wo, dtype=np.float32)
    wf = np.asarray(L0_ffn_weight, dtype=np.float32)
    bq = np.asarray(L0_attn_bq, dtype=np.float32)
    bk = np.asarray(L0_attn_bk, dtype=np.float32)
    bv = np.asarray(L0_attn_bv, dtype=np.float32)
    bo = np.asarray(L0_attn_bo, dtype=np.float32)
    proj = np.asarray(L0_attn_proj, dtype=np.float32)

    cA = np.ascontiguousarray
    wqT = cA(wq.T)
    wkT = cA(wk.T)
    wvT = cA(wv.T)
    woT = cA(wo.T)
    wfT = cA(wf.T)
    bqi = cA(bq.reshape(8, 128).T)
    bki = cA(bk.reshape(8, 128).T)
    bvi = bv.reshape(1, D)
    boi = bo.reshape(1, D)
    pT2 = np.zeros((128, 256), dtype=np.float32)
    pT2[0:64, 0:128] = proj.T
    pT2[64:128, 128:256] = proj.T
    id128i = np.eye(128, dtype=np.float32).astype(bf)
    id2i = np.eye(2, dtype=np.float32)
    triuTi = np.triu(np.ones((128, 128), dtype=np.float32))
    ob2 = np.zeros((128, 2), dtype=np.float32)
    ob2[:64, 0] = 0.5
    ob2[64:, 1] = 0.5
    oblki = ob2.astype(bf)
    orowi = np.ones((1, 128), dtype=np.float32).astype(bf)
    idSi = np.eye(128, dtype=np.float32)

    if not _NC_CACHE:
        _NC_CACHE.append(build_nc())
    nc = _NC_CACHE[0]

    in_maps = []
    for core in range(8):
        b, s = core // 4, core % 4
        segwv = np.zeros((128, 4), dtype=np.float32)
        segwv[:, :s] = 1.0
        in_maps.append(dict(
            xseg=cA(x[b, s * SEG:(s + 1) * SEG, :]),
            wqT=wqT, wkT=wkT, wvT=wvT, woT=woT, wfT=wfT,
            bqi=bqi, bki=bki, bvi=bvi, boi=boi, projT2=pT2,
            id128i=id128i, id2i=id2i, triuTi=triuTi, oblki=oblki,
            orowi=orowi, segwi=segwv, idSi=idSi,
        ))
    res = run_bass_kernel_spmd(nc, in_maps, core_ids=list(range(8)))
    out = np.empty((B, T, D), dtype=np.float32)
    for core in range(8):
        b, s = core // 4, core % 4
        out[b, s * SEG:(s + 1) * SEG, :] = res.results[core]["yseg"]
    return out


# version 23
# speedup vs baseline: 1.0588x; 1.0588x over previous
import math
from contextlib import ExitStack

import numpy as np
import ml_dtypes

import concourse.bass as bass
import concourse.mybir as mybir
from concourse.bass_utils import run_bass_kernel_spmd

F32 = mybir.dt.float32
BF16 = mybir.dt.bfloat16
AF = mybir.ActivationFunctionType
ALU = mybir.AluOpType
AX = mybir.AxisListType
bf = ml_dtypes.bfloat16

B, T, D = 2, 16384, 1024
SEG = 4096
NCH = 32
NTT = 8
H, DK, M = 16, 64, 128
CP = 2.8284271247461903
LSM = 0.5 * math.log(128.0)
EPS = 1e-6
LNEPS = 1e-5
R1024 = 1.0 / 1024.0

ENGS = ("sync", "gpsimd", "vector", "scalar", "tensor")

_STAGE = 3
_P1SUB = 9
_NTTRUN = NTT


class Sem:
    __slots__ = ("h", "n")

    def __init__(self, h):
        self.h = h
        self.n = 0

    def inc(self, k=1):
        self.n += k
        return self.n


def build_nc():
    nc = bass.Bass("TRN2", target_bir_lowering=False, debug=False)

    xseg = nc.dram_tensor("xseg", [SEG, D], F32, kind="ExternalInput").ap()
    wqT = nc.dram_tensor("wqT", [D, D], F32, kind="ExternalInput").ap()
    wkT = nc.dram_tensor("wkT", [D, D], F32, kind="ExternalInput").ap()
    wvT = nc.dram_tensor("wvT", [D, D], F32, kind="ExternalInput").ap()
    woT = nc.dram_tensor("woT", [D, D], F32, kind="ExternalInput").ap()
    wfT = nc.dram_tensor("wfT", [D, D], F32, kind="ExternalInput").ap()
    bqi = nc.dram_tensor("bqi", [128, 8], F32, kind="ExternalInput").ap()
    bki = nc.dram_tensor("bki", [128, 8], F32, kind="ExternalInput").ap()
    bvi = nc.dram_tensor("bvi", [1, D], F32, kind="ExternalInput").ap()
    boi = nc.dram_tensor("boi", [1, D], F32, kind="ExternalInput").ap()
    projT2 = nc.dram_tensor("projT2", [128, 256], F32, kind="ExternalInput").ap()
    id128i = nc.dram_tensor("id128i", [128, 128], BF16, kind="ExternalInput").ap()
    id2i = nc.dram_tensor("id2i", [2, 2], F32, kind="ExternalInput").ap()
    triuTi = nc.dram_tensor("triuTi", [128, 128], F32, kind="ExternalInput").ap()
    oblki = nc.dram_tensor("oblki", [128, 2], BF16, kind="ExternalInput").ap()
    orowi = nc.dram_tensor("orowi", [1, 128], BF16, kind="ExternalInput").ap()
    segwi = nc.dram_tensor("segwi", [128, 4], F32, kind="ExternalInput").ap()
    idSi = nc.dram_tensor("idSi", [128, 128], F32, kind="ExternalInput").ap()

    yseg = nc.dram_tensor("yseg", [SEG, D], F32, kind="ExternalOutput").ap()

    xnk_sp = nc.dram_tensor("xnk_sp", [128, NTT * 4096], BF16, kind="Internal").ap()
    vau_sp = nc.dram_tensor("vau_sp", [128, NCH * 1040], BF16, kind="Internal").ap()
    cc_in = nc.dram_tensor("cc_in", [128, 1040], F32, kind="Internal").ap()
    cc_out = nc.dram_tensor("cc_out", [512, 1040], F32, kind="Internal").ap()

    Q = {k: [] for k in ENGS}

    def add(eng, fn):
        Q[eng].append(fn)

    ctx = ExitStack()
    with ctx:
        sbt = lambda name, shape, dt: ctx.enter_context(nc.sbuf_tensor(name, shape, dt))
        pst = lambda name, shape, dt: ctx.enter_context(nc.psum_tensor(name, shape, dt))
        mksem = lambda name: Sem(ctx.enter_context(nc.semaphore(name)))

        wsA = sbt("wsA", [128, 8 * D], BF16)
        wsB = sbt("wsB", [128, 8 * D], BF16)
        wsC = sbt("wsC", [128, 8 * D], BF16)
        projb = sbt("projb", [128, 256], BF16)
        idb = sbt("idb", [128, 128], BF16)
        idS = sbt("idS", [128, 128], F32)
        id2 = sbt("id2", [2, 2], F32)
        triuT = sbt("triuT", [128, 128], F32)
        oblk = sbt("oblk", [128, 2], BF16)
        orow = sbt("orow", [1, 128], BF16)
        segw = sbt("segw", [128, 4], F32)
        bqs = sbt("bqs", [128, 8], F32)
        bks = sbt("bks", [128, 8], F32)
        bqraw = sbt("bqraw", [128, 8], F32)
        bkraw = sbt("bkraw", [128, 8], F32)
        bvb = sbt("bvb", [1, D], BF16)
        bob = sbt("bob", [1, D], BF16)
        lsmn = sbt("lsmn", [128, 1], F32)
        epsc = sbt("epsc", [128, 1], F32)

        xb = [sbt(f"xb{i}", [128, D], F32) for i in range(5)]
        htok = [sbt(f"htok{i}", [128, D], BF16) for i in range(2)]
        hfm = sbt("hfm", [128, 4096], BF16)
        xnq_t = sbt("xnq_t", [128, 4096], BF16)
        xnk_t = [sbt(f"xnk_t{i}", [128, 4096], BF16) for i in range(2)]
        vau_t = [sbt(f"vau_t{i}", [128, 4160], BF16) for i in range(2)]
        xnsqw = sbt("xnsqw", [128, 4096], BF16)
        sqrow = sbt("sqrow", [2, 512], F32)
        sqk_c = sbt("sqk_c", [128, NCH * 16], F32)
        sqq_c = sbt("sqq_c", [128, NCH * 16], F32)
        ln1_s = sbt("ln1_s", [128, NCH], F32)
        ln1_b = sbt("ln1_b", [128, NCH], F32)
        alf = sbt("alf", [128, 32], F32)
        alfb = sbt("alfb", [128, 32], BF16)
        etok = sbt("etok", [128, 2048], BF16)
        eqtok = sbt("eqtok", [128, 2048], BF16)
        kfm = sbt("kfm", [128, 2048], BF16)
        qfm = sbt("qfm", [128, 2048], BF16)
        amask = sbt("amask", [128, 1024], BF16)
        sprev = [sbt(f"sprev{i}", [128, 1040], BF16) for i in range(2)]
        sfin = sbt("sfin", [128, 1040], F32)
        sinit = sfin
        gbuf = sbt("gbuf", [128, 1040], F32)
        sqdump = gbuf
        dgath = sbt("dgath", [128, 16], F32)
        rcol = sbt("rcol", [128, 16], F32)
        attok = [sbt(f"attok{i}", [128, D], BF16) for i in range(2)]
        atfm = sbt("atfm", [128, D], BF16)
        yf = [sbt(f"yf{i}", [128, D], F32) for i in range(2)]
        lnsc = [sbt(f"lnsc{i}", [128, D], F32) for i in range(2)]
        x1b = [sbt(f"x1b{i}", [128, D], F32) for i in range(2)]
        h3tok = [sbt(f"h3tok{i}", [128, D], BF16) for i in range(2)]
        h3fm = sbt("h3fm", [128, D], BF16)
        stat = sbt("stat", [128, 64], F32)
        bstat = sbt("bstat", [128, 56], F32)

        S0 = pst("S0", [128, 455], F32)
        S1 = pst("S1", [128, 455], F32)
        S2 = pst("S2", [128, 130], F32)
        pAB = pst("pAB", [128, 1024], F32)
        pEF = pst("pEF", [128, 1024], F32)
        pBf = pst("pBf", [128, 1024], BF16)

        def s_slice(h):
            if h < 7:
                return S0[:, h * 65:(h + 1) * 65]
            if h < 14:
                return S1[:, (h - 7) * 65:(h - 6) * 65]
            return S2[:, (h - 14) * 65:(h - 13) * 65]

        s_w = mksem("s_w")
        s_wb = mksem("s_wb")
        s_wp = mksem("s_wp")
        s_x = [mksem(f"s_x{i}") for i in range(5)]
        s_ln = mksem("s_ln")
        s_tp = mksem("s_tp")
        f_Bf = mksem("f_Bf")
        s_pj = mksem("s_pj")
        s_um = mksem("s_um")
        s_sq = mksem("s_sq")
        s_sqe = mksem("s_sqe")
        s_xk = mksem("s_xk")
        s_va = mksem("s_va")
        s_sp = [mksem(f"s_sp{i}") for i in range(2)]
        s_rb = [mksem(f"s_rb{i}") for i in range(2)]
        s_ue = mksem("s_ue")
        s_ph = mksem("s_ph")
        s_kv = mksem("s_kv")
        s_sn = mksem("s_sn")
        s_cc = mksem("s_cc")
        s_ccd = mksem("s_ccd")
        s_pe = mksem("s_pe")
        s_am = mksem("s_am")
        s_au = mksem("s_au")
        s_ep = mksem("s_ep")
        s_ve = mksem("s_ve")
        s_ac = mksem("s_ac")
        s_st = mksem("s_st")
        s_ou = [mksem(f"s_ou{i}") for i in range(2)]

        abreads = [dict(), dict()]
        prev_sq_base = [-1]
        efreads = [dict(), dict()]
        nBf = [0]
        bfhalf = [0, 0]
        htrans_vals = []
        attn_tp_vals = []
        x1vals = []
        prev_alb = {"q": 0, "k": 0}
        p1_ln_box = [0]

        def rec_read(dics, b, sem):
            dics[b][sem] = sem.n

        def snap_reads(dics, *banks):
            out = {}
            for b in banks:
                for sem, v in dics[b].items():
                    out[sem] = max(out.get(sem, 0), v)
            return tuple(out.items())

        def wait_snap(e, snapshot):
            for sem, v in snapshot:
                if v > 0:
                    e.wait_ge(sem.h, v)

        def g_loads(e):
            for dst, src in ((idb, id128i), (id2, id2i), (triuT, triuTi),
                             (oblk, oblki), (orow, orowi), (segw, segwi),
                             (bqraw, bqi), (bkraw, bki), (idS, idSi)):
                e.dma_start(out=dst[:], in_=src).then_inc(s_w.h, 16)
            e.dma_start(out=bvb[:], in_=bvi).then_inc(s_w.h, 16)
            e.dma_start(out=bob[:], in_=boi).then_inc(s_w.h, 16)
            e.dma_start(out=projb[:], in_=projT2).then_inc(s_w.h, 16)
            for dst, src in ((wsA, wkT), (wsB, wvT), (wsC, wfT)):
                e.dma_start(out=dst[:], in_=src.rearrange("(j p) o -> p j o", p=128)
                            ).then_inc(s_w.h, 16)
        add("gpsimd", g_loads)
        s_w.inc(16 * 15)
        wload = s_w.n

        def v_prep(e):
            e.wait_ge(s_w.h, wload)
            e.tensor_scalar(out=bqs[:], in0=bqraw[:], scalar1=1.0 / CP,
                            scalar2=None, op0=ALU.mult)
            e.tensor_scalar(out=bks[:], in0=bkraw[:], scalar1=1.0 / CP,
                            scalar2=None, op0=ALU.mult)
            e.memset(lsmn[:], -LSM)
            e.memset(epsc[:], LNEPS)
            e.memset(vau_t[0][:, :], 1.0)
            e.memset(vau_t[1][:, :], 1.0).then_inc(s_wp.h, 1)
        add("vector", v_prep)
        s_wp.inc(1)

        def a_prep(e):
            e.wait_ge(s_w.h, wload)
            e.activation(out=wsC[:], in_=wsC[:], func=AF.Sign).then_inc(s_wp.h, 1)
        add("scalar", a_prep)
        s_wp.inc(1)
        prep_done = s_wp.n

        xload_n = [0, 0]

        def emit_xload(c, phase):
            nb = 4 if phase == 1 else 5
            xbuf = xb[c % nb]
            sx = s_x[c % nb]
            idx = xload_n[phase - 1]
            xload_n[phase - 1] += 1
            val = sx.inc(16)
            waits = []
            if phase == 1:
                if idx >= nb:
                    waits.append((s_ln, idx - nb + 1))
            else:
                if idx < nb:
                    waits.append((s_ln, p1_ln_box[0]))
                else:
                    waits.append((s_ve, x1vals[idx - nb]))

            def f(e, c=c, xbuf=xbuf, waits=tuple(waits), sx=sx):
                for smh, vv in waits:
                    if vv > 0:
                        e.wait_ge(smh.h, vv)
                e.dma_start(out=xbuf[:], in_=xseg[c * 128:(c + 1) * 128, :]
                            ).then_inc(sx.h, 16)
            add("sync", f)
            return xbuf, (sx, val)

        def stat_cols(lnid, c):
            base = lnid * 16 + (c % 2) * 8
            return [stat[:, base + i:base + i + 1] for i in range(7)]

        def emit_stats_math(lnid, c, sum_ap, sumsq_ap, deps, guard,
                            cols=None):
            csum, csumsq, cm, cbarg, cstd, cs_, cneg = (
                cols if cols is not None else stat_cols(lnid, c))
            w0 = s_st.inc(1)
            w1 = s_st.inc(1)
            w2 = s_st.inc(1)
            s_st.inc(1)
            adone = s_st.n

            def a(e, deps=deps, guard=guard, w0=w0, w1=w1, w2=w2):
                for smh, vv in deps:
                    if vv > 0:
                        e.wait_ge(smh.h, vv)
                if guard[1] > 0:
                    e.wait_ge(guard[0].h, guard[1])
                e.activation(out=cm, in_=sum_ap, func=AF.Identity,
                             scale=-R1024).then_inc(s_st.h, 1)
                e.wait_ge(s_st.h, w0)
                e.activation(out=cbarg, in_=cm, func=AF.Square
                             ).then_inc(s_st.h, 1)
                e.wait_ge(s_st.h, w1)
                e.activation(out=cbarg, in_=cbarg, func=AF.Identity,
                             scale=-1.0, bias=epsc[:, 0:1]).then_inc(s_st.h, 1)
                e.wait_ge(s_st.h, w2)
                e.activation(out=cstd, in_=sumsq_ap, func=AF.Sqrt,
                             bias=cbarg, scale=R1024).then_inc(s_st.h, 1)
            add("scalar", a)

            wv = s_ve.inc(1)
            s_ve.inc(1)
            vdone = s_ve.n

            def v2(e, adone=adone, wv=wv):
                e.wait_ge(s_st.h, adone)
                e.reciprocal(out=cs_, in_=cstd).then_inc(s_ve.h, 1)
                e.wait_ge(s_ve.h, wv)
                e.scalar_tensor_tensor(out=cneg, in0=cm, scalar=1.0, in1=cs_,
                                       op0=ALU.mult, op1=ALU.mult
                                       ).then_inc(s_ve.h, 1)
            add("vector", v2)
            return cs_, cneg, (s_ve, vdone)

        def emit_front1_batch(tt):
            par = tt % 2
            base = par * 28
            bc = [bstat[:, base + g * 4: base + (g + 1) * 4] for g in range(7)]
            war = 4 * (tt - 1)
            xs = []
            for cs in range(4):
                c = tt * 4 + cs
                xbuf, xval = emit_xload(c, 1)
                xs.append((xbuf, xval))
                scol = bstat[:, base + cs: base + cs + 1]
                qcol = bstat[:, base + 4 + cs: base + 4 + cs + 1]

                def v(e, xbuf=xbuf, xval=xval, scol=scol, war=war):
                    e.wait_ge(xval[0].h, xval[1])
                    if war > 0:
                        e.wait_ge(s_ln.h, war)
                    e.reduce_sum(out=scol, in_=xbuf[:], axis=AX.X
                                 ).then_inc(s_ve.h, 1)
                add("vector", v)
                s_ve.inc(1)

                aprev = s_ac.n

                def a(e, xbuf=xbuf, xval=xval, qcol=qcol, war=war,
                      aprev=aprev):
                    e.wait_ge(xval[0].h, xval[1])
                    if war > 0:
                        e.wait_ge(s_ln.h, war)
                    if aprev > 0:
                        e.wait_ge(s_ac.h, aprev)
                    e.activation(out=sqdump[:, 0:1024], in_=xbuf[:],
                                 func=AF.Square, accum_out=qcol
                                 ).then_inc(s_ac.h, 1)
                add("scalar", a)
                s_ac.inc(1)
            rval, aval = s_ve.n, s_ac.n
            csum4, csumsq4, cm4, cbarg4, cstd4, cs4, cneg4 = bc
            w0 = s_ve.inc(1)
            w1 = s_ve.inc(1)
            s_ve.inc(1)
            vdone = s_ve.n

            def vS(e, rval=rval, aval=aval, war=war, w0=w0, w1=w1):
                e.wait_ge(s_ve.h, rval)
                e.wait_ge(s_ac.h, aval)
                e.tensor_scalar(out=cm4, in0=csum4, scalar1=-R1024,
                                scalar2=None, op0=ALU.mult).then_inc(s_ve.h, 1)
                e.wait_ge(s_ve.h, w0)
                e.tensor_tensor(out=cbarg4, in0=cm4, in1=cm4, op=ALU.mult
                                ).then_inc(s_ve.h, 1)
                e.wait_ge(s_ve.h, w1)
                e.scalar_tensor_tensor(out=cbarg4, in0=csumsq4, scalar=R1024,
                                       in1=cbarg4, op0=ALU.mult,
                                       op1=ALU.subtract).then_inc(s_ve.h, 1)
            add("vector", vS)

            def aS(e, vdone=vdone):
                e.wait_ge(s_ve.h, vdone)
                e.activation(out=cstd4, in_=cbarg4, func=AF.Sqrt,
                             bias=epsc[:, 0:1]).then_inc(s_st.h, 1)
            add("scalar", aS)
            s_st.inc(1)
            adone = s_st.n

            w2 = s_ve.inc(1)
            s_ve.inc(1)
            svv = s_ve.n

            def v2S(e, adone=adone, w2=w2):
                e.wait_ge(s_st.h, adone)
                e.reciprocal(out=cs4, in_=cstd4).then_inc(s_ve.h, 1)
                e.wait_ge(s_ve.h, w2)
                e.scalar_tensor_tensor(out=cneg4, in0=cm4, scalar=1.0,
                                       in1=cs4, op0=ALU.mult, op1=ALU.mult
                                       ).then_inc(s_ve.h, 1)
            add("vector", v2S)
            scol4, negcol4, sv = cs4, cneg4, (s_ve, svv)

            def v3(e, tt=tt, sv=sv, scol4=scol4, negcol4=negcol4):
                e.wait_ge(sv[0].h, sv[1])
                e.tensor_copy(out=ln1_s[:, tt * 4:(tt + 1) * 4], in_=scol4)
                e.tensor_copy(out=ln1_b[:, tt * 4:(tt + 1) * 4], in_=negcol4
                              ).then_inc(s_ve.h, 1)
            add("vector", v3)
            s_ve.inc(1)
            dep = s_ve.n

            out = []
            tp_base = s_tp.n
            for cs in range(4):
                c = tt * 4 + cs
                xbuf, xval = xs[cs]
                hb = htok[c % 2]
                ht_need = tp_base + cs - 1 if cs >= 2 else (
                    htrans_vals[-1] if htrans_vals else 0)

                def a2(e, c=c, xbuf=xbuf, xval=xval, dep=dep, hb=hb,
                       ht_need=ht_need):
                    e.wait_ge(xval[0].h, xval[1])
                    e.wait_ge(s_ve.h, dep)
                    if ht_need > 0:
                        e.wait_ge(s_tp.h, ht_need)
                    e.activation(out=hb[:], in_=xbuf[:], func=AF.Identity,
                                 bias=ln1_b[:, c:c + 1], scale=ln1_s[:, c:c + 1]
                                 ).then_inc(s_ln.h, 1)
                add("scalar", a2)
                s_ln.inc(1)
                out.append((hb, s_ln.n, xbuf))
            return out

        def emit_front(c, phase):
            xbuf, xval = emit_xload(c, phase)
            if phase == 1:
                cols = stat_cols(0, c)
                csum, csumsq = cols[0], cols[1]

                def v(e, xbuf=xbuf, xval=xval, csum=csum):
                    e.wait_ge(xval[0].h, xval[1])
                    e.reduce_sum(out=csum, in_=xbuf[:], axis=AX.X).then_inc(s_ve.h, 1)
                s_ve.inc(1)
                rval = s_ve.n
                add("vector", v)

                def a(e, xbuf=xbuf, xval=xval, csumsq=csumsq):
                    e.wait_ge(xval[0].h, xval[1])
                    e.activation(out=sqdump[:, 0:1024], in_=xbuf[:], func=AF.Square,
                                 accum_out=csumsq).then_inc(s_ac.h, 1)
                s_ac.inc(1)
                aval = s_ac.n
                add("scalar", a)
                guard = (s_ln, s_ln.n - 1)
                scol, negcol, sv = emit_stats_math(
                    0, c, csum, csumsq, [(s_ve, rval), (s_ac, aval)], guard)

                def v3(e, c=c, sv=sv, scol=scol, negcol=negcol):
                    e.wait_ge(sv[0].h, sv[1])
                    e.tensor_copy(out=ln1_s[:, c:c + 1], in_=scol)
                    e.tensor_copy(out=ln1_b[:, c:c + 1], in_=negcol).then_inc(s_ve.h, 1)
                s_ve.inc(1)
                add("vector", v3)
                dep_sem, dep = s_ve, s_ve.n
            else:
                dep_sem, dep = s_wp, 0

            hb = htok[c % 2]
            nfront = len(htrans_vals)
            ht_need = htrans_vals[nfront - 2] if nfront >= 2 else 0

            def a2(e, c=c, xbuf=xbuf, xval=xval, dep=dep, hb=hb,
                   dep_sem=dep_sem, ht_need=ht_need):
                e.wait_ge(xval[0].h, xval[1])
                if dep > 0:
                    e.wait_ge(dep_sem.h, dep)
                if ht_need > 0:
                    e.wait_ge(s_tp.h, ht_need)
                e.activation(out=hb[:], in_=xbuf[:], func=AF.Identity,
                             bias=ln1_b[:, c:c + 1], scale=ln1_s[:, c:c + 1]
                             ).then_inc(s_ln.h, 1)
            add("scalar", a2)
            s_ln.inc(1)
            return hb, s_ln.n, xbuf

        def emit_htrans(c, hb, lnval, extra_pe_wait):
            cs = c % 4
            bf_need = nBf[0]

            def t(e, hb=hb, lnval=lnval, bf_need=bf_need):
                e.wait_ge(s_ln.h, lnval)
                if bf_need > 0:
                    e.wait_ge(f_Bf.h, bf_need)
                for j in range(8):
                    ins = e.transpose(out=pBf[:, j * 128:(j + 1) * 128],
                                      in_=hb[:, j * 128:(j + 1) * 128],
                                      identity=idb[:])
                ins.then_inc(s_tp.h, 1)
            add("tensor", t)
            s_tp.inc(1)
            htrans_vals.append(s_tp.n)
            tval = s_tp.n

            def v(e, cs=cs, tval=tval, extra=extra_pe_wait):
                e.wait_ge(s_tp.h, tval)
                if extra[1] > 0:
                    e.wait_ge(extra[0].h, extra[1])
                src = pBf[:, 0:1024].rearrange("p (j t) -> p j t", j=8)
                dst = hfm[:, 0:4096].rearrange("p (j t) -> p j t", j=8)[
                    :, :, cs * 128:(cs + 1) * 128]
                e.tensor_copy(out=dst, in_=src).then_inc(f_Bf.h, 1)
            add("vector", v)
            nBf[0] += 1
            f_Bf.inc(1)
            bfhalf[0] = bfhalf[1] = nBf[0]
            return f_Bf.n

        def emit_tile_proj(tt, which, hf_val, phase, sp_guard=None, w_wait=0,
                           fillers=None):
            if sp_guard is None:
                sp_guard = (s_sp[0], 0)
            wbuf = wsA
            bcol = bks if which == "k" else bqs
            xnt = xnk_t[tt % 2] if which == "k" else xnq_t
            sqc = sqk_c if which == "k" else sqq_c

            sq_jobs = []
            sq_base = s_sq.n
            prev_base = prev_sq_base[0]
            prev_sq_base[0] = sq_base

            for j in range(8):
                bank = j % 2
                po = pAB[:, bank * 512:(bank + 1) * 512]
                absnap = snap_reads(abreads, bank)

                def t(e, j=j, po=po, hf_val=hf_val, wbuf=wbuf, absnap=absnap,
                      w_wait=w_wait):
                    e.wait_ge(f_Bf.h, hf_val)
                    e.wait_ge(s_wp.h, prep_done)
                    if w_wait > 0:
                        e.wait_ge(s_w.h, w_wait)
                    wait_snap(e, absnap)
                    for kk in range(8):
                        ins = e.matmul(
                            out=po,
                            lhsT=wbuf[:, kk * D + j * 128: kk * D + (j + 1) * 128],
                            rhs=hfm[:, kk * 512:(kk + 1) * 512],
                            start=(kk == 0), stop=(kk == 7),
                            skip_group_check=True)
                    ins.then_inc(s_pj.h, 1)
                add("tensor", t)
                s_pj.inc(1)
                mmval = s_pj.n

                sq_guard = prev_base + 2 * j + 1 if prev_base >= 0 else 0

                def a(e, j=j, po=po, mmval=mmval, xnt=xnt, bcol=bcol,
                      sp_guard=sp_guard, sq_guard=sq_guard):
                    e.wait_ge(s_pj.h, mmval)
                    if sp_guard[1] > 0:
                        e.wait_ge(sp_guard[0].h, sp_guard[1])
                    if sq_guard > 0:
                        e.wait_ge(s_sq.h, sq_guard)
                    e.activation(out=xnt[:, j * 512:(j + 1) * 512], in_=po,
                                 func=AF.Identity, bias=bcol[:, j:j + 1],
                                 scale=1.0 / CP).then_inc(s_xk.h, 1)
                    e.activation(out=xnsqw[:, j * 512:(j + 1) * 512], in_=po,
                                 func=AF.Square, bias=bcol[:, j:j + 1],
                                 scale=1.0 / CP).then_inc(s_ac.h, 1)
                add("scalar", a)
                s_xk.inc(1)
                s_ac.inc(1)
                sqin = s_ac.n
                rec_read(abreads, bank, s_ac)
                rows_val = sq_base + 2 * j + 1
                sq_jobs.append((j, sqin, rows_val))

            for (j, sqin, rows_val) in sq_jobs:
                efsnap0 = snap_reads(efreads, 0)

                def t2(e, j=j, sqin=sqin, efsnap0=efsnap0):
                    e.wait_ge(s_ac.h, sqin)
                    wait_snap(e, efsnap0)
                    e.matmul(out=pEF[0:2, 0:512], lhsT=oblk[:],
                             rhs=xnsqw[:, j * 512:(j + 1) * 512],
                             start=True, stop=True, skip_group_check=True
                             ).then_inc(s_sq.h, 1)
                add("tensor", t2)
                s_sq.inc(1)

                def v(e, rows_val=rows_val):
                    e.wait_ge(s_sq.h, rows_val)
                    e.tensor_copy(out=sqrow[:], in_=pEF[0:2, 0:512]
                                  ).then_inc(s_sqe.h, 1)
                add("vector", v)
                s_sqe.inc(1)
                rev = s_sqe.n
                rec_read(efreads, 0, s_sqe)
                if fillers:
                    fillers.pop(0)()

                efsnap1 = snap_reads(efreads, 1)

                def t3(e, rev=rev, efsnap1=efsnap1):
                    e.wait_ge(s_sqe.h, rev)
                    wait_snap(e, efsnap1)
                    for cs_ in range(4):
                        ins = e.transpose(
                            out=pEF[:, 512 + cs_ * 2:512 + cs_ * 2 + 2],
                            in_=sqrow[:, cs_ * 128:(cs_ + 1) * 128],
                            identity=id2[:])
                    ins.then_inc(s_sq.h, 1)
                add("tensor", t3)
                s_sq.inc(1)
                trval = s_sq.n

                def v2(e, tt=tt, j=j, trval=trval, sqc=sqc):
                    e.wait_ge(s_sq.h, trval)
                    src = pEF[:, 512:520].rearrange("p (c two) -> p c two", two=2)
                    dst = sqc[:, 0:NCH * 16].rearrange("p (cc hh) -> p cc hh", hh=16)[
                        :, tt * 4:(tt + 1) * 4, 2 * j:2 * j + 2]
                    e.tensor_copy(out=dst, in_=src).then_inc(s_sqe.h, 1)
                add("vector", v2)
                s_sqe.inc(1)
                rec_read(efreads, 1, s_sqe)
            return s_xk.n, s_sqe.n

        def emit_vproj_half(c, half, hf_val, sp_guard):
            cs = c % 4
            vt = vau_t[(c // 4) % 2]
            po = pAB[:, half * 512:(half + 1) * 512]
            absnap = snap_reads(abreads, half)

            def t(e, cs=cs, half=half, po=po, hf_val=hf_val, absnap=absnap):
                e.wait_ge(f_Bf.h, hf_val)
                wait_snap(e, absnap)
                for kk in range(8):
                    e.matmul(
                        out=po,
                        lhsT=hfm[:, kk * 512 + cs * 128: kk * 512 + (cs + 1) * 128],
                        rhs=wsB[:, kk * D + half * 512: kk * D + (half + 1) * 512],
                        start=(kk == 0), stop=False, skip_group_check=True)
                e.matmul(out=po, lhsT=orow[:, 0:128],
                         rhs=bvb[:, half * 512:(half + 1) * 512],
                         start=False, stop=True, skip_group_check=True
                         ).then_inc(s_pj.h, 1)
            add("tensor", t)
            s_pj.inc(1)
            mmval = s_pj.n

            def v(e, c=c, cs=cs, half=half, po=po, mmval=mmval, vt=vt,
                  sp_guard=sp_guard):
                e.wait_ge(s_pj.h, mmval)
                if sp_guard[1] > 0:
                    e.wait_ge(sp_guard[0].h, sp_guard[1])
                src = po.rearrange("p (hh d) -> p hh d", hh=8)
                dst = vt[:, cs * 1040:(cs + 1) * 1040].rearrange(
                    "p (hh dd) -> p hh dd", hh=16)[:, half * 8:(half + 1) * 8, 0:64]
                e.tensor_copy(out=dst, in_=src).then_inc(s_va.h, 1)
            add("vector", v)
            s_va.inc(1)
            rec_read(abreads, half, s_va)
            return s_va.n

        def emit_phi(c, which, deps, etbuf=None, fills=None, end=None):
            xnt = xnk_t[(c // 4) % 2] if which == "k" else xnq_t
            sqc = sqk_c if which == "k" else sqq_c
            et = etbuf if etbuf is not None else (etok if which == "k" else eqtok)
            acol = 16 if which == "k" else 0
            cs = c % 4
            bk = c % 2
            po = pAB[:, bk * 512:(bk + 1) * 512]

            alb_guard = prev_alb[which]

            def a0(e, c=c, sqc=sqc, deps=deps, acol=acol, alb_guard=alb_guard):
                for smh, vv in deps:
                    if vv > 0:
                        e.wait_ge(smh.h, vv)
                if alb_guard > 0:
                    e.wait_ge(s_ve.h, alb_guard)
                src = sqc[:, 0:NCH * 16].rearrange("p (cc hh) -> p cc hh", hh=16)[:, c, :]
                e.wait_ge(s_wp.h, prep_done)
                e.activation(out=alf[:, acol:acol + 16], in_=src,
                             func=AF.Exp, bias=lsmn[:, 0:1], scale=-1.0
                             ).then_inc(s_ac.h, 1)
            add("scalar", a0)
            s_ac.inc(1)
            alval = s_ac.n

            def v0(e, alval=alval, acol=acol):
                e.wait_ge(s_ac.h, alval)
                e.tensor_copy(out=alfb[:, acol:acol + 16],
                              in_=alf[:, acol:acol + 16]).then_inc(s_ve.h, 1)
            add("vector", v0)
            s_ve.inc(1)
            albval = s_ve.n
            prev_alb[which] = albval

            ue_hist = []
            for rnd in range(4):
                X = bk if rnd % 2 == 0 else 1 - bk
                poX = pAB[:, X * 512:(X + 1) * 512]
                if rnd < 2:
                    waitset = snap_reads(abreads, X)
                    rdeps = tuple(deps)
                else:
                    waitset = ((s_ue, ue_hist[rnd - 2]),)
                    rdeps = ()

                def t(e, cs=cs, rnd=rnd, xnt=xnt, rdeps=rdeps,
                      waitset=waitset, poX=poX):
                    for smh, vv in rdeps:
                        if vv > 0:
                            e.wait_ge(smh.h, vv)
                    wait_snap(e, waitset)
                    for pp in range(2):
                        pair = rnd * 2 + pp
                        lhs = xnt[:, pair * 512 + cs * 128: pair * 512 + (cs + 1) * 128]
                        ins = e.matmul(out=poX[:, pp * 256:(pp + 1) * 256],
                                       lhsT=lhs, rhs=projb[:],
                                       start=True, stop=True,
                                       skip_group_check=True)
                    ins.then_inc(s_um.h, 1)
                add("tensor", t)
                s_um.inc(1)
                umval = s_um.n

                def a(e, rnd=rnd, umval=umval, et=et, poX=poX):
                    e.wait_ge(s_um.h, umval)
                    e.activation(out=et[:, rnd * 512:(rnd + 1) * 512],
                                 in_=poX, func=AF.Exp).then_inc(s_ue.h, 1)
                add("scalar", a)
                s_ue.inc(1)
                rec_read(abreads, X, s_ue)
                ue_hist.append(s_ue.n)
                if fills is not None and rnd in fills:
                    fills[rnd]()
            ue_last = ue_hist[-1]
            if end is not None:
                end()

            wv0 = s_ve.inc(1)

            def v(e, ueval=ue_last, albval=albval, et=et, acol=acol, wv0=wv0):
                e.wait_ge(s_ue.h, ueval)
                e.wait_ge(s_ve.h, albval)
                a3 = alfb[:, acol:acol + 16].unsqueeze(2).broadcast_to([128, 16, 128])
                e3 = et[:, 0:2048].rearrange("p (hh m) -> p hh m", hh=16)
                e.tensor_tensor(out=e3, in0=e3, in1=a3, op=ALU.mult
                                ).then_inc(s_ve.h, 1)
                e.wait_ge(s_ve.h, wv0)
                e.tensor_scalar(out=et[:, 0:2048], in0=et[:, 0:2048],
                                scalar1=EPS, scalar2=None, op0=ALU.add
                                ).then_inc(s_ph.h, 1)
            add("vector", v)
            s_ph.inc(1)
            return s_ph.n

        def emit_kv(c, phval, vwaits, phase, snval=None, first=False, etbuf=None):
            vt = vau_t[(c // 4) % 2]
            cs = c % 4
            et = etbuf if etbuf is not None else etok
            kvorder = [0, 7, 14, 1, 8, 15, 2, 9, 3, 10, 4, 11, 5, 12, 6, 13]

            def t(e, cs=cs, phval=phval, vwaits=tuple(vwaits), vt=vt, first=first,
                  snval=snval, et=et):
                e.wait_ge(s_ph.h, phval)
                for smh, vv in vwaits:
                    if vv > 0:
                        e.wait_ge(smh.h, vv)
                if snval is not None:
                    e.wait_ge(s_sn.h, snval)
                for h in kvorder:
                    st = first and (h in (0, 7, 14))
                    ins = e.matmul(out=s_slice(h),
                                   lhsT=et[:, h * 128:(h + 1) * 128],
                                   rhs=vt[:, cs * 1040 + h * 65: cs * 1040 + (h + 1) * 65],
                                   start=st, stop=False, skip_group_check=True)
                ins.then_inc(s_kv.h, 1)
            add("tensor", t)
            s_kv.inc(1)
            return s_kv.n

        def finish():
            block = ctx.enter_context(nc.Block())

            @block.sync
            def _(e):
                for f in Q["sync"]:
                    f(e)

            @block.gpsimd
            def _(e):
                for f in Q["gpsimd"]:
                    f(e)

            @block.vector
            def _(e):
                for f in Q["vector"]:
                    f(e)

            @block.scalar
            def _(e):
                for f in Q["scalar"]:
                    f(e)

            @block.tensor
            def _(e):
                for f in Q["tensor"]:
                    f(e)

        tile_uk_done = {}
        tile_m6_done = {}
        last_proj_pe = [(s_pj, 0)]
        dbg_last = [(s_ln, 0)]

        for tt in range(_NTTRUN):
            hf_tile = None
            fr = emit_front1_batch(tt)
            for cs in range(4):
                c = tt * 4 + cs
                hb, lnval, _ = fr[cs]
                if _P1SUB >= 2:
                    hf_tile = emit_htrans(c, hb, lnval,
                                          last_proj_pe[0] if cs == 0 else (s_pj, 0))
            dbg_last[0] = (s_ln, s_ln.n) if _P1SUB < 2 else (f_Bf, f_Bf.n)
            if _P1SUB < 3:
                continue
            prev_sp = ((s_sp[tt % 2], (tt // 2) * 32)
                       if (tt >= 2 and _P1SUB >= 7) else (s_sp[0], 0))
            vav = {}

            def mk_fill(c, half, hf_tile=hf_tile, prev_sp=prev_sp, vav=vav):
                def f():
                    vav[(c % 4, half)] = emit_vproj_half(c, half, hf_tile,
                                                         prev_sp)
                return f

            fillers = ([mk_fill(tt * 4 + cs, half)
                        for cs in range(4) for half in range(2)]
                       if _P1SUB >= 4 else None)
            xkval, sqev = emit_tile_proj(tt, "k", hf_tile, 1, sp_guard=prev_sp,
                                         fillers=fillers)
            dbg_last[0] = (s_sqe, s_sqe.n)
            if _P1SUB < 4:
                continue
            vavals = [vav[(cs, 1)] for cs in range(4)]
            last_proj_pe[0] = (s_pj, s_pj.n)
            dbg_last[0] = (s_va, s_va.n)
            if _P1SUB < 5:
                continue

            kv_pend = None
            for cs in range(4):
                c = tt * 4 + cs
                etb = etok if c % 2 == 0 else eqtok
                phv = emit_phi(c, "k", [(s_xk, xkval), (s_sqe, sqev)], etbuf=etb)
                if _P1SUB >= 6:
                    if kv_pend is not None:
                        emit_kv(*kv_pend)
                    kv_pend = (c, phv, [(s_va, vavals[cs])], 1, None, c == 0, etb)
            if kv_pend is not None:
                emit_kv(*kv_pend)
            tile_uk_done[tt] = s_um.n
            dbg_last[0] = (s_ph, s_ph.n) if _P1SUB < 6 else (s_kv, s_kv.n)
            if _P1SUB < 7:
                continue

            kvv = s_kv.n
            xnt = xnk_t[tt % 2]
            vt = vau_t[tt % 2]
            ssp = s_sp[tt % 2]

            def sp(e, tt=tt, kvv=kvv, xnt=xnt, vt=vt, ssp=ssp):
                e.wait_ge(s_kv.h, kvv)
                e.dma_start(out=xnk_sp[:, tt * 4096:(tt + 1) * 4096],
                            in_=xnt[:]).then_inc(ssp.h, 16)
                e.dma_start(out=vau_sp[:, tt * 4160:(tt + 1) * 4160],
                            in_=vt[:]).then_inc(ssp.h, 16)
            add("gpsimd", sp)
            ssp.inc(32)

        p1_kv = s_kv.n
        p1_sp = (s_sp[0].n, s_sp[1].n)
        p1_ln_box[0] = s_ln.n

        if _P1SUB < 8:
            def stD(e, w=dbg_last[0][1], sh=dbg_last[0][0]):
                if w > 0:
                    e.wait_ge(sh.h, w)
                e.dma_start(out=yseg[0:128, 0:1024], in_=sfin[:, 0:1024]
                            ).then_inc(s_ou[0].h, 16)
                e.wait_ge(s_ou[0].h, 16)
            add("sync", stD)
            finish()
            return nc

        def v_exc(e, p1_kv=p1_kv):
            e.wait_ge(s_kv.h, p1_kv)
            e.tensor_copy(out=sfin[:, 0:455], in_=S0[:])
            e.tensor_copy(out=sfin[:, 455:910], in_=S1[:])
            e.tensor_copy(out=sfin[:, 910:1040], in_=S2[:]).then_inc(s_ve.h, 1)
        add("vector", v_exc)
        s_ve.inc(1)
        evacval = s_ve.n

        if _STAGE == 1:
            def st1(e, v=evacval):
                e.wait_ge(s_ve.h, v)
                e.dma_start(out=yseg[0:128, 0:1024], in_=sfin[:, 0:1024]
                            ).then_inc(s_ou[0].h, 16)
                e.dma_start(out=yseg[128:256, 0:16], in_=sfin[:, 1024:1040]
                            ).then_inc(s_ou[0].h, 16)
                e.wait_ge(s_ou[0].h, 32)
            add("sync", st1)
            finish()
            return nc

        p1_pj = s_pj.n

        def g_wreA(e, p1_pj=p1_pj):
            e.wait_ge(s_pj.h, p1_pj)
            e.dma_start(out=wsA[:], in_=wqT.rearrange("(j p) o -> p j o", p=128)
                        ).then_inc(s_w.h, 16)
        add("gpsimd", g_wreA)
        s_w.inc(16)
        wreA_val = s_w.n

        def g_wreB(e, p1_pj=p1_pj):
            e.dma_start(out=wsB[:], in_=woT.rearrange("(j p) o -> p j o", p=128)
                        ).then_inc(s_wb.h, 16)
        add("gpsimd", g_wreB)
        s_wb.inc(16)
        wreB_val = s_wb.n

        def emit_p2_head(tt):
            prev_uk = tile_uk_done.get(("p2", tt - 2), 0)
            prev_m6 = tile_m6_done.get(tt - 2, 0)
            srb = s_rb[tt % 2]
            rbval = (srb, srb.inc(32))

            def frb(e, tt=tt, prev_uk=prev_uk, prev_m6=prev_m6, srb=srb):
                e.wait_ge(s_sp[0].h, p1_sp[0])
                e.wait_ge(s_sp[1].h, p1_sp[1])
                if prev_uk > 0:
                    e.wait_ge(s_um.h, prev_uk)
                if prev_m6 > 0:
                    e.wait_ge(s_au.h, prev_m6)
                e.dma_start(out=xnk_t[tt % 2][:],
                            in_=xnk_sp[:, tt * 4096:(tt + 1) * 4096]
                            ).then_inc(srb.h, 16)
                e.dma_start(out=vau_t[tt % 2][:],
                            in_=vau_sp[:, tt * 4160:(tt + 1) * 4160]
                            ).then_inc(srb.h, 16)
            add("sync", frb)

            hf_tile = None
            xbufs_tile = []
            for cs in range(4):
                c = tt * 4 + cs
                hb, lnval, xbuf_c = emit_front(c, 2)
                xbufs_tile.append(xbuf_c)
                hf_tile = emit_htrans(c, hb, lnval,
                                      last_proj_pe[0] if cs == 0 else (s_pj, 0))
            xqval, sqqev = emit_tile_proj(tt, "q", hf_tile, 2, w_wait=wreA_val)
            last_proj_pe[0] = (s_pj, s_pj.n)
            return rbval, xqval, sqqev, xbufs_tile

        p2_head0 = emit_p2_head(0)

        def g_exc(e, evacval=evacval, p1_sp=p1_sp):
            e.wait_ge(s_ve.h, evacval)
            e.wait_ge(s_sp[0].h, p1_sp[0])
            e.wait_ge(s_sp[1].h, p1_sp[1])
            e.dma_start(out=cc_in[:, :], in_=sfin[:]).then_inc(s_cc.h, 16)
            e.wait_ge(s_cc.h, 16)
            e.collective_compute(
                "AllGather", ALU.bypass,
                replica_groups=[[0, 1, 2, 3], [4, 5, 6, 7]],
                ins=[cc_in[:, :].opt()], outs=[cc_out[:, :].opt()],
            ).then_inc(s_ccd.h, 1)
        add("gpsimd", g_exc)
        s_cc.inc(16)
        s_ccd.inc(1)
        ccval = s_ccd.n

        def v_ms(e, ccval=ccval):
            e.wait_ge(s_ccd.h, ccval)
            e.memset(sinit[:], 0.0).then_inc(s_ve.h, 1)
        add("vector", v_ms)
        s_ve.inc(1)
        gw = s_ve.n

        for r in range(4):
            def g_g(e, r=r, gw=gw):
                e.wait_ge(s_ve.h, gw)
                e.dma_start(out=gbuf[:],
                            in_=cc_out[r * 128:(r + 1) * 128, :]).then_inc(s_cc.h, 16)
            add("gpsimd", g_g)
            s_cc.inc(16)
            gval = s_cc.n

            def v_g(e, r=r, gval=gval):
                e.wait_ge(s_cc.h, gval)
                e.scalar_tensor_tensor(out=sinit[:], in0=gbuf[:],
                                       scalar=segw[:, r:r + 1], in1=sinit[:],
                                       op0=ALU.mult, op1=ALU.add
                                       ).then_inc(s_ve.h, 1)
            add("vector", v_g)
            s_ve.inc(1)
            gw = s_ve.n
        initval = s_ve.n

        if _STAGE == 2:
            def st2(e, v=initval):
                e.wait_ge(s_ve.h, v)
                e.dma_start(out=yseg[0:128, 0:1024], in_=sinit[:, 0:1024]
                            ).then_inc(s_ou[0].h, 16)
                e.dma_start(out=yseg[128:256, 0:16], in_=sinit[:, 1024:1040]
                            ).then_inc(s_ou[0].h, 16)
                e.wait_ge(s_ou[0].h, 32)
            add("sync", st2)
            finish()
            return nc

        def t_init(e, initval=initval):
            e.wait_ge(s_ve.h, initval)
            e.matmul(out=S0[:], lhsT=idS[:], rhs=sinit[:, 0:455], start=True,
                     stop=False, skip_group_check=True)
            e.matmul(out=S1[:], lhsT=idS[:], rhs=sinit[:, 455:910], start=True,
                     stop=False, skip_group_check=True)
            e.matmul(out=S2[:], lhsT=idS[:], rhs=sinit[:, 910:1040], start=True,
                     stop=False, skip_group_check=True).then_inc(s_pe.h, 1)
        add("tensor", t_init)
        s_pe.inc(1)
        sinj = s_pe.n

        def emit_tail_tT(tc):
            atb, epval = tc["atb"], tc["ep"]
            bf_need = nBf[0]

            def tT(e, atb=atb, epval=epval, bf_need=bf_need):
                e.wait_ge(s_ep.h, epval)
                if bf_need > 0:
                    e.wait_ge(f_Bf.h, bf_need)
                for j in range(8):
                    ins = e.transpose(out=pBf[:, j * 128:(j + 1) * 128],
                                      in_=atb[:, j * 128:(j + 1) * 128],
                                      identity=idb[:])
                ins.then_inc(s_tp.h, 1)
            add("tensor", tT)
            s_tp.inc(1)
            attn_tp_vals.append(s_tp.n)
            tc["tT"] = s_tp.n

        def emit_tail_vT(tc):
            tval = tc["tT"]
            wo_pe_prev = s_pj.n

            def vT(e, tval=tval, wo_prev=wo_pe_prev):
                e.wait_ge(s_tp.h, tval)
                if wo_prev > 0:
                    e.wait_ge(s_pj.h, wo_prev)
                e.tensor_copy(out=atfm[:], in_=pBf[:, 0:1024]).then_inc(f_Bf.h, 1)
            add("vector", vT)
            nBf[0] += 1
            f_Bf.inc(1)
            bfhalf[0] = bfhalf[1] = nBf[0]
            tc["atval"] = f_Bf.n
            tc["cols2"] = stat_cols(1, tc["c"])
            tc["y2"] = yf[tc["c"] % 2]
            tc["ln2_evac"] = []

        def emit_tail_wo(tc, half):
            bk, PO, atval = tc["bk"], tc["PO"], tc["atval"]
            cols2, y2 = tc["cols2"], tc["y2"]
            absnap = snap_reads(abreads, bk)

            def tW(e, half=half, atval=atval, absnap=absnap,
                   w_wait=wreB_val, PO=PO):
                e.wait_ge(s_wb.h, w_wait)
                e.wait_ge(f_Bf.h, atval)
                wait_snap(e, absnap)
                for kk in range(8):
                    e.matmul(
                        out=PO, lhsT=atfm[:, kk * 128:(kk + 1) * 128],
                        rhs=wsB[:, kk * D + half * 512: kk * D + (half + 1) * 512],
                        start=(kk == 0), stop=False,
                        skip_group_check=True)
                e.matmul(out=PO, lhsT=orow[:, 0:128],
                         rhs=bob[:, half * 512:(half + 1) * 512],
                         start=False, stop=True, skip_group_check=True
                         ).then_inc(s_pj.h, 1)
            add("tensor", tW)
            s_pj.inc(1)
            woval = s_pj.n

            def aW(e, half=half, woval=woval, y2=y2, cols2=cols2, PO=PO):
                e.wait_ge(s_pj.h, woval)
                e.activation(out=y2[:, half * 512:(half + 1) * 512],
                             in_=PO, func=AF.Identity,
                             accum_out=cols2[0] if half == 0 else cols2[3]
                             ).then_inc(s_ac.h, 1)
            add("scalar", aW)
            s_ac.inc(1)
            rec_read(abreads, bk, s_ac)
            tc["ln2_evac"].append(s_ac.n)

        def emit_tail_b(tc):
            c, xbuf = tc["c"], tc["xbuf"]
            cols2, y2, ln2_evac = tc["cols2"], tc["y2"], tc["ln2_evac"]
            def aWs(e, ev=ln2_evac[1], cols2=cols2):
                e.wait_ge(s_ac.h, ev)
                e.activation(out=cols2[0], in_=cols2[0], func=AF.Identity,
                             bias=cols2[3]).then_inc(s_ac.h, 1)
            add("scalar", aWs)
            s_ac.inc(1)

            def aW2(e, ev=ln2_evac[1], y2=y2, cols2=cols2):
                e.wait_ge(s_ac.h, ev)
                e.activation(out=sqdump[:, 0:1024], in_=y2[:], func=AF.Square,
                             accum_out=cols2[1]).then_inc(s_ac.h, 1)
            add("scalar", aW2)
            s_ac.inc(1)
            sq2val = s_ac.n

            s2, neg2, st2 = emit_stats_math(
                1, c, cols2[0], cols2[1], [(s_ac, sq2val)], (s_ve, 0))

            lsc = lnsc[c % 2]

            def aW3(e, y2=y2, st2=st2, s2=s2, neg2=neg2, lsc=lsc):
                e.wait_ge(st2[0].h, st2[1])
                e.activation(out=lsc[:], in_=y2[:], func=AF.Identity,
                             bias=neg2, scale=s2).then_inc(s_ac.h, 1)
            add("scalar", aW3)
            s_ac.inc(1)
            ln2val = s_ac.n

            x1 = x1b[c % 2]
            cols3 = stat_cols(2, c)
            x1_guard = s_ou[c % 2].n

            def vR(e, lsc=lsc, ln2val=ln2val, x1=x1, xbuf=xbuf,
                   cols3=cols3, x1_guard=x1_guard, sou_g=s_ou[c % 2]):
                e.wait_ge(s_ac.h, ln2val)
                if x1_guard > 0:
                    e.wait_ge(sou_g.h, x1_guard)
                e.scalar_tensor_tensor(out=x1[:], in0=lsc[:], scalar=1.0,
                                       in1=xbuf[:], op0=ALU.mult, op1=ALU.add,
                                       accum_out=cols3[0]).then_inc(s_ve.h, 1)
            add("vector", vR)
            s_ve.inc(1)
            x1vals.append(s_ve.n)
            x1val = s_ve.n

            def aL3(e, x1=x1, x1val=x1val, cols3=cols3):
                e.wait_ge(s_ve.h, x1val)
                e.activation(out=sqdump[:, 0:1024], in_=x1[:], func=AF.Square,
                             accum_out=cols3[1]).then_inc(s_ac.h, 1)
            add("scalar", aL3)
            s_ac.inc(1)
            sq3val = s_ac.n

            s3, neg3, st3 = emit_stats_math(
                2, c, cols3[0], cols3[1], [(s_ve, x1val), (s_ac, sq3val)],
                (s_ve, 0))

            h3 = h3tok[c % 2]

            def aL3b(e, x1=x1, st3=st3, s3=s3, neg3=neg3, h3=h3):
                e.wait_ge(st3[0].h, st3[1])
                e.activation(out=h3[:], in_=x1[:], func=AF.Identity,
                             bias=neg3, scale=s3).then_inc(s_ac.h, 1)
            add("scalar", aL3b)
            s_ac.inc(1)
            tc["h3"], tc["h3val"], tc["lsc"], tc["x1"] = h3, s_ac.n, lsc, x1

        def emit_tail_h3(tc):
            h3, h3val = tc["h3"], tc["h3val"]
            bf_need = nBf[0]

            def tH3(e, h3=h3, h3val=h3val, bf_need=bf_need):
                e.wait_ge(s_ac.h, h3val)
                if bf_need > 0:
                    e.wait_ge(f_Bf.h, bf_need)
                for j in range(8):
                    ins = e.transpose(out=pBf[:, j * 128:(j + 1) * 128],
                                      in_=h3[:, j * 128:(j + 1) * 128],
                                      identity=idb[:])
                ins.then_inc(s_tp.h, 1)
            add("tensor", tH3)
            s_tp.inc(1)
            tval = s_tp.n
            ffn_prev = s_pj.n
            tc["tH3"] = (tval, ffn_prev)

        def emit_tail_f(tc):
            c, bk, PO = tc["c"], tc["bk"], tc["PO"]
            lsc, x1 = tc["lsc"], tc["x1"]
            tval, ffn_prev = tc["tH3"]

            def vH3(e, tval=tval, ffn_prev=ffn_prev):
                e.wait_ge(s_tp.h, tval)
                if ffn_prev > 0:
                    e.wait_ge(s_pj.h, ffn_prev)
                e.tensor_copy(out=h3fm[:], in_=pBf[:, 0:1024]).then_inc(f_Bf.h, 1)
            add("vector", vH3)
            nBf[0] += 1
            f_Bf.inc(1)
            bfhalf[0] = bfhalf[1] = nBf[0]
            h3fval = f_Bf.n
            cols4 = stat_cols(3, c)
            fb = yf[c % 2]
            ffn_evac = []
            for half in range(2):
                absnap = snap_reads(abreads, bk)

                def tF(e, half=half, h3fval=h3fval, absnap=absnap, PO=PO):
                    e.wait_ge(f_Bf.h, h3fval)
                    wait_snap(e, absnap)
                    for kk in range(8):
                        ins = e.matmul(
                            out=PO, lhsT=h3fm[:, kk * 128:(kk + 1) * 128],
                            rhs=wsC[:, kk * D + half * 512: kk * D + (half + 1) * 512],
                            start=(kk == 0), stop=(kk == 7),
                            skip_group_check=True)
                    ins.then_inc(s_pj.h, 1)
                add("tensor", tF)
                s_pj.inc(1)
                ffval = s_pj.n

                def aF(e, half=half, ffval=ffval, fb=fb, cols4=cols4, PO=PO):
                    e.wait_ge(s_pj.h, ffval)
                    e.activation(out=fb[:, half * 512:(half + 1) * 512],
                                 in_=PO, func=AF.Identity,
                                 accum_out=cols4[0] if half == 0 else cols4[3]
                                 ).then_inc(s_ac.h, 1)
                add("scalar", aF)
                s_ac.inc(1)
                rec_read(abreads, bk, s_ac)
                ffn_evac.append(s_ac.n)

            def aFs(e, ev=ffn_evac[1], cols4=cols4):
                e.wait_ge(s_ac.h, ev)
                e.activation(out=cols4[0], in_=cols4[0], func=AF.Identity,
                             bias=cols4[3]).then_inc(s_ac.h, 1)
            add("scalar", aFs)
            s_ac.inc(1)

            def aF2(e, ev=ffn_evac[1], fb=fb, cols4=cols4):
                e.wait_ge(s_ac.h, ev)
                e.activation(out=sqdump[:, 0:1024], in_=fb[:], func=AF.Square,
                             accum_out=cols4[1]).then_inc(s_ac.h, 1)
            add("scalar", aF2)
            s_ac.inc(1)
            sq4val = s_ac.n

            s4, neg4, st4 = emit_stats_math(
                3, c, cols4[0], cols4[1], [(s_ac, sq4val)], (s_ve, 0))

            def aL4(e, fb=fb, st4=st4, s4=s4, neg4=neg4, lsc=lsc):
                e.wait_ge(st4[0].h, st4[1])
                e.activation(out=lsc[:], in_=fb[:], func=AF.Identity,
                             bias=neg4, scale=s4).then_inc(s_ac.h, 1)
            add("scalar", aL4)
            s_ac.inc(1)
            ln4val = s_ac.n

            sou = s_ou[c % 2]

            def vO(e, lsc=lsc, ln4val=ln4val, x1=x1):
                e.wait_ge(s_ac.h, ln4val)
                e.tensor_add(out=x1[:], in0=x1[:], in1=lsc[:]).then_inc(s_ve.h, 1)
            add("vector", vO)
            s_ve.inc(1)
            oval = s_ve.n

            def fO(e, c=c, x1=x1, oval=oval, sou=sou):
                e.wait_ge(s_ve.h, oval)
                e.dma_start(out=yseg[c * 128:(c + 1) * 128, :], in_=x1[:]
                            ).then_inc(sou.h, 16)
            add("gpsimd", fO)
            sou.inc(16)

        ptc = None
        for tt in range(_NTTRUN):
            if tt == 0:
                rbval, xqval, sqqev, xbufs_tile = p2_head0
            else:
                rbval, xqval, sqqev, xbufs_tile = emit_p2_head(tt)

            for cs in range(4):
                c = tt * 4 + cs
                xbuf = xbufs_tile[cs]
                bk = c % 2
                PO = pAB[:, bk * 512:(bk + 1) * 512]
                prev = ptc

                spb = sprev[c % 2]
                kv_prev = s_kv.n
                m5_prev = s_au.n - 16

                def vsn(e, spb=spb, kv_prev=kv_prev, m5_prev=m5_prev, sinj=sinj):
                    e.wait_ge(s_pe.h, sinj)
                    e.wait_ge(s_kv.h, kv_prev)
                    if m5_prev > 0:
                        e.wait_ge(s_au.h, m5_prev)
                    e.tensor_copy(out=spb[:, 0:455], in_=S0[:])
                    e.tensor_copy(out=spb[:, 455:910], in_=S1[:])
                    e.tensor_copy(out=spb[:, 910:1040], in_=S2[:]).then_inc(s_sn.h, 1)
                add("vector", vsn)
                s_sn.inc(1)
                snval = s_sn.n

                def emit_fm(et, fmdst, phv, rnd):
                    bf_need = nBf[0]

                    def t(e, et=et, rnd=rnd, phv=phv, bf_need=bf_need):
                        e.wait_ge(s_ph.h, phv)
                        if bf_need > 0:
                            e.wait_ge(f_Bf.h, bf_need)
                        for hh in range(8):
                            h = rnd * 8 + hh
                            ins = e.transpose(
                                out=pBf[:, hh * 128:(hh + 1) * 128],
                                in_=et[:, h * 128:(h + 1) * 128],
                                identity=idb[:])
                        ins.then_inc(s_tp.h, 1)
                    add("tensor", t)
                    s_tp.inc(1)
                    tval = s_tp.n

                    def v(e, fmdst=fmdst, rnd=rnd, tval=tval):
                        e.wait_ge(s_tp.h, tval)
                        e.tensor_copy(out=fmdst[:, rnd * 1024:(rnd + 1) * 1024],
                                      in_=pBf[:, 0:1024]).then_inc(f_Bf.h, 1)
                    add("vector", v)
                    nBf[0] += 1
                    f_Bf.inc(1)
                    bfhalf[0] = bfhalf[1] = nBf[0]

                phq = emit_phi(c, "q", [(s_xk, xqval), (s_sqe, sqqev)])

                def fill1(p=prev):
                    if p is not None:
                        emit_tail_tT(p)

                def fill2(p=prev):
                    if p is not None:
                        emit_tail_vT(p)
                    emit_fm(eqtok, qfm, phq, 0)

                def fill3():
                    emit_fm(eqtok, qfm, phq, 1)

                phk = emit_phi(c, "k", [rbval],
                               fills={1: fill1, 2: fill2, 3: fill3})


                if prev is not None:
                    emit_tail_wo(prev, 0)
                emit_kv(c, phk, [rbval], 2, snval=snval)
                emit_fm(etok, kfm, phk, 0)
                if prev is not None:
                    emit_tail_wo(prev, 1)
                emit_fm(etok, kfm, phk, 1)
                fm_done = f_Bf.n

                vt = vau_t[tt % 2]
                for half in range(2):
                    if half == 1 and prev is not None:
                        emit_tail_b(prev)
                    vavals_r = []
                    for rr in range(2):
                        X = bk if (half * 2 + rr) % 2 == 0 else 1 - bk
                        POX = pAB[:, X * 512:(X + 1) * 512]
                        absnap = snap_reads(abreads, X)

                        def tA(e, half=half, rr=rr, fm_done=fm_done,
                               absnap=absnap, POX=POX):
                            e.wait_ge(f_Bf.h, fm_done)
                            wait_snap(e, absnap)
                            for q4 in range(4):
                                h = half * 8 + rr * 4 + q4
                                ins = e.matmul(out=POX[:, q4 * 128:(q4 + 1) * 128],
                                               lhsT=kfm[:, h * 128:(h + 1) * 128],
                                               rhs=qfm[:, h * 128:(h + 1) * 128],
                                               start=True, stop=True,
                                               skip_group_check=True)
                            ins.then_inc(s_pe.h, 1)
                        add("tensor", tA)
                        s_pe.inc(1)
                        aval = s_pe.n

                        am_guard = s_au.n - 4

                        def vA(e, rr=rr, aval=aval, am_guard=am_guard, POX=POX):
                            e.wait_ge(s_pe.h, aval)
                            if am_guard > 0:
                                e.wait_ge(s_au.h, am_guard)
                            src = POX.rearrange("p (q4 cc) -> p q4 cc", q4=4)
                            m3 = triuT[:].unsqueeze(1).broadcast_to([128, 4, 128])
                            dst = amask[:, rr * 512:(rr + 1) * 512].rearrange(
                                "p (q4 cc) -> p q4 cc", q4=4)
                            e.tensor_tensor(out=dst, in0=src, in1=m3,
                                            op=ALU.mult).then_inc(s_am.h, 1)
                        add("vector", vA)
                        s_am.inc(1)
                        rec_read(abreads, X, s_am)
                        vavals_r.append(s_am.n)

                    if half == 1 and prev is not None:
                        emit_tail_h3(prev)
                    for rr in range(2):
                        amval = vavals_r[rr]
                        for q4 in range(4):
                            hh = rr * 4 + q4
                            h = half * 8 + hh
                            ef_bank = hh // 4
                            base = ef_bank * 512 + (hh % 4) * 65
                            efsnap = snap_reads(efreads, ef_bank) if (hh % 4) == 0 else ()

                            def tM(e, h=h, hh=hh, base=base, amval=amval,
                                   snval=snval, vt=vt, cs=cs, spb=spb,
                                   efsnap=efsnap, rr=rr, q4=q4):
                                if q4 == 0:
                                    e.wait_ge(s_am.h, amval)
                                    e.wait_ge(s_sn.h, snval)
                                wait_snap(e, efsnap)
                                out = pEF[:, base:base + 65]
                                e.matmul(out=out, lhsT=qfm[:, h * 128:(h + 1) * 128],
                                         rhs=spb[:, h * 65:(h + 1) * 65],
                                         start=(hh % 4 == 0), stop=False,
                                         skip_group_check=True)
                                e.matmul(out=out,
                                         lhsT=amask[:, (rr * 4 + q4) * 128:(rr * 4 + q4 + 1) * 128],
                                         rhs=vt[:, cs * 1040 + h * 65: cs * 1040 + (h + 1) * 65],
                                         start=False, stop=True,
                                         skip_group_check=True).then_inc(s_au.h, 1)
                            add("tensor", tM)
                            s_au.inc(1)

                    auval = s_au.n
                    atb = attok[c % 2]
                    natt = len(attn_tp_vals)
                    at_guard = attn_tp_vals[natt - 2] if natt >= 2 else 0
                    wv1 = s_ve.inc(1)
                    wv2 = s_ve.inc(1)

                    def vE(e, half=half, auval=auval, atb=atb, at_guard=at_guard,
                           wv1=wv1, wv2=wv2):
                        e.wait_ge(s_au.h, auval)
                        if half == 0 and at_guard > 0:
                            e.wait_ge(s_tp.h, at_guard)
                        for g in range(2):
                            src = pEF[:, g * 512:g * 512 + 260].rearrange(
                                "p (hh d) -> p hh d", hh=4)[:, :, 64:65].squeeze(2)
                            ins = e.tensor_scalar(
                                out=dgath[:, half * 8 + g * 4:half * 8 + g * 4 + 4],
                                in0=src, scalar1=EPS, scalar2=None, op0=ALU.add)
                        ins.then_inc(s_ve.h, 1)
                        e.wait_ge(s_ve.h, wv1)
                        e.reciprocal(out=rcol[:, half * 8:half * 8 + 8],
                                     in_=dgath[:, half * 8:half * 8 + 8]
                                     ).then_inc(s_ve.h, 1)
                        e.wait_ge(s_ve.h, wv2)
                        for g in range(2):
                            num = pEF[:, g * 512:g * 512 + 260].rearrange(
                                "p (hh d) -> p hh d", hh=4)[:, :, 0:64]
                            rc = rcol[:, half * 8 + g * 4:half * 8 + g * 4 + 4]
                            rc3 = rc.unsqueeze(2).broadcast_to([128, 4, 64])
                            dst = atb[:, (half * 8 + g * 4) * 64:(half * 8 + g * 4 + 4) * 64]
                            dst3 = dst.rearrange("p (hh d) -> p hh d", hh=4)
                            ins = e.tensor_tensor(out=dst3, in0=num, in1=rc3,
                                                  op=ALU.mult)
                        ins.then_inc(s_ep.h, 1)
                    add("vector", vE)
                    s_ep.inc(1)
                    rec_read(efreads, 0, s_ep)
                    rec_read(efreads, 1, s_ep)
                epval = s_ep.n

                if prev is not None:
                    emit_tail_f(prev)
                ptc = dict(c=c, bk=bk, PO=PO, atb=attok[c % 2], ep=epval,
                           xbuf=xbuf)

            tile_uk_done[("p2", tt)] = s_um.n
            tile_m6_done[tt] = s_au.n

        emit_tail_tT(ptc)
        emit_tail_vT(ptc)
        emit_tail_wo(ptc, 0)
        emit_tail_wo(ptc, 1)
        emit_tail_b(ptc)
        emit_tail_h3(ptc)
        emit_tail_f(ptc)

        totals = (s_ou[0].n, s_ou[1].n)

        def fin(e, totals=totals):
            e.wait_ge(s_ou[0].h, totals[0])
            e.wait_ge(s_ou[1].h, totals[1])
        add("sync", fin)

        finish()

    return nc


_NC_CACHE = []


def kernel(L0_attn_x, L0_attn_Wq, L0_attn_bq, L0_attn_Wk, L0_attn_bk,
           L0_attn_Wv, L0_attn_bv, L0_attn_Wo, L0_attn_bo, L0_attn_proj,
           L0_ffn_weight):
    x = np.asarray(L0_attn_x, dtype=np.float32)
    wq = np.asarray(L0_attn_Wq, dtype=np.float32)
    wk = np.asarray(L0_attn_Wk, dtype=np.float32)
    wv = np.asarray(L0_attn_Wv, dtype=np.float32)
    wo = np.asarray(L0_attn_Wo, dtype=np.float32)
    wf = np.asarray(L0_ffn_weight, dtype=np.float32)
    bq = np.asarray(L0_attn_bq, dtype=np.float32)
    bk = np.asarray(L0_attn_bk, dtype=np.float32)
    bv = np.asarray(L0_attn_bv, dtype=np.float32)
    bo = np.asarray(L0_attn_bo, dtype=np.float32)
    proj = np.asarray(L0_attn_proj, dtype=np.float32)

    cA = np.ascontiguousarray
    wqT = cA(wq.T)
    wkT = cA(wk.T)
    wvT = cA(wv.T)
    woT = cA(wo.T)
    wfT = cA(wf.T)
    bqi = cA(bq.reshape(8, 128).T)
    bki = cA(bk.reshape(8, 128).T)
    bvi = bv.reshape(1, D)
    boi = bo.reshape(1, D)
    pT2 = np.zeros((128, 256), dtype=np.float32)
    pT2[0:64, 0:128] = proj.T
    pT2[64:128, 128:256] = proj.T
    id128i = np.eye(128, dtype=np.float32).astype(bf)
    id2i = np.eye(2, dtype=np.float32)
    triuTi = np.triu(np.ones((128, 128), dtype=np.float32))
    ob2 = np.zeros((128, 2), dtype=np.float32)
    ob2[:64, 0] = 0.5
    ob2[64:, 1] = 0.5
    oblki = ob2.astype(bf)
    orowi = np.ones((1, 128), dtype=np.float32).astype(bf)
    idSi = np.eye(128, dtype=np.float32)

    if not _NC_CACHE:
        _NC_CACHE.append(build_nc())
    nc = _NC_CACHE[0]

    in_maps = []
    for core in range(8):
        b, s = core // 4, core % 4
        segwv = np.zeros((128, 4), dtype=np.float32)
        segwv[:, :s] = 1.0
        in_maps.append(dict(
            xseg=cA(x[b, s * SEG:(s + 1) * SEG, :]),
            wqT=wqT, wkT=wkT, wvT=wvT, woT=woT, wfT=wfT,
            bqi=bqi, bki=bki, bvi=bvi, boi=boi, projT2=pT2,
            id128i=id128i, id2i=id2i, triuTi=triuTi, oblki=oblki,
            orowi=orowi, segwi=segwv, idSi=idSi,
        ))
    res = run_bass_kernel_spmd(nc, in_maps, core_ids=list(range(8)))
    out = np.empty((B, T, D), dtype=np.float32)
    for core in range(8):
        b, s = core // 4, core % 4
        out[b, s * SEG:(s + 1) * SEG, :] = res.results[core]["yseg"]
    return out


# version 24 (final)
# speedup vs baseline: 1.0923x; 1.0317x over previous
import math
from contextlib import ExitStack

import numpy as np
import ml_dtypes

import concourse.bass as bass
import concourse.mybir as mybir
from concourse.bass_utils import run_bass_kernel_spmd

F32 = mybir.dt.float32
BF16 = mybir.dt.bfloat16
AF = mybir.ActivationFunctionType
ALU = mybir.AluOpType
AX = mybir.AxisListType
bf = ml_dtypes.bfloat16

B, T, D = 2, 16384, 1024
SEG = 4096
NCH = 32
NTT = 8
H, DK, M = 16, 64, 128
CP = 2.8284271247461903
LSM = 0.5 * math.log(128.0)
EPS = 1e-6
LNEPS = 1e-5
R1024 = 1.0 / 1024.0

ENGS = ("sync", "gpsimd", "vector", "scalar", "tensor")

_STAGE = 3
_P1SUB = 9
_NTTRUN = NTT


class Sem:
    __slots__ = ("h", "n")

    def __init__(self, h):
        self.h = h
        self.n = 0

    def inc(self, k=1):
        self.n += k
        return self.n


def build_nc():
    nc = bass.Bass("TRN2", target_bir_lowering=False, debug=False)

    xseg = nc.dram_tensor("xseg", [SEG, D], F32, kind="ExternalInput").ap()
    wqT = nc.dram_tensor("wqT", [D, D], F32, kind="ExternalInput").ap()
    wkT = nc.dram_tensor("wkT", [D, D], F32, kind="ExternalInput").ap()
    wvT = nc.dram_tensor("wvT", [D, D], F32, kind="ExternalInput").ap()
    woT = nc.dram_tensor("woT", [D, D], F32, kind="ExternalInput").ap()
    wfT = nc.dram_tensor("wfT", [D, D], F32, kind="ExternalInput").ap()
    bqi = nc.dram_tensor("bqi", [128, 8], F32, kind="ExternalInput").ap()
    bki = nc.dram_tensor("bki", [128, 8], F32, kind="ExternalInput").ap()
    bvi = nc.dram_tensor("bvi", [1, D], F32, kind="ExternalInput").ap()
    boi = nc.dram_tensor("boi", [1, D], F32, kind="ExternalInput").ap()
    projT2 = nc.dram_tensor("projT2", [128, 256], F32, kind="ExternalInput").ap()
    id128i = nc.dram_tensor("id128i", [128, 128], BF16, kind="ExternalInput").ap()
    id2i = nc.dram_tensor("id2i", [2, 2], F32, kind="ExternalInput").ap()
    triuTi = nc.dram_tensor("triuTi", [128, 128], F32, kind="ExternalInput").ap()
    oblki = nc.dram_tensor("oblki", [128, 2], BF16, kind="ExternalInput").ap()
    orowi = nc.dram_tensor("orowi", [1, 128], BF16, kind="ExternalInput").ap()
    segwi = nc.dram_tensor("segwi", [128, 4], F32, kind="ExternalInput").ap()
    idSi = nc.dram_tensor("idSi", [128, 128], F32, kind="ExternalInput").ap()

    yseg = nc.dram_tensor("yseg", [SEG, D], F32, kind="ExternalOutput").ap()

    xnk_sp = nc.dram_tensor("xnk_sp", [128, NTT * 4096], BF16, kind="Internal").ap()
    vau_sp = nc.dram_tensor("vau_sp", [128, NCH * 1040], BF16, kind="Internal").ap()
    cc_in = nc.dram_tensor("cc_in", [128, 1040], F32, kind="Internal").ap()
    cc_out = nc.dram_tensor("cc_out", [512, 1040], F32, kind="Internal").ap()

    Q = {k: [] for k in ENGS}

    def add(eng, fn):
        Q[eng].append(fn)

    ctx = ExitStack()
    with ctx:
        sbt = lambda name, shape, dt: ctx.enter_context(nc.sbuf_tensor(name, shape, dt))
        pst = lambda name, shape, dt: ctx.enter_context(nc.psum_tensor(name, shape, dt))
        mksem = lambda name: Sem(ctx.enter_context(nc.semaphore(name)))

        wsA = sbt("wsA", [128, 8 * D], BF16)
        wsB = sbt("wsB", [128, 8 * D], BF16)
        wsC = sbt("wsC", [128, 8 * D], BF16)
        projb = sbt("projb", [128, 256], BF16)
        idb = sbt("idb", [128, 128], BF16)
        idS = sbt("idS", [128, 128], F32)
        id2 = sbt("id2", [2, 2], F32)
        triuT = sbt("triuT", [128, 128], F32)
        oblk = sbt("oblk", [128, 2], BF16)
        orow = sbt("orow", [1, 128], BF16)
        segw = sbt("segw", [128, 4], F32)
        bqs = sbt("bqs", [128, 8], F32)
        bks = sbt("bks", [128, 8], F32)
        bqraw = sbt("bqraw", [128, 8], F32)
        bkraw = sbt("bkraw", [128, 8], F32)
        bvb = sbt("bvb", [1, D], BF16)
        bob = sbt("bob", [1, D], BF16)
        lsmn = sbt("lsmn", [128, 1], F32)
        epsc = sbt("epsc", [128, 1], F32)

        xb = [sbt(f"xb{i}", [128, D], F32) for i in range(5)]
        htok = [sbt(f"htok{i}", [128, D], BF16) for i in range(2)]
        hfm = sbt("hfm", [128, 4096], BF16)
        xnq_t = sbt("xnq_t", [128, 4096], BF16)
        xnk_t = [sbt(f"xnk_t{i}", [128, 4096], BF16) for i in range(2)]
        vau_t = [sbt(f"vau_t{i}", [128, 4160], BF16) for i in range(2)]
        xnsqw = sbt("xnsqw", [128, 4096], BF16)
        sqrow = sbt("sqrow", [2, 512], F32)
        sqk_c = sbt("sqk_c", [128, NCH * 16], F32)
        sqq_c = sbt("sqq_c", [128, NCH * 16], F32)
        ln1_s = sbt("ln1_s", [128, NCH], F32)
        ln1_b = sbt("ln1_b", [128, NCH], F32)
        alf = sbt("alf", [128, 32], F32)
        alfb = sbt("alfb", [128, 32], BF16)
        etok = sbt("etok", [128, 2048], BF16)
        eqtok = sbt("eqtok", [128, 2048], BF16)
        kfm = sbt("kfm", [128, 2048], BF16)
        qfm = sbt("qfm", [128, 2048], BF16)
        amask = sbt("amask", [128, 1024], BF16)
        sprev = [sbt(f"sprev{i}", [128, 1040], BF16) for i in range(2)]
        sfin = sbt("sfin", [128, 1040], F32)
        sinit = sfin
        gbuf = sbt("gbuf", [128, 1040], F32)
        sqdump = gbuf
        dgath = sbt("dgath", [128, 16], F32)
        rcol = sbt("rcol", [128, 16], F32)
        attok = [sbt(f"attok{i}", [128, D], BF16) for i in range(2)]
        atfm = sbt("atfm", [128, D], BF16)
        yf = [sbt(f"yf{i}", [128, D], F32) for i in range(2)]
        lnsc = [sbt(f"lnsc{i}", [128, D], F32) for i in range(2)]
        x1b = [sbt(f"x1b{i}", [128, D], F32) for i in range(2)]
        h3tok = [sbt(f"h3tok{i}", [128, D], BF16) for i in range(2)]
        h3fm = sbt("h3fm", [128, D], BF16)
        stat = sbt("stat", [128, 64], F32)
        bstat = sbt("bstat", [128, 56], F32)

        S0 = pst("S0", [128, 455], F32)
        S1 = pst("S1", [128, 455], F32)
        S2 = pst("S2", [128, 130], F32)
        pAB = pst("pAB", [128, 1024], F32)
        pEF = pst("pEF", [128, 1024], F32)
        pBf = pst("pBf", [128, 1024], BF16)

        def s_slice(h):
            if h < 7:
                return S0[:, h * 65:(h + 1) * 65]
            if h < 14:
                return S1[:, (h - 7) * 65:(h - 6) * 65]
            return S2[:, (h - 14) * 65:(h - 13) * 65]

        s_w = mksem("s_w")
        s_wb = mksem("s_wb")
        s_wp = mksem("s_wp")
        s_x = [mksem(f"s_x{i}") for i in range(5)]
        s_ln = mksem("s_ln")
        s_tp = mksem("s_tp")
        f_Bf = mksem("f_Bf")
        s_pj = mksem("s_pj")
        s_um = mksem("s_um")
        s_sq = mksem("s_sq")
        s_sqe = mksem("s_sqe")
        s_xk = mksem("s_xk")
        s_va = mksem("s_va")
        s_sp = [mksem(f"s_sp{i}") for i in range(2)]
        s_rb = [mksem(f"s_rb{i}") for i in range(2)]
        s_ue = mksem("s_ue")
        s_ph = mksem("s_ph")
        s_kv = mksem("s_kv")
        s_sn = mksem("s_sn")
        s_cc = mksem("s_cc")
        s_ccd = mksem("s_ccd")
        s_pe = mksem("s_pe")
        s_am = mksem("s_am")
        s_au = mksem("s_au")
        s_ep = mksem("s_ep")
        s_ve = mksem("s_ve")
        s_ac = mksem("s_ac")
        s_st = mksem("s_st")
        s_ou = [mksem(f"s_ou{i}") for i in range(2)]

        abreads = [dict(), dict()]
        prev_sq_base = [-1]
        efreads = [dict(), dict()]
        nBf = [0]
        bfhalf = [0, 0]
        htrans_vals = []
        attn_tp_vals = []
        x1vals = []
        prev_alb = {"q": 0, "k": 0}
        p1_ln_box = [0]

        def rec_read(dics, b, sem):
            dics[b][sem] = sem.n

        def snap_reads(dics, *banks):
            out = {}
            for b in banks:
                for sem, v in dics[b].items():
                    out[sem] = max(out.get(sem, 0), v)
            return tuple(out.items())

        def wait_snap(e, snapshot):
            for sem, v in snapshot:
                if v > 0:
                    e.wait_ge(sem.h, v)

        def g_loads(e):
            for dst, src in ((idb, id128i), (id2, id2i), (triuT, triuTi),
                             (oblk, oblki), (orow, orowi), (segw, segwi),
                             (bqraw, bqi), (bkraw, bki), (idS, idSi)):
                e.dma_start(out=dst[:], in_=src).then_inc(s_w.h, 16)
            e.dma_start(out=bvb[:], in_=bvi).then_inc(s_w.h, 16)
            e.dma_start(out=bob[:], in_=boi).then_inc(s_w.h, 16)
            e.dma_start(out=projb[:], in_=projT2).then_inc(s_w.h, 16)
            for dst, src in ((wsA, wkT), (wsB, wvT), (wsC, wfT)):
                e.dma_start(out=dst[:], in_=src.rearrange("(j p) o -> p j o", p=128)
                            ).then_inc(s_w.h, 16)
        add("gpsimd", g_loads)
        s_w.inc(16 * 15)
        wload = s_w.n

        def v_prep(e):
            e.wait_ge(s_w.h, wload)
            e.tensor_scalar(out=bqs[:], in0=bqraw[:], scalar1=1.0 / CP,
                            scalar2=None, op0=ALU.mult)
            e.tensor_scalar(out=bks[:], in0=bkraw[:], scalar1=1.0 / CP,
                            scalar2=None, op0=ALU.mult)
            e.memset(lsmn[:], -LSM)
            e.memset(epsc[:], LNEPS)
            e.memset(vau_t[0][:, :], 1.0)
            e.memset(vau_t[1][:, :], 1.0).then_inc(s_wp.h, 1)
        add("vector", v_prep)
        s_wp.inc(1)

        def a_prep(e):
            e.wait_ge(s_w.h, wload)
            e.activation(out=wsC[:], in_=wsC[:], func=AF.Sign).then_inc(s_wp.h, 1)
        add("scalar", a_prep)
        s_wp.inc(1)
        prep_done = s_wp.n

        xload_n = [0, 0]

        def emit_xload(c, phase):
            nb = 4 if phase == 1 else 5
            xbuf = xb[c % nb]
            sx = s_x[c % nb]
            idx = xload_n[phase - 1]
            xload_n[phase - 1] += 1
            val = sx.inc(16)
            waits = []
            if phase == 1:
                if idx >= nb:
                    waits.append((s_ln, idx - nb + 1))
            else:
                if idx < nb:
                    waits.append((s_ln, p1_ln_box[0]))
                else:
                    waits.append((s_ve, x1vals[idx - nb]))

            def f(e, c=c, xbuf=xbuf, waits=tuple(waits), sx=sx):
                for smh, vv in waits:
                    if vv > 0:
                        e.wait_ge(smh.h, vv)
                e.dma_start(out=xbuf[:], in_=xseg[c * 128:(c + 1) * 128, :]
                            ).then_inc(sx.h, 16)
            add("sync", f)
            return xbuf, (sx, val)

        def stat_cols(lnid, c):
            base = lnid * 16 + (c % 2) * 8
            return [stat[:, base + i:base + i + 1] for i in range(7)]

        def emit_stats_math(lnid, c, sum_ap, sumsq_ap, deps, guard,
                            cols=None):
            csum, csumsq, cm, cbarg, cstd, cs_, cneg = (
                cols if cols is not None else stat_cols(lnid, c))
            w0 = s_st.inc(1)
            w1 = s_st.inc(1)
            w2 = s_st.inc(1)
            s_st.inc(1)
            adone = s_st.n

            def a(e, deps=deps, guard=guard, w0=w0, w1=w1, w2=w2):
                for smh, vv in deps:
                    if vv > 0:
                        e.wait_ge(smh.h, vv)
                if guard[1] > 0:
                    e.wait_ge(guard[0].h, guard[1])
                e.activation(out=cm, in_=sum_ap, func=AF.Identity,
                             scale=-R1024).then_inc(s_st.h, 1)
                e.wait_ge(s_st.h, w0)
                e.activation(out=cbarg, in_=cm, func=AF.Square
                             ).then_inc(s_st.h, 1)
                e.wait_ge(s_st.h, w1)
                e.activation(out=cbarg, in_=cbarg, func=AF.Identity,
                             scale=-1.0, bias=epsc[:, 0:1]).then_inc(s_st.h, 1)
                e.wait_ge(s_st.h, w2)
                e.activation(out=cstd, in_=sumsq_ap, func=AF.Sqrt,
                             bias=cbarg, scale=R1024).then_inc(s_st.h, 1)
            add("scalar", a)

            wv = s_ve.inc(1)
            s_ve.inc(1)
            vdone = s_ve.n

            def v2(e, adone=adone, wv=wv):
                e.wait_ge(s_st.h, adone)
                e.reciprocal(out=cs_, in_=cstd).then_inc(s_ve.h, 1)
                e.wait_ge(s_ve.h, wv)
                e.scalar_tensor_tensor(out=cneg, in0=cm, scalar=1.0, in1=cs_,
                                       op0=ALU.mult, op1=ALU.mult
                                       ).then_inc(s_ve.h, 1)
            add("vector", v2)
            return cs_, cneg, (s_ve, vdone)

        def emit_front1_batch(tt):
            par = tt % 2
            base = par * 28
            bc = [bstat[:, base + g * 4: base + (g + 1) * 4] for g in range(7)]
            war = 4 * (tt - 1)
            xs = []
            for cs in range(4):
                c = tt * 4 + cs
                xbuf, xval = emit_xload(c, 1)
                xs.append((xbuf, xval))
                scol = bstat[:, base + cs: base + cs + 1]
                qcol = bstat[:, base + 4 + cs: base + 4 + cs + 1]

                def v(e, xbuf=xbuf, xval=xval, scol=scol, war=war):
                    e.wait_ge(xval[0].h, xval[1])
                    if war > 0:
                        e.wait_ge(s_ln.h, war)
                    e.reduce_sum(out=scol, in_=xbuf[:], axis=AX.X
                                 ).then_inc(s_ve.h, 1)
                add("vector", v)
                s_ve.inc(1)

                aprev = s_ac.n

                def a(e, xbuf=xbuf, xval=xval, qcol=qcol, war=war,
                      aprev=aprev):
                    e.wait_ge(xval[0].h, xval[1])
                    if war > 0:
                        e.wait_ge(s_ln.h, war)
                    if aprev > 0:
                        e.wait_ge(s_ac.h, aprev)
                    e.activation(out=sqdump[:, 0:1024], in_=xbuf[:],
                                 func=AF.Square, accum_out=qcol
                                 ).then_inc(s_ac.h, 1)
                add("scalar", a)
                s_ac.inc(1)
            rval, aval = s_ve.n, s_ac.n
            csum4, csumsq4, cm4, cbarg4, cstd4, cs4, cneg4 = bc
            w0 = s_ve.inc(1)
            w1 = s_ve.inc(1)
            s_ve.inc(1)
            vdone = s_ve.n

            def vS(e, rval=rval, aval=aval, war=war, w0=w0, w1=w1):
                e.wait_ge(s_ve.h, rval)
                e.wait_ge(s_ac.h, aval)
                e.tensor_scalar(out=cm4, in0=csum4, scalar1=-R1024,
                                scalar2=None, op0=ALU.mult).then_inc(s_ve.h, 1)
                e.wait_ge(s_ve.h, w0)
                e.tensor_tensor(out=cbarg4, in0=cm4, in1=cm4, op=ALU.mult
                                ).then_inc(s_ve.h, 1)
                e.wait_ge(s_ve.h, w1)
                e.scalar_tensor_tensor(out=cbarg4, in0=csumsq4, scalar=R1024,
                                       in1=cbarg4, op0=ALU.mult,
                                       op1=ALU.subtract).then_inc(s_ve.h, 1)
            add("vector", vS)

            def aS(e, vdone=vdone):
                e.wait_ge(s_ve.h, vdone)
                e.activation(out=cstd4, in_=cbarg4, func=AF.Sqrt,
                             bias=epsc[:, 0:1]).then_inc(s_st.h, 1)
            add("scalar", aS)
            s_st.inc(1)
            adone = s_st.n

            w2 = s_ve.inc(1)
            s_ve.inc(1)
            svv = s_ve.n

            def v2S(e, adone=adone, w2=w2):
                e.wait_ge(s_st.h, adone)
                e.reciprocal(out=cs4, in_=cstd4).then_inc(s_ve.h, 1)
                e.wait_ge(s_ve.h, w2)
                e.scalar_tensor_tensor(out=cneg4, in0=cm4, scalar=1.0,
                                       in1=cs4, op0=ALU.mult, op1=ALU.mult
                                       ).then_inc(s_ve.h, 1)
            add("vector", v2S)
            scol4, negcol4, sv = cs4, cneg4, (s_ve, svv)

            def v3(e, tt=tt, sv=sv, scol4=scol4, negcol4=negcol4):
                e.wait_ge(sv[0].h, sv[1])
                e.tensor_copy(out=ln1_s[:, tt * 4:(tt + 1) * 4], in_=scol4)
                e.tensor_copy(out=ln1_b[:, tt * 4:(tt + 1) * 4], in_=negcol4
                              ).then_inc(s_ve.h, 1)
            add("vector", v3)
            s_ve.inc(1)
            dep = s_ve.n

            out = []
            tp_base = s_tp.n
            for cs in range(4):
                c = tt * 4 + cs
                xbuf, xval = xs[cs]
                hb = htok[c % 2]
                ht_need = tp_base + cs - 1 if cs >= 2 else (
                    htrans_vals[-1] if htrans_vals else 0)

                def a2(e, c=c, xbuf=xbuf, xval=xval, dep=dep, hb=hb,
                       ht_need=ht_need):
                    e.wait_ge(xval[0].h, xval[1])
                    e.wait_ge(s_ve.h, dep)
                    if ht_need > 0:
                        e.wait_ge(s_tp.h, ht_need)
                    e.activation(out=hb[:], in_=xbuf[:], func=AF.Identity,
                                 bias=ln1_b[:, c:c + 1], scale=ln1_s[:, c:c + 1]
                                 ).then_inc(s_ln.h, 1)
                add("scalar", a2)
                s_ln.inc(1)
                out.append((hb, s_ln.n, xbuf))
            return out

        def emit_front(c, phase):
            xbuf, xval = emit_xload(c, phase)
            if phase == 1:
                cols = stat_cols(0, c)
                csum, csumsq = cols[0], cols[1]

                def v(e, xbuf=xbuf, xval=xval, csum=csum):
                    e.wait_ge(xval[0].h, xval[1])
                    e.reduce_sum(out=csum, in_=xbuf[:], axis=AX.X).then_inc(s_ve.h, 1)
                s_ve.inc(1)
                rval = s_ve.n
                add("vector", v)

                def a(e, xbuf=xbuf, xval=xval, csumsq=csumsq):
                    e.wait_ge(xval[0].h, xval[1])
                    e.activation(out=sqdump[:, 0:1024], in_=xbuf[:], func=AF.Square,
                                 accum_out=csumsq).then_inc(s_ac.h, 1)
                s_ac.inc(1)
                aval = s_ac.n
                add("scalar", a)
                guard = (s_ln, s_ln.n - 1)
                scol, negcol, sv = emit_stats_math(
                    0, c, csum, csumsq, [(s_ve, rval), (s_ac, aval)], guard)

                def v3(e, c=c, sv=sv, scol=scol, negcol=negcol):
                    e.wait_ge(sv[0].h, sv[1])
                    e.tensor_copy(out=ln1_s[:, c:c + 1], in_=scol)
                    e.tensor_copy(out=ln1_b[:, c:c + 1], in_=negcol).then_inc(s_ve.h, 1)
                s_ve.inc(1)
                add("vector", v3)
                dep_sem, dep = s_ve, s_ve.n
            else:
                dep_sem, dep = s_wp, 0

            hb = htok[c % 2]
            nfront = len(htrans_vals)
            ht_need = htrans_vals[nfront - 2] if nfront >= 2 else 0

            def a2(e, c=c, xbuf=xbuf, xval=xval, dep=dep, hb=hb,
                   dep_sem=dep_sem, ht_need=ht_need):
                e.wait_ge(xval[0].h, xval[1])
                if dep > 0:
                    e.wait_ge(dep_sem.h, dep)
                if ht_need > 0:
                    e.wait_ge(s_tp.h, ht_need)
                e.activation(out=hb[:], in_=xbuf[:], func=AF.Identity,
                             bias=ln1_b[:, c:c + 1], scale=ln1_s[:, c:c + 1]
                             ).then_inc(s_ln.h, 1)
            add("scalar", a2)
            s_ln.inc(1)
            return hb, s_ln.n, xbuf

        def emit_htrans(c, hb, lnval, extra_pe_wait):
            cs = c % 4
            bf_need = nBf[0]

            def t(e, hb=hb, lnval=lnval, bf_need=bf_need):
                e.wait_ge(s_ln.h, lnval)
                if bf_need > 0:
                    e.wait_ge(f_Bf.h, bf_need)
                for j in range(8):
                    ins = e.transpose(out=pBf[:, j * 128:(j + 1) * 128],
                                      in_=hb[:, j * 128:(j + 1) * 128],
                                      identity=idb[:])
                ins.then_inc(s_tp.h, 1)
            add("tensor", t)
            s_tp.inc(1)
            htrans_vals.append(s_tp.n)
            tval = s_tp.n

            def v(e, cs=cs, tval=tval, extra=extra_pe_wait):
                e.wait_ge(s_tp.h, tval)
                if extra[1] > 0:
                    e.wait_ge(extra[0].h, extra[1])
                src = pBf[:, 0:1024].rearrange("p (j t) -> p j t", j=8)
                dst = hfm[:, 0:4096].rearrange("p (j t) -> p j t", j=8)[
                    :, :, cs * 128:(cs + 1) * 128]
                e.tensor_copy(out=dst, in_=src).then_inc(f_Bf.h, 1)
            add("vector", v)
            nBf[0] += 1
            f_Bf.inc(1)
            bfhalf[0] = bfhalf[1] = nBf[0]
            return f_Bf.n

        def emit_tile_proj(tt, which, hf_val, phase, sp_guard=None, w_wait=0,
                           fillers=None):
            if sp_guard is None:
                sp_guard = (s_sp[0], 0)
            wbuf = wsA
            bcol = bks if which == "k" else bqs
            xnt = xnk_t[tt % 2] if which == "k" else xnq_t
            sqc = sqk_c if which == "k" else sqq_c

            sq_jobs = []
            sq_base = s_sq.n
            prev_base = prev_sq_base[0]
            prev_sq_base[0] = sq_base

            for j in range(8):
                bank = j % 2
                po = pAB[:, bank * 512:(bank + 1) * 512]
                absnap = snap_reads(abreads, bank)

                def t(e, j=j, po=po, hf_val=hf_val, wbuf=wbuf, absnap=absnap,
                      w_wait=w_wait):
                    if j == 0:
                        e.wait_ge(f_Bf.h, hf_val)
                        e.wait_ge(s_wp.h, prep_done)
                        if w_wait > 0:
                            e.wait_ge(s_w.h, w_wait)
                    wait_snap(e, absnap)
                    for kk in range(8):
                        ins = e.matmul(
                            out=po,
                            lhsT=wbuf[:, kk * D + j * 128: kk * D + (j + 1) * 128],
                            rhs=hfm[:, kk * 512:(kk + 1) * 512],
                            start=(kk == 0), stop=(kk == 7),
                            skip_group_check=True)
                    ins.then_inc(s_pj.h, 1)
                add("tensor", t)
                s_pj.inc(1)
                mmval = s_pj.n

                sq_guard = prev_base + 2 * j + 1 if prev_base >= 0 else 0

                def a(e, j=j, po=po, mmval=mmval, xnt=xnt, bcol=bcol,
                      sp_guard=sp_guard, sq_guard=sq_guard):
                    e.wait_ge(s_pj.h, mmval)
                    if sp_guard[1] > 0:
                        e.wait_ge(sp_guard[0].h, sp_guard[1])
                    if sq_guard > 0:
                        e.wait_ge(s_sq.h, sq_guard)
                    e.activation(out=xnt[:, j * 512:(j + 1) * 512], in_=po,
                                 func=AF.Identity, bias=bcol[:, j:j + 1],
                                 scale=1.0 / CP).then_inc(s_xk.h, 1)
                    e.activation(out=xnsqw[:, j * 512:(j + 1) * 512], in_=po,
                                 func=AF.Square, bias=bcol[:, j:j + 1],
                                 scale=1.0 / CP).then_inc(s_ac.h, 1)
                add("scalar", a)
                s_xk.inc(1)
                s_ac.inc(1)
                sqin = s_ac.n
                rec_read(abreads, bank, s_ac)
                rows_val = sq_base + 2 * j + 1
                sq_jobs.append((j, sqin, rows_val))

            for (j, sqin, rows_val) in sq_jobs:
                efsnap0 = snap_reads(efreads, 0)

                def t2(e, j=j, sqin=sqin, efsnap0=efsnap0):
                    e.wait_ge(s_ac.h, sqin)
                    wait_snap(e, efsnap0)
                    e.matmul(out=pEF[0:2, 0:512], lhsT=oblk[:],
                             rhs=xnsqw[:, j * 512:(j + 1) * 512],
                             start=True, stop=True, skip_group_check=True
                             ).then_inc(s_sq.h, 1)
                add("tensor", t2)
                s_sq.inc(1)

                def v(e, rows_val=rows_val):
                    e.wait_ge(s_sq.h, rows_val)
                    e.tensor_copy(out=sqrow[:], in_=pEF[0:2, 0:512]
                                  ).then_inc(s_sqe.h, 1)
                add("vector", v)
                s_sqe.inc(1)
                rev = s_sqe.n
                rec_read(efreads, 0, s_sqe)
                if fillers:
                    fillers.pop(0)()

                efsnap1 = snap_reads(efreads, 1)

                def t3(e, rev=rev, efsnap1=efsnap1):
                    e.wait_ge(s_sqe.h, rev)
                    wait_snap(e, efsnap1)
                    for cs_ in range(4):
                        ins = e.transpose(
                            out=pEF[:, 512 + cs_ * 2:512 + cs_ * 2 + 2],
                            in_=sqrow[:, cs_ * 128:(cs_ + 1) * 128],
                            identity=id2[:])
                    ins.then_inc(s_sq.h, 1)
                add("tensor", t3)
                s_sq.inc(1)
                trval = s_sq.n

                def v2(e, tt=tt, j=j, trval=trval, sqc=sqc):
                    e.wait_ge(s_sq.h, trval)
                    src = pEF[:, 512:520].rearrange("p (c two) -> p c two", two=2)
                    dst = sqc[:, 0:NCH * 16].rearrange("p (cc hh) -> p cc hh", hh=16)[
                        :, tt * 4:(tt + 1) * 4, 2 * j:2 * j + 2]
                    e.tensor_copy(out=dst, in_=src).then_inc(s_sqe.h, 1)
                add("vector", v2)
                s_sqe.inc(1)
                rec_read(efreads, 1, s_sqe)
            return s_xk.n, s_sqe.n

        def emit_vproj_half(c, half, hf_val, sp_guard):
            cs = c % 4
            vt = vau_t[(c // 4) % 2]
            po = pAB[:, half * 512:(half + 1) * 512]
            absnap = snap_reads(abreads, half)

            def t(e, cs=cs, half=half, po=po, hf_val=hf_val, absnap=absnap):
                wait_snap(e, absnap)
                for kk in range(8):
                    e.matmul(
                        out=po,
                        lhsT=hfm[:, kk * 512 + cs * 128: kk * 512 + (cs + 1) * 128],
                        rhs=wsB[:, kk * D + half * 512: kk * D + (half + 1) * 512],
                        start=(kk == 0), stop=False, skip_group_check=True)
                e.matmul(out=po, lhsT=orow[:, 0:128],
                         rhs=bvb[:, half * 512:(half + 1) * 512],
                         start=False, stop=True, skip_group_check=True
                         ).then_inc(s_pj.h, 1)
            add("tensor", t)
            s_pj.inc(1)
            mmval = s_pj.n

            def v(e, c=c, cs=cs, half=half, po=po, mmval=mmval, vt=vt,
                  sp_guard=sp_guard):
                e.wait_ge(s_pj.h, mmval)
                if sp_guard[1] > 0:
                    e.wait_ge(sp_guard[0].h, sp_guard[1])
                src = po.rearrange("p (hh d) -> p hh d", hh=8)
                dst = vt[:, cs * 1040:(cs + 1) * 1040].rearrange(
                    "p (hh dd) -> p hh dd", hh=16)[:, half * 8:(half + 1) * 8, 0:64]
                e.tensor_copy(out=dst, in_=src).then_inc(s_va.h, 1)
            add("vector", v)
            s_va.inc(1)
            rec_read(abreads, half, s_va)
            return s_va.n

        def emit_phi(c, which, deps, etbuf=None, fills=None, end=None):
            xnt = xnk_t[(c // 4) % 2] if which == "k" else xnq_t
            sqc = sqk_c if which == "k" else sqq_c
            et = etbuf if etbuf is not None else (etok if which == "k" else eqtok)
            acol = 16 if which == "k" else 0
            cs = c % 4
            bk = c % 2
            po = pAB[:, bk * 512:(bk + 1) * 512]

            alb_guard = prev_alb[which]

            def a0(e, c=c, sqc=sqc, deps=deps, acol=acol, alb_guard=alb_guard):
                for smh, vv in deps:
                    if vv > 0:
                        e.wait_ge(smh.h, vv)
                if alb_guard > 0:
                    e.wait_ge(s_ve.h, alb_guard)
                src = sqc[:, 0:NCH * 16].rearrange("p (cc hh) -> p cc hh", hh=16)[:, c, :]
                e.wait_ge(s_wp.h, prep_done)
                e.activation(out=alf[:, acol:acol + 16], in_=src,
                             func=AF.Exp, bias=lsmn[:, 0:1], scale=-1.0
                             ).then_inc(s_ac.h, 1)
            add("scalar", a0)
            s_ac.inc(1)
            alval = s_ac.n

            def v0(e, alval=alval, acol=acol):
                e.wait_ge(s_ac.h, alval)
                e.tensor_copy(out=alfb[:, acol:acol + 16],
                              in_=alf[:, acol:acol + 16]).then_inc(s_ve.h, 1)
            add("vector", v0)
            s_ve.inc(1)
            albval = s_ve.n
            prev_alb[which] = albval

            ue_hist = []
            for rnd in range(4):
                X = bk if rnd % 2 == 0 else 1 - bk
                poX = pAB[:, X * 512:(X + 1) * 512]
                if rnd < 2:
                    waitset = snap_reads(abreads, X)
                    rdeps = tuple(deps) if rnd == 0 else ()
                else:
                    waitset = ((s_ue, ue_hist[rnd - 2]),)
                    rdeps = ()

                def t(e, cs=cs, rnd=rnd, xnt=xnt, rdeps=rdeps,
                      waitset=waitset, poX=poX):
                    for smh, vv in rdeps:
                        if vv > 0:
                            e.wait_ge(smh.h, vv)
                    wait_snap(e, waitset)
                    for pp in range(2):
                        pair = rnd * 2 + pp
                        lhs = xnt[:, pair * 512 + cs * 128: pair * 512 + (cs + 1) * 128]
                        ins = e.matmul(out=poX[:, pp * 256:(pp + 1) * 256],
                                       lhsT=lhs, rhs=projb[:],
                                       start=True, stop=True,
                                       skip_group_check=True)
                    ins.then_inc(s_um.h, 1)
                add("tensor", t)
                s_um.inc(1)
                umval = s_um.n

                def a(e, rnd=rnd, umval=umval, et=et, poX=poX):
                    e.wait_ge(s_um.h, umval)
                    e.activation(out=et[:, rnd * 512:(rnd + 1) * 512],
                                 in_=poX, func=AF.Exp).then_inc(s_ue.h, 1)
                add("scalar", a)
                s_ue.inc(1)
                rec_read(abreads, X, s_ue)
                ue_hist.append(s_ue.n)
                if fills is not None and rnd in fills:
                    fills[rnd]()
            ue_last = ue_hist[-1]
            if end is not None:
                end()

            wv0 = s_ve.inc(1)

            def v(e, ueval=ue_last, albval=albval, et=et, acol=acol, wv0=wv0):
                e.wait_ge(s_ue.h, ueval)
                e.wait_ge(s_ve.h, albval)
                a3 = alfb[:, acol:acol + 16].unsqueeze(2).broadcast_to([128, 16, 128])
                e3 = et[:, 0:2048].rearrange("p (hh m) -> p hh m", hh=16)
                e.tensor_tensor(out=e3, in0=e3, in1=a3, op=ALU.mult
                                ).then_inc(s_ve.h, 1)
                e.wait_ge(s_ve.h, wv0)
                e.tensor_scalar(out=et[:, 0:2048], in0=et[:, 0:2048],
                                scalar1=EPS, scalar2=None, op0=ALU.add
                                ).then_inc(s_ph.h, 1)
            add("vector", v)
            s_ph.inc(1)
            return s_ph.n

        def emit_kv(c, phval, vwaits, phase, snval=None, first=False, etbuf=None):
            vt = vau_t[(c // 4) % 2]
            cs = c % 4
            et = etbuf if etbuf is not None else etok
            kvorder = [0, 7, 14, 1, 8, 15, 2, 9, 3, 10, 4, 11, 5, 12, 6, 13]

            def t(e, cs=cs, phval=phval, vwaits=tuple(vwaits), vt=vt, first=first,
                  snval=snval, et=et):
                e.wait_ge(s_ph.h, phval)
                for smh, vv in vwaits:
                    if vv > 0:
                        e.wait_ge(smh.h, vv)
                if snval is not None:
                    e.wait_ge(s_sn.h, snval)
                for h in kvorder:
                    st = first and (h in (0, 7, 14))
                    ins = e.matmul(out=s_slice(h),
                                   lhsT=et[:, h * 128:(h + 1) * 128],
                                   rhs=vt[:, cs * 1040 + h * 65: cs * 1040 + (h + 1) * 65],
                                   start=st, stop=False, skip_group_check=True)
                ins.then_inc(s_kv.h, 1)
            add("tensor", t)
            s_kv.inc(1)
            return s_kv.n

        def finish():
            block = ctx.enter_context(nc.Block())

            @block.sync
            def _(e):
                for f in Q["sync"]:
                    f(e)

            @block.gpsimd
            def _(e):
                for f in Q["gpsimd"]:
                    f(e)

            @block.vector
            def _(e):
                for f in Q["vector"]:
                    f(e)

            @block.scalar
            def _(e):
                for f in Q["scalar"]:
                    f(e)

            @block.tensor
            def _(e):
                for f in Q["tensor"]:
                    f(e)

        tile_uk_done = {}
        tile_m6_done = {}
        last_proj_pe = [(s_pj, 0)]
        dbg_last = [(s_ln, 0)]

        for tt in range(_NTTRUN):
            hf_tile = None
            fr = emit_front1_batch(tt)
            for cs in range(4):
                c = tt * 4 + cs
                hb, lnval, _ = fr[cs]
                if _P1SUB >= 2:
                    hf_tile = emit_htrans(c, hb, lnval,
                                          last_proj_pe[0] if cs == 0 else (s_pj, 0))
            dbg_last[0] = (s_ln, s_ln.n) if _P1SUB < 2 else (f_Bf, f_Bf.n)
            if _P1SUB < 3:
                continue
            prev_sp = ((s_sp[tt % 2], (tt // 2) * 32)
                       if (tt >= 2 and _P1SUB >= 7) else (s_sp[0], 0))
            vav = {}

            def mk_fill(c, half, hf_tile=hf_tile, prev_sp=prev_sp, vav=vav):
                def f():
                    vav[(c % 4, half)] = emit_vproj_half(c, half, hf_tile,
                                                         prev_sp)
                return f

            fillers = ([mk_fill(tt * 4 + cs, half)
                        for cs in range(4) for half in range(2)]
                       if _P1SUB >= 4 else None)
            xkval, sqev = emit_tile_proj(tt, "k", hf_tile, 1, sp_guard=prev_sp,
                                         fillers=fillers)
            dbg_last[0] = (s_sqe, s_sqe.n)
            if _P1SUB < 4:
                continue
            vavals = [vav[(cs, 1)] for cs in range(4)]
            last_proj_pe[0] = (s_pj, s_pj.n)
            dbg_last[0] = (s_va, s_va.n)
            if _P1SUB < 5:
                continue

            kv_pend = None
            for cs in range(4):
                c = tt * 4 + cs
                etb = etok if c % 2 == 0 else eqtok
                phv = emit_phi(c, "k", [(s_xk, xkval), (s_sqe, sqev)], etbuf=etb)
                if _P1SUB >= 6:
                    if kv_pend is not None:
                        emit_kv(*kv_pend)
                    kv_pend = (c, phv, [(s_va, vavals[cs])], 1, None, c == 0, etb)
            if kv_pend is not None:
                emit_kv(*kv_pend)
            tile_uk_done[tt] = s_um.n
            dbg_last[0] = (s_ph, s_ph.n) if _P1SUB < 6 else (s_kv, s_kv.n)
            if _P1SUB < 7:
                continue

            kvv = s_kv.n
            xnt = xnk_t[tt % 2]
            vt = vau_t[tt % 2]
            ssp = s_sp[tt % 2]

            def sp(e, tt=tt, kvv=kvv, xnt=xnt, vt=vt, ssp=ssp):
                e.wait_ge(s_kv.h, kvv)
                e.dma_start(out=xnk_sp[:, tt * 4096:(tt + 1) * 4096],
                            in_=xnt[:]).then_inc(ssp.h, 16)
                e.dma_start(out=vau_sp[:, tt * 4160:(tt + 1) * 4160],
                            in_=vt[:]).then_inc(ssp.h, 16)
            add("gpsimd", sp)
            ssp.inc(32)

        p1_kv = s_kv.n
        p1_sp = (s_sp[0].n, s_sp[1].n)
        p1_ln_box[0] = s_ln.n

        if _P1SUB < 8:
            def stD(e, w=dbg_last[0][1], sh=dbg_last[0][0]):
                if w > 0:
                    e.wait_ge(sh.h, w)
                e.dma_start(out=yseg[0:128, 0:1024], in_=sfin[:, 0:1024]
                            ).then_inc(s_ou[0].h, 16)
                e.wait_ge(s_ou[0].h, 16)
            add("sync", stD)
            finish()
            return nc

        def v_exc(e, p1_kv=p1_kv):
            e.wait_ge(s_kv.h, p1_kv)
            e.tensor_copy(out=sfin[:, 0:455], in_=S0[:])
            e.tensor_copy(out=sfin[:, 455:910], in_=S1[:])
            e.tensor_copy(out=sfin[:, 910:1040], in_=S2[:]).then_inc(s_ve.h, 1)
        add("vector", v_exc)
        s_ve.inc(1)
        evacval = s_ve.n

        if _STAGE == 1:
            def st1(e, v=evacval):
                e.wait_ge(s_ve.h, v)
                e.dma_start(out=yseg[0:128, 0:1024], in_=sfin[:, 0:1024]
                            ).then_inc(s_ou[0].h, 16)
                e.dma_start(out=yseg[128:256, 0:16], in_=sfin[:, 1024:1040]
                            ).then_inc(s_ou[0].h, 16)
                e.wait_ge(s_ou[0].h, 32)
            add("sync", st1)
            finish()
            return nc

        p1_pj = s_pj.n

        def g_wreA(e, p1_pj=p1_pj):
            e.wait_ge(s_pj.h, p1_pj)
            e.dma_start(out=wsA[:], in_=wqT.rearrange("(j p) o -> p j o", p=128)
                        ).then_inc(s_w.h, 16)
        add("gpsimd", g_wreA)
        s_w.inc(16)
        wreA_val = s_w.n

        def g_wreB(e, p1_pj=p1_pj):
            e.dma_start(out=wsB[:], in_=woT.rearrange("(j p) o -> p j o", p=128)
                        ).then_inc(s_wb.h, 16)
        add("gpsimd", g_wreB)
        s_wb.inc(16)
        wreB_val = s_wb.n

        def emit_p2_head(tt):
            prev_uk = tile_uk_done.get(("p2", tt - 2), 0)
            prev_m6 = tile_m6_done.get(tt - 2, 0)
            srb = s_rb[tt % 2]
            rbval = (srb, srb.inc(32))

            def frb(e, tt=tt, prev_uk=prev_uk, prev_m6=prev_m6, srb=srb):
                e.wait_ge(s_sp[0].h, p1_sp[0])
                e.wait_ge(s_sp[1].h, p1_sp[1])
                if prev_uk > 0:
                    e.wait_ge(s_um.h, prev_uk)
                if prev_m6 > 0:
                    e.wait_ge(s_au.h, prev_m6)
                e.dma_start(out=xnk_t[tt % 2][:],
                            in_=xnk_sp[:, tt * 4096:(tt + 1) * 4096]
                            ).then_inc(srb.h, 16)
                e.dma_start(out=vau_t[tt % 2][:],
                            in_=vau_sp[:, tt * 4160:(tt + 1) * 4160]
                            ).then_inc(srb.h, 16)
            add("sync", frb)

            hf_tile = None
            xbufs_tile = []
            for cs in range(4):
                c = tt * 4 + cs
                hb, lnval, xbuf_c = emit_front(c, 2)
                xbufs_tile.append(xbuf_c)
                hf_tile = emit_htrans(c, hb, lnval,
                                      last_proj_pe[0] if cs == 0 else (s_pj, 0))
            xqval, sqqev = emit_tile_proj(tt, "q", hf_tile, 2, w_wait=wreA_val)
            last_proj_pe[0] = (s_pj, s_pj.n)
            return rbval, xqval, sqqev, xbufs_tile

        p2_head0 = emit_p2_head(0)

        def g_exc(e, evacval=evacval, p1_sp=p1_sp):
            e.wait_ge(s_ve.h, evacval)
            e.wait_ge(s_sp[0].h, p1_sp[0])
            e.wait_ge(s_sp[1].h, p1_sp[1])
            e.dma_start(out=cc_in[:, :], in_=sfin[:]).then_inc(s_cc.h, 16)
            e.wait_ge(s_cc.h, 16)
            e.collective_compute(
                "AllGather", ALU.bypass,
                replica_groups=[[0, 1, 2, 3], [4, 5, 6, 7]],
                ins=[cc_in[:, :].opt()], outs=[cc_out[:, :].opt()],
            ).then_inc(s_ccd.h, 1)
        add("gpsimd", g_exc)
        s_cc.inc(16)
        s_ccd.inc(1)
        ccval = s_ccd.n

        def v_ms(e, ccval=ccval):
            e.wait_ge(s_ccd.h, ccval)
            e.memset(sinit[:], 0.0).then_inc(s_ve.h, 1)
        add("vector", v_ms)
        s_ve.inc(1)
        gw = s_ve.n

        for r in range(4):
            def g_g(e, r=r, gw=gw):
                e.wait_ge(s_ve.h, gw)
                e.dma_start(out=gbuf[:],
                            in_=cc_out[r * 128:(r + 1) * 128, :]).then_inc(s_cc.h, 16)
            add("gpsimd", g_g)
            s_cc.inc(16)
            gval = s_cc.n

            def v_g(e, r=r, gval=gval):
                e.wait_ge(s_cc.h, gval)
                e.scalar_tensor_tensor(out=sinit[:], in0=gbuf[:],
                                       scalar=segw[:, r:r + 1], in1=sinit[:],
                                       op0=ALU.mult, op1=ALU.add
                                       ).then_inc(s_ve.h, 1)
            add("vector", v_g)
            s_ve.inc(1)
            gw = s_ve.n
        initval = s_ve.n

        if _STAGE == 2:
            def st2(e, v=initval):
                e.wait_ge(s_ve.h, v)
                e.dma_start(out=yseg[0:128, 0:1024], in_=sinit[:, 0:1024]
                            ).then_inc(s_ou[0].h, 16)
                e.dma_start(out=yseg[128:256, 0:16], in_=sinit[:, 1024:1040]
                            ).then_inc(s_ou[0].h, 16)
                e.wait_ge(s_ou[0].h, 32)
            add("sync", st2)
            finish()
            return nc

        def t_init(e, initval=initval):
            e.wait_ge(s_ve.h, initval)
            e.matmul(out=S0[:], lhsT=idS[:], rhs=sinit[:, 0:455], start=True,
                     stop=False, skip_group_check=True)
            e.matmul(out=S1[:], lhsT=idS[:], rhs=sinit[:, 455:910], start=True,
                     stop=False, skip_group_check=True)
            e.matmul(out=S2[:], lhsT=idS[:], rhs=sinit[:, 910:1040], start=True,
                     stop=False, skip_group_check=True).then_inc(s_pe.h, 1)
        add("tensor", t_init)
        s_pe.inc(1)
        sinj = s_pe.n

        def emit_tail_tT(tc):
            atb, epval = tc["atb"], tc["ep"]
            bf_need = nBf[0]

            def tT(e, atb=atb, epval=epval, bf_need=bf_need):
                e.wait_ge(s_ep.h, epval)
                if bf_need > 0:
                    e.wait_ge(f_Bf.h, bf_need)
                for j in range(8):
                    ins = e.transpose(out=pBf[:, j * 128:(j + 1) * 128],
                                      in_=atb[:, j * 128:(j + 1) * 128],
                                      identity=idb[:])
                ins.then_inc(s_tp.h, 1)
            add("tensor", tT)
            s_tp.inc(1)
            attn_tp_vals.append(s_tp.n)
            tc["tT"] = s_tp.n

        def emit_tail_vT(tc):
            tval = tc["tT"]
            wo_pe_prev = s_pj.n

            def vT(e, tval=tval, wo_prev=wo_pe_prev):
                e.wait_ge(s_tp.h, tval)
                if wo_prev > 0:
                    e.wait_ge(s_pj.h, wo_prev)
                e.tensor_copy(out=atfm[:], in_=pBf[:, 0:1024]).then_inc(f_Bf.h, 1)
            add("vector", vT)
            nBf[0] += 1
            f_Bf.inc(1)
            bfhalf[0] = bfhalf[1] = nBf[0]
            tc["atval"] = f_Bf.n
            tc["cols2"] = stat_cols(1, tc["c"])
            tc["y2"] = yf[tc["c"] % 2]
            tc["ln2_evac"] = []

        def emit_tail_wo(tc, half):
            bk, PO, atval = tc["bk"], tc["PO"], tc["atval"]
            cols2, y2 = tc["cols2"], tc["y2"]
            absnap = snap_reads(abreads, bk)

            def tW(e, half=half, atval=atval, absnap=absnap,
                   w_wait=wreB_val, PO=PO):
                e.wait_ge(s_wb.h, w_wait)
                e.wait_ge(f_Bf.h, atval)
                wait_snap(e, absnap)
                for kk in range(8):
                    e.matmul(
                        out=PO, lhsT=atfm[:, kk * 128:(kk + 1) * 128],
                        rhs=wsB[:, kk * D + half * 512: kk * D + (half + 1) * 512],
                        start=(kk == 0), stop=False,
                        skip_group_check=True)
                e.matmul(out=PO, lhsT=orow[:, 0:128],
                         rhs=bob[:, half * 512:(half + 1) * 512],
                         start=False, stop=True, skip_group_check=True
                         ).then_inc(s_pj.h, 1)
            add("tensor", tW)
            s_pj.inc(1)
            woval = s_pj.n

            def aW(e, half=half, woval=woval, y2=y2, cols2=cols2, PO=PO):
                e.wait_ge(s_pj.h, woval)
                e.activation(out=y2[:, half * 512:(half + 1) * 512],
                             in_=PO, func=AF.Identity,
                             accum_out=cols2[0] if half == 0 else cols2[3]
                             ).then_inc(s_ac.h, 1)
            add("scalar", aW)
            s_ac.inc(1)
            rec_read(abreads, bk, s_ac)
            tc["ln2_evac"].append(s_ac.n)

        def emit_tail_b(tc):
            c, xbuf = tc["c"], tc["xbuf"]
            cols2, y2, ln2_evac = tc["cols2"], tc["y2"], tc["ln2_evac"]
            def aWs(e, ev=ln2_evac[1], cols2=cols2):
                e.wait_ge(s_ac.h, ev)
                e.activation(out=cols2[0], in_=cols2[0], func=AF.Identity,
                             bias=cols2[3]).then_inc(s_ac.h, 1)
            add("scalar", aWs)
            s_ac.inc(1)

            def aW2(e, ev=ln2_evac[1], y2=y2, cols2=cols2):
                e.wait_ge(s_ac.h, ev)
                e.activation(out=sqdump[:, 0:1024], in_=y2[:], func=AF.Square,
                             accum_out=cols2[1]).then_inc(s_ac.h, 1)
            add("scalar", aW2)
            s_ac.inc(1)
            sq2val = s_ac.n

            s2, neg2, st2 = emit_stats_math(
                1, c, cols2[0], cols2[1], [(s_ac, sq2val)], (s_ve, 0))

            lsc = lnsc[c % 2]

            def aW3(e, y2=y2, st2=st2, s2=s2, neg2=neg2, lsc=lsc):
                e.wait_ge(st2[0].h, st2[1])
                e.activation(out=lsc[:], in_=y2[:], func=AF.Identity,
                             bias=neg2, scale=s2).then_inc(s_ac.h, 1)
            add("scalar", aW3)
            s_ac.inc(1)
            ln2val = s_ac.n

            x1 = x1b[c % 2]
            cols3 = stat_cols(2, c)
            x1_guard = s_ou[c % 2].n

            def vR(e, lsc=lsc, ln2val=ln2val, x1=x1, xbuf=xbuf,
                   cols3=cols3, x1_guard=x1_guard, sou_g=s_ou[c % 2]):
                e.wait_ge(s_ac.h, ln2val)
                if x1_guard > 0:
                    e.wait_ge(sou_g.h, x1_guard)
                e.scalar_tensor_tensor(out=x1[:], in0=lsc[:], scalar=1.0,
                                       in1=xbuf[:], op0=ALU.mult, op1=ALU.add,
                                       accum_out=cols3[0]).then_inc(s_ve.h, 1)
            add("vector", vR)
            s_ve.inc(1)
            x1vals.append(s_ve.n)
            x1val = s_ve.n

            def aL3(e, x1=x1, x1val=x1val, cols3=cols3):
                e.wait_ge(s_ve.h, x1val)
                e.activation(out=sqdump[:, 0:1024], in_=x1[:], func=AF.Square,
                             accum_out=cols3[1]).then_inc(s_ac.h, 1)
            add("scalar", aL3)
            s_ac.inc(1)
            sq3val = s_ac.n

            s3, neg3, st3 = emit_stats_math(
                2, c, cols3[0], cols3[1], [(s_ve, x1val), (s_ac, sq3val)],
                (s_ve, 0))

            h3 = h3tok[c % 2]

            def aL3b(e, x1=x1, st3=st3, s3=s3, neg3=neg3, h3=h3):
                e.wait_ge(st3[0].h, st3[1])
                e.activation(out=h3[:], in_=x1[:], func=AF.Identity,
                             bias=neg3, scale=s3).then_inc(s_ac.h, 1)
            add("scalar", aL3b)
            s_ac.inc(1)
            tc["h3"], tc["h3val"], tc["lsc"], tc["x1"] = h3, s_ac.n, lsc, x1

        def emit_tail_h3(tc):
            h3, h3val = tc["h3"], tc["h3val"]
            bf_need = nBf[0]

            def tH3(e, h3=h3, h3val=h3val, bf_need=bf_need):
                e.wait_ge(s_ac.h, h3val)
                if bf_need > 0:
                    e.wait_ge(f_Bf.h, bf_need)
                for j in range(8):
                    ins = e.transpose(out=pBf[:, j * 128:(j + 1) * 128],
                                      in_=h3[:, j * 128:(j + 1) * 128],
                                      identity=idb[:])
                ins.then_inc(s_tp.h, 1)
            add("tensor", tH3)
            s_tp.inc(1)
            tval = s_tp.n
            ffn_prev = s_pj.n
            tc["tH3"] = (tval, ffn_prev)

        def emit_tail_f(tc):
            c, bk, PO = tc["c"], tc["bk"], tc["PO"]
            lsc, x1 = tc["lsc"], tc["x1"]
            tval, ffn_prev = tc["tH3"]

            def vH3(e, tval=tval, ffn_prev=ffn_prev):
                e.wait_ge(s_tp.h, tval)
                if ffn_prev > 0:
                    e.wait_ge(s_pj.h, ffn_prev)
                e.tensor_copy(out=h3fm[:], in_=pBf[:, 0:1024]).then_inc(f_Bf.h, 1)
            add("vector", vH3)
            nBf[0] += 1
            f_Bf.inc(1)
            bfhalf[0] = bfhalf[1] = nBf[0]
            h3fval = f_Bf.n
            cols4 = stat_cols(3, c)
            fb = yf[c % 2]
            ffn_evac = []
            for half in range(2):
                absnap = snap_reads(abreads, bk)

                def tF(e, half=half, h3fval=h3fval, absnap=absnap, PO=PO):
                    e.wait_ge(f_Bf.h, h3fval)
                    wait_snap(e, absnap)
                    for kk in range(8):
                        ins = e.matmul(
                            out=PO, lhsT=h3fm[:, kk * 128:(kk + 1) * 128],
                            rhs=wsC[:, kk * D + half * 512: kk * D + (half + 1) * 512],
                            start=(kk == 0), stop=(kk == 7),
                            skip_group_check=True)
                    ins.then_inc(s_pj.h, 1)
                add("tensor", tF)
                s_pj.inc(1)
                ffval = s_pj.n

                def aF(e, half=half, ffval=ffval, fb=fb, cols4=cols4, PO=PO):
                    e.wait_ge(s_pj.h, ffval)
                    e.activation(out=fb[:, half * 512:(half + 1) * 512],
                                 in_=PO, func=AF.Identity,
                                 accum_out=cols4[0] if half == 0 else cols4[3]
                                 ).then_inc(s_ac.h, 1)
                add("scalar", aF)
                s_ac.inc(1)
                rec_read(abreads, bk, s_ac)
                ffn_evac.append(s_ac.n)

            def aFs(e, ev=ffn_evac[1], cols4=cols4):
                e.wait_ge(s_ac.h, ev)
                e.activation(out=cols4[0], in_=cols4[0], func=AF.Identity,
                             bias=cols4[3]).then_inc(s_ac.h, 1)
            add("scalar", aFs)
            s_ac.inc(1)

            def aF2(e, ev=ffn_evac[1], fb=fb, cols4=cols4):
                e.wait_ge(s_ac.h, ev)
                e.activation(out=sqdump[:, 0:1024], in_=fb[:], func=AF.Square,
                             accum_out=cols4[1]).then_inc(s_ac.h, 1)
            add("scalar", aF2)
            s_ac.inc(1)
            sq4val = s_ac.n

            s4, neg4, st4 = emit_stats_math(
                3, c, cols4[0], cols4[1], [(s_ac, sq4val)], (s_ve, 0))

            def aL4(e, fb=fb, st4=st4, s4=s4, neg4=neg4, lsc=lsc):
                e.wait_ge(st4[0].h, st4[1])
                e.activation(out=lsc[:], in_=fb[:], func=AF.Identity,
                             bias=neg4, scale=s4).then_inc(s_ac.h, 1)
            add("scalar", aL4)
            s_ac.inc(1)
            ln4val = s_ac.n

            sou = s_ou[c % 2]

            def vO(e, lsc=lsc, ln4val=ln4val, x1=x1):
                e.wait_ge(s_ac.h, ln4val)
                e.tensor_add(out=x1[:], in0=x1[:], in1=lsc[:]).then_inc(s_ve.h, 1)
            add("vector", vO)
            s_ve.inc(1)
            oval = s_ve.n

            def fO(e, c=c, x1=x1, oval=oval, sou=sou):
                e.wait_ge(s_ve.h, oval)
                e.dma_start(out=yseg[c * 128:(c + 1) * 128, :], in_=x1[:]
                            ).then_inc(sou.h, 16)
            add("gpsimd", fO)
            sou.inc(16)

        ptc = None
        for tt in range(_NTTRUN):
            if tt == 0:
                rbval, xqval, sqqev, xbufs_tile = p2_head0
            else:
                rbval, xqval, sqqev, xbufs_tile = emit_p2_head(tt)

            for cs in range(4):
                c = tt * 4 + cs
                xbuf = xbufs_tile[cs]
                bk = c % 2
                PO = pAB[:, bk * 512:(bk + 1) * 512]
                prev = ptc

                spb = sprev[c % 2]
                kv_prev = s_kv.n
                m5_prev = s_au.n - 16

                def vsn(e, spb=spb, kv_prev=kv_prev, m5_prev=m5_prev, sinj=sinj):
                    e.wait_ge(s_pe.h, sinj)
                    e.wait_ge(s_kv.h, kv_prev)
                    if m5_prev > 0:
                        e.wait_ge(s_au.h, m5_prev)
                    e.tensor_copy(out=spb[:, 0:455], in_=S0[:])
                    e.tensor_copy(out=spb[:, 455:910], in_=S1[:])
                    e.tensor_copy(out=spb[:, 910:1040], in_=S2[:]).then_inc(s_sn.h, 1)
                add("vector", vsn)
                s_sn.inc(1)
                snval = s_sn.n

                def emit_fm(et, fmdst, phv, rnd):
                    bf_need = nBf[0]

                    def t(e, et=et, rnd=rnd, phv=phv, bf_need=bf_need):
                        if rnd == 0:
                            e.wait_ge(s_ph.h, phv)
                        if bf_need > 0:
                            e.wait_ge(f_Bf.h, bf_need)
                        for hh in range(8):
                            h = rnd * 8 + hh
                            ins = e.transpose(
                                out=pBf[:, hh * 128:(hh + 1) * 128],
                                in_=et[:, h * 128:(h + 1) * 128],
                                identity=idb[:])
                        ins.then_inc(s_tp.h, 1)
                    add("tensor", t)
                    s_tp.inc(1)
                    tval = s_tp.n

                    def v(e, fmdst=fmdst, rnd=rnd, tval=tval):
                        e.wait_ge(s_tp.h, tval)
                        e.tensor_copy(out=fmdst[:, rnd * 1024:(rnd + 1) * 1024],
                                      in_=pBf[:, 0:1024]).then_inc(f_Bf.h, 1)
                    add("vector", v)
                    nBf[0] += 1
                    f_Bf.inc(1)
                    bfhalf[0] = bfhalf[1] = nBf[0]

                phq = emit_phi(c, "q", [(s_xk, xqval), (s_sqe, sqqev)])

                def fill1(p=prev):
                    if p is not None:
                        emit_tail_tT(p)

                def fill2(p=prev):
                    if p is not None:
                        emit_tail_vT(p)
                    emit_fm(eqtok, qfm, phq, 0)

                def fill3():
                    emit_fm(eqtok, qfm, phq, 1)

                phk = emit_phi(c, "k", [rbval],
                               fills={1: fill1, 2: fill2, 3: fill3})


                if prev is not None:
                    emit_tail_wo(prev, 0)
                emit_kv(c, phk, [rbval], 2, snval=snval)
                emit_fm(etok, kfm, phk, 0)
                if prev is not None:
                    emit_tail_wo(prev, 1)
                emit_fm(etok, kfm, phk, 1)
                fm_done = f_Bf.n

                vt = vau_t[tt % 2]
                for half in range(2):
                    if half == 1 and prev is not None:
                        emit_tail_b(prev)
                    vavals_r = []
                    for rr in range(2):
                        X = bk if (half * 2 + rr) % 2 == 0 else 1 - bk
                        POX = pAB[:, X * 512:(X + 1) * 512]
                        absnap = snap_reads(abreads, X)

                        def tA(e, half=half, rr=rr, fm_done=fm_done,
                               absnap=absnap, POX=POX):
                            e.wait_ge(f_Bf.h, fm_done)
                            wait_snap(e, absnap)
                            for q4 in range(4):
                                h = half * 8 + rr * 4 + q4
                                ins = e.matmul(out=POX[:, q4 * 128:(q4 + 1) * 128],
                                               lhsT=kfm[:, h * 128:(h + 1) * 128],
                                               rhs=qfm[:, h * 128:(h + 1) * 128],
                                               start=True, stop=True,
                                               skip_group_check=True)
                            ins.then_inc(s_pe.h, 1)
                        add("tensor", tA)
                        s_pe.inc(1)
                        aval = s_pe.n

                        am_guard = s_au.n - 4

                        def vA(e, rr=rr, aval=aval, am_guard=am_guard, POX=POX):
                            e.wait_ge(s_pe.h, aval)
                            if am_guard > 0:
                                e.wait_ge(s_au.h, am_guard)
                            src = POX.rearrange("p (q4 cc) -> p q4 cc", q4=4)
                            m3 = triuT[:].unsqueeze(1).broadcast_to([128, 4, 128])
                            dst = amask[:, rr * 512:(rr + 1) * 512].rearrange(
                                "p (q4 cc) -> p q4 cc", q4=4)
                            e.tensor_tensor(out=dst, in0=src, in1=m3,
                                            op=ALU.mult).then_inc(s_am.h, 1)
                        add("vector", vA)
                        s_am.inc(1)
                        rec_read(abreads, X, s_am)
                        vavals_r.append(s_am.n)

                    if half == 1 and prev is not None:
                        emit_tail_h3(prev)
                    for rr in range(2):
                        amval = vavals_r[rr]
                        for q4 in range(4):
                            hh = rr * 4 + q4
                            h = half * 8 + hh
                            ef_bank = hh // 4
                            base = ef_bank * 512 + (hh % 4) * 65
                            efsnap = snap_reads(efreads, ef_bank) if (hh % 4) == 0 else ()

                            def tM(e, h=h, hh=hh, base=base, amval=amval,
                                   snval=snval, vt=vt, cs=cs, spb=spb,
                                   efsnap=efsnap, rr=rr, q4=q4):
                                if q4 == 0:
                                    e.wait_ge(s_am.h, amval)
                                    e.wait_ge(s_sn.h, snval)
                                wait_snap(e, efsnap)
                                out = pEF[:, base:base + 65]
                                e.matmul(out=out, lhsT=qfm[:, h * 128:(h + 1) * 128],
                                         rhs=spb[:, h * 65:(h + 1) * 65],
                                         start=(hh % 4 == 0), stop=False,
                                         skip_group_check=True)
                                e.matmul(out=out,
                                         lhsT=amask[:, (rr * 4 + q4) * 128:(rr * 4 + q4 + 1) * 128],
                                         rhs=vt[:, cs * 1040 + h * 65: cs * 1040 + (h + 1) * 65],
                                         start=False, stop=True,
                                         skip_group_check=True).then_inc(s_au.h, 1)
                            add("tensor", tM)
                            s_au.inc(1)

                    auval = s_au.n
                    atb = attok[c % 2]
                    natt = len(attn_tp_vals)
                    at_guard = attn_tp_vals[natt - 2] if natt >= 2 else 0
                    wv1 = s_ve.inc(1)
                    wv2 = s_ve.inc(1)

                    def vE(e, half=half, auval=auval, atb=atb, at_guard=at_guard,
                           wv1=wv1, wv2=wv2):
                        e.wait_ge(s_au.h, auval)
                        if half == 0 and at_guard > 0:
                            e.wait_ge(s_tp.h, at_guard)
                        for g in range(2):
                            src = pEF[:, g * 512:g * 512 + 260].rearrange(
                                "p (hh d) -> p hh d", hh=4)[:, :, 64:65].squeeze(2)
                            ins = e.tensor_scalar(
                                out=dgath[:, half * 8 + g * 4:half * 8 + g * 4 + 4],
                                in0=src, scalar1=EPS, scalar2=None, op0=ALU.add)
                        ins.then_inc(s_ve.h, 1)
                        e.wait_ge(s_ve.h, wv1)
                        e.reciprocal(out=rcol[:, half * 8:half * 8 + 8],
                                     in_=dgath[:, half * 8:half * 8 + 8]
                                     ).then_inc(s_ve.h, 1)
                        e.wait_ge(s_ve.h, wv2)
                        for g in range(2):
                            num = pEF[:, g * 512:g * 512 + 260].rearrange(
                                "p (hh d) -> p hh d", hh=4)[:, :, 0:64]
                            rc = rcol[:, half * 8 + g * 4:half * 8 + g * 4 + 4]
                            rc3 = rc.unsqueeze(2).broadcast_to([128, 4, 64])
                            dst = atb[:, (half * 8 + g * 4) * 64:(half * 8 + g * 4 + 4) * 64]
                            dst3 = dst.rearrange("p (hh d) -> p hh d", hh=4)
                            ins = e.tensor_tensor(out=dst3, in0=num, in1=rc3,
                                                  op=ALU.mult)
                        ins.then_inc(s_ep.h, 1)
                    add("vector", vE)
                    s_ep.inc(1)
                    rec_read(efreads, 0, s_ep)
                    rec_read(efreads, 1, s_ep)
                epval = s_ep.n

                if prev is not None:
                    emit_tail_f(prev)
                ptc = dict(c=c, bk=bk, PO=PO, atb=attok[c % 2], ep=epval,
                           xbuf=xbuf)

            tile_uk_done[("p2", tt)] = s_um.n
            tile_m6_done[tt] = s_au.n

        emit_tail_tT(ptc)
        emit_tail_vT(ptc)
        emit_tail_wo(ptc, 0)
        emit_tail_wo(ptc, 1)
        emit_tail_b(ptc)
        emit_tail_h3(ptc)
        emit_tail_f(ptc)

        totals = (s_ou[0].n, s_ou[1].n)

        def fin(e, totals=totals):
            e.wait_ge(s_ou[0].h, totals[0])
            e.wait_ge(s_ou[1].h, totals[1])
        add("sync", fin)

        finish()

    return nc


_NC_CACHE = []


def kernel(L0_attn_x, L0_attn_Wq, L0_attn_bq, L0_attn_Wk, L0_attn_bk,
           L0_attn_Wv, L0_attn_bv, L0_attn_Wo, L0_attn_bo, L0_attn_proj,
           L0_ffn_weight):
    x = np.asarray(L0_attn_x, dtype=np.float32)
    wq = np.asarray(L0_attn_Wq, dtype=np.float32)
    wk = np.asarray(L0_attn_Wk, dtype=np.float32)
    wv = np.asarray(L0_attn_Wv, dtype=np.float32)
    wo = np.asarray(L0_attn_Wo, dtype=np.float32)
    wf = np.asarray(L0_ffn_weight, dtype=np.float32)
    bq = np.asarray(L0_attn_bq, dtype=np.float32)
    bk = np.asarray(L0_attn_bk, dtype=np.float32)
    bv = np.asarray(L0_attn_bv, dtype=np.float32)
    bo = np.asarray(L0_attn_bo, dtype=np.float32)
    proj = np.asarray(L0_attn_proj, dtype=np.float32)

    cA = np.ascontiguousarray
    wqT = cA(wq.T)
    wkT = cA(wk.T)
    wvT = cA(wv.T)
    woT = cA(wo.T)
    wfT = cA(wf.T)
    bqi = cA(bq.reshape(8, 128).T)
    bki = cA(bk.reshape(8, 128).T)
    bvi = bv.reshape(1, D)
    boi = bo.reshape(1, D)
    pT2 = np.zeros((128, 256), dtype=np.float32)
    pT2[0:64, 0:128] = proj.T
    pT2[64:128, 128:256] = proj.T
    id128i = np.eye(128, dtype=np.float32).astype(bf)
    id2i = np.eye(2, dtype=np.float32)
    triuTi = np.triu(np.ones((128, 128), dtype=np.float32))
    ob2 = np.zeros((128, 2), dtype=np.float32)
    ob2[:64, 0] = 0.5
    ob2[64:, 1] = 0.5
    oblki = ob2.astype(bf)
    orowi = np.ones((1, 128), dtype=np.float32).astype(bf)
    idSi = np.eye(128, dtype=np.float32)

    if not _NC_CACHE:
        _NC_CACHE.append(build_nc())
    nc = _NC_CACHE[0]

    in_maps = []
    for core in range(8):
        b, s = core // 4, core % 4
        segwv = np.zeros((128, 4), dtype=np.float32)
        segwv[:, :s] = 1.0
        in_maps.append(dict(
            xseg=cA(x[b, s * SEG:(s + 1) * SEG, :]),
            wqT=wqT, wkT=wkT, wvT=wvT, woT=woT, wfT=wfT,
            bqi=bqi, bki=bki, bvi=bvi, boi=boi, projT2=pT2,
            id128i=id128i, id2i=id2i, triuTi=triuTi, oblki=oblki,
            orowi=orowi, segwi=segwv, idSi=idSi,
        ))
    res = run_bass_kernel_spmd(nc, in_maps, core_ids=list(range(8)))
    out = np.empty((B, T, D), dtype=np.float32)
    for core in range(8):
        b, s = core // 4, core % 4
        out[b, s * SEG:(s + 1) * SEG, :] = res.results[core]["yseg"]
    return out
